# version 19
# baseline (speedup 1.0000x reference)
import numpy as np
from contextlib import ExitStack
import concourse.bass as bass
import concourse.mybir as mybir
from concourse.bass_utils import run_bass_kernel_spmd

F32 = mybir.dt.float32
BF16 = mybir.dt.bfloat16
U32 = mybir.dt.uint32
AF = mybir.ActivationFunctionType
ALU = mybir.AluOpType
AX = mybir.AxisListType

B, T, C = 4, 8192, 1024
NCORES = 8
TSH = T // 2
LN_EPS = 1e-5
ATT_EPS = 1e-6
HEADS, DH = 16, 64
D_INNER, D_ST, DT_RANK, D_CONV = 2048, 16, 64, 4
MOE_E = 8

K_LN_EPS, K_ATT_EPS, K_ONE, K_NEG1 = 0, 1, 2, 3
NCONST = 4


def make_consts():
    v = np.zeros((1, NCONST), np.float32)
    v[0, K_LN_EPS] = LN_EPS
    v[0, K_ATT_EPS] = ATT_EPS
    v[0, K_ONE] = 1.0
    v[0, K_NEG1] = -1.0
    return np.repeat(v, 128, axis=0)



class EQ:

    def __init__(self, prog, q, name, dma_inc=16):
        self.p = prog
        self.q = q
        self.name = name
        self.dname = name + "_d"
        self.sem = prog.sem(name)
        self.dma_inc = dma_inc
        self.selfsync = True
        self.lazy_inc = False
        self._pending = None

    def _waits(self, after):
        p = self.p
        if self.selfsync:
            if p.cnt[self.name] > 0:
                self.q.wait_ge(self.sem, p.cnt[self.name])
            if p.cnt.get(self.dname, 0) > 0:
                self.q.wait_ge(p.sems[self.dname], p.cnt[self.dname])
        if after:
            for (sname, val) in after:
                if sname == self.name:
                    continue
                if val > 0:
                    self.q.wait_ge(p.sems[sname], val)

    def tok(self):
        if self._pending is not None:
            self._pending.then_inc(self.sem, 1)
            self.p.cnt[self.name] += 1
            self._pending = None
        return (self.name, self.p.cnt[self.name])

    def _emit(self, fname, a, kw, inc, sem=None, name=None):
        after = kw.pop("after", None)
        self._waits(after)
        ins = getattr(self.q, fname)(*a, **kw)
        if self.lazy_inc and sem is None:
            self._pending = ins
            return (self.name, self.p.cnt[self.name])
        ins.then_inc(sem or self.sem, inc)
        self.p.cnt[name or self.name] += inc
        return (name or self.name, self.p.cnt[name or self.name])

    def dma(self, out, in_, **kw):
        dsem = self.p.sem(self.dname)
        return self._emit("dma_start", (), dict(out=out, in_=in_, **kw),
                          self.dma_inc, sem=dsem, name=self.dname)

    def __getattr__(self, fname):
        def call(*a, **kw):
            return self._emit(fname, a, kw, 1)
        return call


class Prog:
    def __init__(self):
        self.nc = bass.Bass("TRN2", target_bir_lowering=False)
        self.stack = ExitStack()
        self.cnt = {}
        self.sems = {}
        self.inputs = {}
        self.outputs = {}
        self._phase = None
        self._phase_stack = None
        self._nphase = 0

    def phase(self, name):
        from contextlib import contextmanager

        @contextmanager
        def _ph():
            self._phase = f"{self._nphase}_{name}"
            self._nphase += 1
            self._phase_stack = ExitStack()
            try:
                yield self
            finally:
                self.barrier()
                self._phase_stack.close()
                self._phase_stack = None
                self._phase = None
        return _ph()

    def barrier(self):
        for eq in (self.PE, self.V, self.SC, self.GP, self.SY):
            if getattr(eq, "_pending", None) is not None:
                eq.tok()
        nc = self.nc
        engines = [nc.sync, nc.gpsimd, nc.vector, nc.scalar, nc.tensor]
        for q in engines:
            for name, cval in self.cnt.items():
                if cval > 0:
                    q.wait_ge(self.sems[name], cval)

    def din(self, name, shape, dtype=F32):
        ap = self.nc.dram_tensor(name, list(shape), dtype, kind="ExternalInput").ap()
        self.inputs[name] = (tuple(shape), dtype)
        return ap

    def dout(self, name, shape, dtype=F32):
        ap = self.nc.dram_tensor(name, list(shape), dtype, kind="ExternalOutput").ap()
        self.outputs[name] = (tuple(shape), dtype)
        return ap

    def dtmp(self, name, shape, dtype=F32, shared=False):
        return self.nc.dram_tensor(name, list(shape), dtype,
                                   addr_space=("Shared" if shared else "Local")).ap()

    def sb(self, name, shape, dtype):
        if self._phase is not None:
            name = f"{self._phase}_{name}"
            return self._phase_stack.enter_context(
                self.nc.sbuf_tensor(name, list(shape), dtype))
        return self.stack.enter_context(self.nc.sbuf_tensor(name, list(shape), dtype))

    def psum(self, name, shape, dtype=F32):
        if self._phase is not None:
            name = f"{self._phase}_{name}"
            return self._phase_stack.enter_context(
                self.nc.psum_tensor(name, list(shape), dtype))
        return self.stack.enter_context(self.nc.psum_tensor(name, list(shape), dtype))

    def sem(self, name):
        if name not in self.sems:
            self.sems[name] = self.stack.enter_context(self.nc.semaphore(name))
            self.cnt[name] = 0
        return self.sems[name]

    def begin(self):
        self.block = self.stack.enter_context(self.nc.Block())
        nc = self.nc
        self.SY = EQ(self, nc.sync, "q_sy")
        self.GP = EQ(self, nc.gpsimd, "q_gp")
        self.V = EQ(self, nc.vector, "q_v")
        self.SC = EQ(self, nc.scalar, "q_sc")
        self.PE = EQ(self, nc.tensor, "q_pe")
        self.PE.selfsync = True
        self.PE.lazy_inc = True

        return self

    def finish(self):
        for eq in (self.PE, self.V, self.SC, self.GP, self.SY):
            if getattr(eq, "_pending", None) is not None:
                eq.tok()
        nc = self.nc
        for name in list(self.cnt):
            if self.cnt[name] > 0:
                nc.gpsimd.wait_ge(self.sems[name], self.cnt[name])
        self.stack.close()
        return self.nc



def e_ln(P, x_ap, st, n_feat, eps_col, out_negm, out_rstd, after=()):
    V, SC = P.V, P.SC
    t = V.scalar_tensor_tensor(out=st["sq"][:, 0:n_feat], in0=x_ap, scalar=0.0,
                               in1=x_ap, op0=ALU.add, op1=ALU.mult,
                               accum_out=st["s2"][:, 0:1], after=list(after))
    V.tensor_reduce(out=st["s1"][:, 0:1], in_=x_ap, axis=AX.X, op=ALU.add)
    V.tensor_scalar(out=out_negm, in0=st["s1"][:, 0:1], scalar1=-1.0 / n_feat,
                    scalar2=None, op0=ALU.mult)
    V.tensor_mul(out=st["s1"][:, 0:1], in0=out_negm, in1=out_negm)
    tv = V.scalar_tensor_tensor(out=out_rstd, in0=st["s2"][:, 0:1], scalar=1.0 / n_feat,
                                in1=st["s1"][:, 0:1], op0=ALU.mult, op1=ALU.subtract)
    ts = SC.activation(out=out_rstd, in_=out_rstd, func=AF.Sqrt, bias=eps_col,
                       after=[tv])
    tr = V.reciprocal(out=out_rstd, in_=out_rstd, after=[ts])
    return tr


def e_transpose_row(P, src_bf16_ap_fn, dst_bf16_ap_fn, nchunks, ps_t, ident,
                    after=(), evac_q=None, key=None):
    V = evac_q or P.V
    PE = P.PE
    if not hasattr(P, "_pst_last"):
        P._pst_last = {}
    if key is None:
        key = id(ps_t)
    ps_full = ps_t if isinstance(ps_t, bass.AP) else ps_t[:]
    nslot = 2 if ps_full.shape[1] >= 256 else 1
    slots = [ps_full[:, 128 * s:128 * (s + 1)] for s in range(nslot)]
    tprev = P._pst_last.get(key)
    if not isinstance(tprev, list):
        tprev = [tprev] * nslot
    tok = None
    for j in range(nchunks):
        s = j % nslot
        aft = list(after) if j == 0 else []
        if tprev[s] is not None:
            aft.append(tprev[s])
        PE.transpose(slots[s], src_bf16_ap_fn(j), ident[:], after=aft)
        tok_pe = PE.tok()
        tok = V.tensor_copy(out=dst_bf16_ap_fn(j), in_=slots[s], after=[tok_pe])
        tprev[s] = tok
    P._pst_last[key] = tprev
    return tok


def e_load_weight_bf16(P, dram_ap, sb_stage, sb_dst_fn, nchunks, fwidth,
                       conv_q=None, conv_after=(), pre_sliced=False):
    SY = P.SY
    CV = conv_q or P.SC
    if not hasattr(P, "_stage_last"):
        P._stage_last = {}
    key = id(sb_stage)
    toks = P._stage_last.get(key, [None, None])
    prev_both = [t for t in toks if t is not None]
    srcv = dram_ap if pre_sliced else dram_ap.rearrange("(j p) f -> j p f", p=128)
    tok = None
    for j in range(nchunks):
        half = j % 2
        if j < 2:
            aft = list(prev_both)
        else:
            aft = []
            if toks[half] is not None:
                aft.append(toks[half])
        tl = SY.dma(sb_stage[:, half * fwidth:(half + 1) * fwidth], srcv[j], after=aft)
        caft = [tl] + (list(conv_after) if j == 0 else [])
        tok = CV.copy(out=sb_dst_fn(j), in_=sb_stage[:, half * fwidth:(half + 1) * fwidth],
                      after=caft)
        toks[half] = tok
    P._stage_last[key] = toks
    return tok



def emit_ffn0(P, io, tsh=TSH, emit_xin0=False):
    x_in = io["x"]
    wsi = io["wsi"]
    wso = io["wso"]
    w1 = io["w1"]
    w2 = io["w2"]
    wg = io["wg"]
    gb = io["gb"]
    ident_in = io["ident"]
    consts = io["consts"]
    x_out = io["xout"]
    xin0_out = io.get("xin0")

    ntt = tsh // 128
    QT = min(4, ntt)
    NQ = ntt // QT

    x_t = x_in.rearrange("(n p) c -> n p c", p=128)
    xo_t = x_out.rearrange("(n p) c -> n p c", p=128)
    if emit_xin0:
        xi_t = xin0_out.rearrange("(n p) c -> n p c", p=128)

    cst = P.sb("cst", [128, NCONST], F32)
    idb = P.sb("idb", [128, 128], BF16)
    wstage = P.sb("wstage", [128, 2 * 2048], F32)
    wsi_b = P.sb("wsi_b", [128, 8 * 2048], BF16)
    w1_b = P.sb("w1_b", [128, 8 * 2048], BF16)
    wso_b = P.sb("wso_b", [128, 8 * 1024], BF16)
    wg_b = P.sb("wg_b", [128, 8 * MOE_E], BF16)
    gb_s = P.sb("gb_s", [128, MOE_E], F32)
    w2_b = P.sb("w2_b", [128, 2 * 8 * 1024], BF16)

    hidt = P.sb("hidt", [128, QT * 1024], BF16)
    acc = P.sb("acc", [128, QT * 1024], F32)
    xres = P.sb("xres", [128, QT * 1024], F32)
    wdns = P.sb("wdns", [128, QT * MOE_E], F32)

    xt = P.sb("xt", [128, C], F32)
    sq = P.sb("sq", [128, 2048], F32)
    s1 = P.sb("s1", [128, 1], F32)
    s2 = P.sb("s2", [128, 1], F32)
    negm = P.sb("negm", [128, 1], F32)
    rstd = P.sb("rstd", [128, 1], F32)
    xlb = P.sb("xlb", [128, C], BF16)
    xlt = P.sb("xlt", [128, C], BF16)
    siln = P.sb("siln", [128, 1024], BF16)
    mid = P.sb("mid", [128, 1024], BF16)
    midt = P.sb("midt", [128, 1024], BF16)
    g8 = P.sb("g8", [128, MOE_E], F32)
    gm = P.sb("gm", [128, MOE_E], F32)
    gcol = P.sb("gcol", [128, 1], F32)

    ps_u = P.psum("ps_u", [128, 2048])
    ps_o = P.psum("ps_o", [128, 1024])
    ps_t = P.psum("ps_t", [128, 128], BF16)
    ps_g = P.psum("ps_g", [128, MOE_E])

    st = {"sq": sq, "s1": s1, "s2": s2}

    SY, GP, V, SC, PE = P.SY, P.GP, P.V, P.SC, P.PE

    t_c = SY.dma(cst[:], consts)
    t_i = SY.dma(idb[:], ident_in)
    t_g = SY.dma(gb_s[:], gb)
    SC.copy(out=s1[:, 0:1], in_=cst[:, 0:1], after=[t_c, t_g])
    tw1 = e_load_weight_bf16(P, wsi, wstage,
                             lambda j: wsi_b[:, j * 2048:(j + 1) * 2048], 8, 2048)
    tw2 = e_load_weight_bf16(P, w1, wstage,
                             lambda j: w1_b[:, j * 2048:(j + 1) * 2048], 8, 2048)
    tw3 = e_load_weight_bf16(P, wso, wstage,
                             lambda j: wso_b[:, j * 1024:(j + 1) * 1024], 8, 1024)
    tw4 = e_load_weight_bf16(P, wg, wstage,
                             lambda j: wg_b[:, j * MOE_E:(j + 1) * MOE_E], 8, MOE_E)
    eps_ln = cst[:, K_LN_EPS:K_LN_EPS + 1]

    war_xt = None
    war_sq = None
    war_psg = None
    war_psu = []
    war_pso = None
    t_hid_last = None
    t_we_mm = None

    for q in range(NQ):
        for it in range(QT):
            i = q * QT + it
            aft_x = [tw4]
            if war_xt is not None:
                aft_x.append(war_xt)
            tx = SY.dma(xt[:], x_t[i], after=aft_x)
            tres = V.tensor_copy(out=xres[:, it * 1024:(it + 1) * 1024], in_=xt[:],
                                 after=[tx])
            aft_ln = [tx] + ([war_sq] if war_sq else [])
            tln = e_ln(P, xt[:], st, C, eps_ln, negm[:, 0:1], rstd[:, 0:1],
                       after=aft_ln)
            txl = V.tensor_scalar(out=xlb[:], in0=xt[:], scalar1=negm[:, 0:1],
                                  scalar2=rstd[:, 0:1], op0=ALU.add, op1=ALU.mult)
            war_xt = txl
            tt = e_transpose_row(P, lambda j: xlb[:, j * 128:(j + 1) * 128],
                                 lambda j: xlt[:, j * 128:(j + 1) * 128],
                                 8, ps_t, idb, after=[txl, t_i])
            for j in range(8):
                aft = [tt] + ([war_psg] if war_psg else []) if j == 0 else []
                PE.matmul(ps_g[:], xlt[:, j * 128:(j + 1) * 128],
                          wg_b[:, j * MOE_E:(j + 1) * MOE_E],
                          start=(j == 0), stop=(j == 7), after=aft)
            tpe_g = PE.tok()
            V.tensor_add(out=g8[:], in0=ps_g[:], in1=gb_s[:], after=[tpe_g])
            V.reduce_max(out=s1[:, 0:1], in_=g8[:], axis=AX.X)
            tneg = V.tensor_scalar(out=s1[:, 0:1], in0=s1[:, 0:1], scalar1=-1.0,
                                   scalar2=None, op0=ALU.mult)
            tex = SC.activation(out=g8[:], in_=g8[:], func=AF.Exp, bias=s1[:, 0:1],
                                after=[tneg])
            V.tensor_reduce(out=s2[:, 0:1], in_=g8[:], axis=AX.X, op=ALU.add,
                            after=[tex])
            V.reciprocal(out=s2[:, 0:1], in_=s2[:, 0:1])
            V.tensor_scalar(out=g8[:], in0=g8[:], scalar1=s2[:, 0:1], scalar2=None,
                            op0=ALU.mult)
            V.reduce_max(out=s1[:, 0:1], in_=g8[:], axis=AX.X)
            V.tensor_scalar(out=gm[:], in0=g8[:], scalar1=s1[:, 0:1], scalar2=None,
                            op0=ALU.is_ge)
            V.tensor_scalar(out=gm[:], in0=gm[:], scalar1=-1.0, scalar2=1.0,
                            op0=ALU.mult, op1=ALU.add)
            V.tensor_mul(out=gm[:], in0=g8[:], in1=gm[:])
            V.reduce_max(out=s2[:, 0:1], in_=gm[:], axis=AX.X)
            V.tensor_scalar(out=gm[:], in0=g8[:], scalar1=s2[:, 0:1], scalar2=None,
                            op0=ALU.is_ge)
            war_psg = V.tensor_mul(out=wdns[:, it * MOE_E:(it + 1) * MOE_E],
                                   in0=g8[:], in1=gm[:])

            for f in range(4):
                for j in range(8):
                    aft = ([tw1, tt] + war_psu) if (f == 0 and j == 0) else []
                    PE.matmul(ps_u[:, f * 512:(f + 1) * 512],
                              xlt[:, j * 128:(j + 1) * 128],
                              wsi_b[:, j * 2048 + f * 512: j * 2048 + (f + 1) * 512],
                              start=(j == 0), stop=(j == 7), after=aft)
            tpe_u = PE.tok()
            ts1 = SC.activation(out=siln[:], in_=ps_u[:, 0:1024], func=AF.Silu,
                                after=[tpe_u])
            tm1 = V.tensor_mul(out=mid[:], in0=ps_u[:, 1024:2048], in1=siln[:],
                               after=[ts1, tpe_u])
            war_psu = [ts1, tm1]
            ttm = e_transpose_row(P, lambda j: mid[:, j * 128:(j + 1) * 128],
                                  lambda j: midt[:, j * 128:(j + 1) * 128],
                                  8, ps_t, idb, after=[tm1])
            for f in range(2):
                for j in range(8):
                    aft = ([ttm, tw3] + ([war_pso] if war_pso else [])) \
                        if (f == 0 and j == 0) else []
                    PE.matmul(ps_o[:, f * 512:(f + 1) * 512],
                              midt[:, j * 128:(j + 1) * 128],
                              wso_b[:, j * 1024 + f * 512: j * 1024 + (f + 1) * 512],
                              start=(j == 0), stop=(j == 7), after=aft)
            tpe_o = PE.tok()
            war_pso = V.tensor_copy(out=acc[:, it * 1024:(it + 1) * 1024], in_=ps_o[:],
                                    after=[tpe_o])

            for f in range(4):
                for j in range(8):
                    aft = ([tw2] + war_psu) if (f == 0 and j == 0) else []
                    PE.matmul(ps_u[:, f * 512:(f + 1) * 512],
                              xlt[:, j * 128:(j + 1) * 128],
                              w1_b[:, j * 2048 + f * 512: j * 2048 + (f + 1) * 512],
                              start=(j == 0), stop=(j == 7), after=aft)
            tpe_h = PE.tok()
            ts2 = SC.activation(out=siln[:], in_=ps_u[:, 0:1024], func=AF.Silu,
                                after=[tpe_h])
            tm2 = V.tensor_mul(out=mid[:], in0=ps_u[:, 1024:2048], in1=siln[:],
                               after=[ts2, tpe_h])
            war_psu = [ts2, tm2]
            t_hid_last = e_transpose_row(
                P, lambda j: mid[:, j * 128:(j + 1) * 128],
                lambda j: hidt[:, it * 1024 + j * 128: it * 1024 + (j + 1) * 128],
                8, ps_t, idb, after=[tm2])

        for e in range(MOE_E):
            caft = [t_we_mm] if t_we_mm else []
            twe = e_load_weight_bf16(P, w2[e], wstage,
                                     lambda j: w2_b[:, j * 1024:(j + 1) * 1024],
                                     8, 1024, conv_after=caft)
            for it in range(QT):
                for f in range(2):
                    for j in range(8):
                        aft = ([twe, t_hid_last] +
                               ([war_pso] if war_pso else [])) \
                            if (f == 0 and j == 0) else []
                        PE.matmul(ps_o[:, f * 512:(f + 1) * 512],
                                  hidt[:, it * 1024 + j * 128: it * 1024 + (j + 1) * 128],
                                  w2_b[:, j * 1024 + f * 512: j * 1024 + (f + 1) * 512],
                                  start=(j == 0), stop=(j == 7), after=aft)
                t_we_mm = PE.tok()
                V.tensor_copy(out=gcol[:, 0:1],
                              in_=wdns[:, it * MOE_E + e: it * MOE_E + e + 1],
                              after=[t_we_mm])
                war_pso = V.scalar_tensor_tensor(
                    out=acc[:, it * 1024:(it + 1) * 1024],
                    in0=ps_o[:], scalar=gcol[:, 0:1],
                    in1=acc[:, it * 1024:(it + 1) * 1024],
                    op0=ALU.mult, op1=ALU.add)

        for it in range(QT):
            i = q * QT + it
            a_ap = acc[:, it * 1024:(it + 1) * 1024]
            r_ap = xres[:, it * 1024:(it + 1) * 1024]
            aft_ln = [war_sq] if war_sq else []
            if war_xt is not None:
                aft_ln.append(war_xt)
            tln = e_ln(P, a_ap, st, C, eps_ln, negm[:, 0:1], rstd[:, 0:1],
                       after=aft_ln)
            tv = V.tensor_scalar(out=xt[:], in0=a_ap, scalar1=negm[:, 0:1],
                                 scalar2=rstd[:, 0:1], op0=ALU.add, op1=ALU.mult)
            tv = V.tensor_add(out=xt[:], in0=xt[:], in1=r_ap)
            tst = GP.dma(xo_t[i], xt[:], after=[tv])
            war_xt = tst
            war_sq = None
            if emit_xin0:
                tln2 = e_ln(P, xt[:], st, C, eps_ln, negm[:, 0:1], rstd[:, 0:1])
                tv2 = V.tensor_scalar(out=sq[:, 0:1024], in0=xt[:],
                                      scalar1=negm[:, 0:1], scalar2=rstd[:, 0:1],
                                      op0=ALU.add, op1=ALU.mult)
                war_sq = GP.dma(xi_t[i], sq[:, 0:1024], after=[tv2])


def build_ffn0(tsh=TSH, emit_xin0=False):
    P = Prog()
    io = {
        "x": P.din("x", [tsh, C]),
        "wsi": P.din("wsi", [C, 2048]),
        "wso": P.din("wso", [C, C]),
        "w1": P.din("w1", [C, 2048]),
        "w2": P.din("w2", [MOE_E, C, C]),
        "wg": P.din("wg", [C, MOE_E]),
        "gb": P.din("gb", [128, MOE_E]),
        "ident": P.din("ident", [128, 128], BF16),
        "consts": P.din("consts", [128, NCONST]),
        "xout": P.dout("xout", [tsh, C]),
    }
    if emit_xin0:
        io["xin0"] = P.dout("xin0", [tsh, C])
    P.begin()
    emit_ffn0(P, io, tsh=tsh, emit_xin0=emit_xin0)
    nc = P.finish()
    return nc, P




def emit_attn0(P, io, tsh=TSH, tpre=TSH):
    xpre = io["xpre"]
    xown = io["xown"]
    gate = io["gate"]
    wq = io["wq"]
    wk = io["wk"]
    wv = io["wv"]
    wo = io["wo"]
    bq_in = io["bq"]
    bk_in = io["bk"]
    bv_in = io["bv"]
    bo_in = io["bo"]
    ones_row = io["ones_row"]
    ones_col = io["ones_col"]
    tri = io["tri"]
    ident_in = io["ident"]
    consts = io["consts"]
    x_out = io["xout"]

    npre = tpre // 128
    ntt = tsh // 128
    xp_t = xpre.rearrange("(n p) c -> n p c", p=128)
    xo_in_t = xown.rearrange("(n p) c -> n p c", p=128)
    xo_t = x_out.rearrange("(n p) c -> n p c", p=128)

    cst = P.sb("cst", [128, NCONST], F32)
    idb = P.sb("idb", [128, 128], BF16)
    orow = P.sb("orow", [1, 128], BF16)
    ocol = P.sb("ocol", [128, 1], BF16)
    gcol = P.sb("gcol", [128, 1], F32)
    gcol_b = P.sb("gcol_b", [128, 1], BF16)
    trim = P.sb("trim", [128, 128], F32)
    wstage = P.sb("wstage", [128, 2 * 2048], F32)
    wq_b = P.sb("wq_b", [128, 8 * 1024], BF16)
    wk_b = P.sb("wk_b", [128, 8 * 1024], BF16)
    wv_b = P.sb("wv_b", [128, 8 * 1024], BF16)
    wo_b = P.sb("wo_b", [128, 8 * 1024], BF16)
    brows = P.sb("brows", [1, 4 * C], F32)
    bq_r = P.sb("bq_r", [1, C], BF16)
    bk_r = P.sb("bk_r", [1, C], BF16)
    bv_r = P.sb("bv_r", [1, C], BF16)
    bo_r = P.sb("bo_r", [1, C], BF16)

    S_sb = P.sb("S_sb", [128, 8 * 64], F32)
    S_bf = P.sb("S_bf", [128, 8 * 64], BF16)
    ks_sb = P.sb("ks_sb", [128, 8], F32)
    ks_bf = P.sb("ks_bf", [128, 8], BF16)

    xt = P.sb("xt", [128, C], F32)
    sq = P.sb("sq", [128, C], F32)
    s1 = P.sb("s1", [128, 1], F32)
    s2 = P.sb("s2", [128, 1], F32)
    negm = P.sb("negm", [128, 1], F32)
    rstd = P.sb("rstd", [128, 1], F32)
    xlb = P.sb("xlb", [128, C], BF16)
    xlt = P.sb("xlt", [128, C], BF16)
    eluw = P.sb("eluw", [128, C], F32)
    relw = P.sb("relw", [128, C], F32)
    qp = P.sb("qp", [128, C], BF16)
    kp = P.sb("kp", [128, C], BF16)
    vtok = P.sb("vtok", [128, C], BF16)
    scm = P.sb("scm", [128, 128], BF16)
    ktr = P.sb("ktr", [128, 128], BF16)
    ctxs = P.sb("ctxs", [128, 128], BF16)
    rnrow = P.sb("rnrow", [128, 128], F32)
    onesf = P.sb("onesf", [128, 64], F32)
    afm = P.sb("afm", [128, 8 * 128], BF16)
    oacc = P.sb("oacc", [128, C], F32)

    PS_A = P.psum("PS_A", [128, 1024])
    PS_B = P.psum("PS_B", [128, 512])
    PS_K = P.psum("PS_K", [128, 1024])
    PS_V = P.psum("PS_V", [128, 1024])
    PS_D = P.psum("PS_D", [128, 256], BF16)

    st = {"sq": sq, "s1": s1, "s2": s2}

    SY, GP, V, SC, PE = P.SY, P.GP, P.V, P.SC, P.PE

    t_c = SY.dma(cst[:], consts)
    t_i = SY.dma(idb[:], ident_in)
    t_or = SY.dma(orow[:], ones_row)
    t_oc = SY.dma(ocol[:], ones_col)
    t_gt = SY.dma(gcol[:], gate)
    t_tr = SY.dma(trim[:], tri)
    SC.copy(out=s1[:, 0:1], in_=cst[:, 0:1], after=[t_c, t_gt, t_tr, t_or, t_oc])
    t_gtb = SC.copy(out=gcol_b[:], in_=gcol[:])
    twq = e_load_weight_bf16(P, wq, wstage, lambda j: wq_b[:, j*1024:(j+1)*1024], 8, 1024)
    twk = e_load_weight_bf16(P, wk, wstage, lambda j: wk_b[:, j*1024:(j+1)*1024], 8, 1024)
    twv = e_load_weight_bf16(P, wv, wstage, lambda j: wv_b[:, j*1024:(j+1)*1024], 8, 1024)
    two = e_load_weight_bf16(P, wo, wstage, lambda j: wo_b[:, j*1024:(j+1)*1024], 8, 1024)
    tba = SY.dma(brows[0:1, 0:C], bq_in)
    tbb = SY.dma(brows[0:1, C:2 * C], bk_in)
    tbc = SY.dma(brows[0:1, 2 * C:3 * C], bv_in)
    tbd = SY.dma(brows[0:1, 3 * C:4 * C], bo_in)
    tb1 = SC.copy(out=bq_r[:], in_=brows[0:1, 0:C], after=[tba])
    tb2 = SC.copy(out=bk_r[:], in_=brows[0:1, C:2 * C], after=[tbb])
    tb4 = SC.copy(out=bv_r[:], in_=brows[0:1, 2 * C:3 * C], after=[tbc])
    tb5 = SC.copy(out=bo_r[:], in_=brows[0:1, 3 * C:4 * C], after=[tbd])
    tz1 = V.memset(S_sb[:], 0.0)
    tz2 = V.memset(S_bf[:], 0.0)
    tz3 = V.memset(ks_sb[:], 0.0)
    tz4 = V.memset(ks_bf[:], 0.0)
    tz5 = V.memset(onesf[:], 1.0)
    eps_ln = cst[:, K_LN_EPS:K_LN_EPS + 1]
    eps_at = cst[:, K_ATT_EPS:K_ATT_EPS + 1]

    war_xt = None
    war_sq = None
    war_psa = None
    war_psk = None
    war_psv = None
    war_sc = None
    war_cn = None
    war_sd = None
    war_rb = None
    war_oacc = None

    def qkv_common(i, x_tiles, phase2):
        nonlocal war_xt, war_sq, war_psa, war_psk, war_psv
        aft_x = []
        if war_xt is not None:
            aft_x.append(war_xt)
        tx = SY.dma(xt[:], x_tiles[i], after=aft_x)
        aft_ln = [tx] + ([war_sq] if war_sq else [])
        e_ln(P, xt[:], st, C, eps_ln, negm[:, 0:1], rstd[:, 0:1], after=aft_ln)
        war_sq = None
        txl = V.tensor_scalar(out=xlb[:], in0=xt[:], scalar1=negm[:, 0:1],
                              scalar2=rstd[:, 0:1], op0=ALU.add, op1=ALU.mult)
        if not phase2:
            war_xt = txl
        tt = e_transpose_row(P, lambda j: xlb[:, j*128:(j+1)*128],
                             lambda j: xlt[:, j*128:(j+1)*128],
                             8, PS_D.ap()[:, 0:128], idb, after=[txl, t_i],
                             key="psd_xlt")
        for p in range(8):
            for j in range(8):
                aft = ([tt, twk] + ([war_psk] if war_psk else [])) if (p == 0 and j == 0) else []
                PE.matmul(PS_K.ap()[:, p*128:(p+1)*128],
                          wk_b[:, j*1024 + p*128: j*1024 + (p+1)*128],
                          xlt[:, j*128:(j+1)*128],
                          start=(j == 0), stop=False, after=aft)
            PE.matmul(PS_K.ap()[:, p*128:(p+1)*128],
                      bk_r[0:1, p*128:(p+1)*128], orow[:],
                      start=False, stop=True, after=([tb2] if p == 0 else []))
        tpek = PE.tok()
        tk1 = V.tensor_scalar(out=eluw[:], in0=PS_K.ap()[:], scalar1=0.0,
                              scalar2=None, op0=ALU.min, after=[tpek])
        tk2 = SC.activation(out=eluw[:], in_=eluw[:], func=AF.Exp, after=[tk1])
        tk3 = V.tensor_scalar(out=relw[:], in0=PS_K.ap()[:], scalar1=0.0,
                              scalar2=None, op0=ALU.max)
        war_psk = tk3
        tk4 = V.tensor_add(out=kp[:], in0=relw[:], in1=eluw[:], after=[tk2])
        for f in range(2):
            for j in range(8):
                aft = ([tt, twv] + (war_psv if isinstance(war_psv, list)
                                     else ([war_psv] if war_psv else []))) \
                    if (f == 0 and j == 0) else []
                PE.matmul(PS_V.ap()[:, f*512:(f+1)*512],
                          xlt[:, j*128:(j+1)*128],
                          wv_b[:, j*1024 + f*512: j*1024 + (f+1)*512],
                          start=(j == 0), stop=False, after=aft)
            PE.matmul(PS_V.ap()[:, f*512:(f+1)*512], orow[:],
                      bv_r[0:1, f*512:(f+1)*512],
                      start=False, stop=True, after=([tb4] if f == 0 else []))
        tpev = PE.tok()
        if phase2:
            tvv = V.tensor_copy(out=vtok[:], in_=PS_V.ap()[:], after=[tpev])
        else:
            tvv = V.tensor_scalar(out=vtok[:], in0=PS_V.ap()[:], scalar1=gcol[:, 0:1],
                                  scalar2=None, op0=ALU.mult, after=[tpev])
        war_psv = tvv
        tq4 = None
        if phase2:
            for p in range(8):
                for j in range(8):
                    aft = ([tt, twq] + ([war_psa] if war_psa else [])) if (p == 0 and j == 0) else []
                    PE.matmul(PS_A.ap()[:, p*128:(p+1)*128],
                              wq_b[:, j*1024 + p*128: j*1024 + (p+1)*128],
                              xlt[:, j*128:(j+1)*128],
                              start=(j == 0), stop=False, after=aft)
                PE.matmul(PS_A.ap()[:, p*128:(p+1)*128],
                          bq_r[0:1, p*128:(p+1)*128], orow[:],
                          start=False, stop=True, after=([tb1] if p == 0 else []))
            tpeq = PE.tok()
            tq1 = V.tensor_scalar(out=eluw[:], in0=PS_A.ap()[:], scalar1=0.0,
                                  scalar2=None, op0=ALU.min, after=[tpeq])
            tq2 = SC.activation(out=eluw[:], in_=eluw[:], func=AF.Exp, after=[tq1])
            tq3 = V.tensor_scalar(out=relw[:], in0=PS_A.ap()[:], scalar1=0.0,
                                  scalar2=None, op0=ALU.max)
            war_psa = tq3
            tq4 = V.tensor_add(out=qp[:], in0=relw[:], in1=eluw[:], after=[tq2])
        return tk4, tvv, tq4

    def state_update(p, tk4, tvv, use_gate):
        nonlocal war_sd
        td_prev = P._pst_last.get("psd_ktr")
        for a in range(2):
            aft = [tk4] + ([td_prev] if td_prev else [])
            PE.transpose(PS_D.ap()[:, 128 + 64*a:128 + 64*(a+1)],
                         kp[64*a:64*(a+1), p*128:(p+1)*128],
                         idb[64*a:64*(a+1), 64*a:64*(a+1)],
                         tile_position=(64*a, 0), after=aft)
            td_prev = None
        tpe = PE.tok()
        tkt = V.tensor_copy(out=ktr[:], in_=PS_D.ap()[:, 128:256], after=[tpe])
        P._pst_last["psd_ktr"] = tkt
        for a in range(2):
            aft0 = [tkt, tvv] + ([war_sd] if (a == 0 and war_sd) else [])
            PE.matmul(PS_V.ap()[64*a:64*(a+1), 0:64],
                      ktr[:, 64*a:64*(a+1)],
                      vtok[:, p*128 + 64*a: p*128 + 64*(a+1)],
                      start=True, stop=True, tile_position=(0, 64*a), after=aft0)
            PE.matmul(PS_V.ap()[64*a:64*(a+1), 64:65],
                      ktr[:, 64*a:64*(a+1)],
                      (gcol_b[:, 0:1] if use_gate else ocol[:]),
                      start=True, stop=True, tile_position=(0, 64*a),
                      after=([t_oc, t_gt] if a == 0 else []))
        tpes = PE.tok()
        V.tensor_add(out=S_sb[:, p*64:(p+1)*64], in0=S_sb[:, p*64:(p+1)*64],
                     in1=PS_V.ap()[:, 0:64], after=[tpes])
        V.tensor_copy(out=S_bf[:, p*64:(p+1)*64], in_=S_sb[:, p*64:(p+1)*64])
        V.tensor_add(out=ks_sb[:, p:p+1], in0=ks_sb[:, p:p+1],
                     in1=PS_V.ap()[:, 64:65])
        war_sd = V.tensor_copy(out=ks_bf[:, p:p+1], in_=ks_sb[:, p:p+1])
        return war_sd

    for i in range(npre):
        tk4, tvv, _ = qkv_common(i, xp_t, phase2=False)
        for p in range(8):
            state_update(p, tk4, tvv, use_gate=True)

    for i in range(ntt):
        tk4, tvv, tq4 = qkv_common(i, xo_in_t, phase2=True)
        for p in range(8):
            tctx = []
            for a in range(2):
                aft = [tq4, tk4] + ([war_sc] if war_sc else [])
                PE.matmul(PS_B.ap()[:, 0:128],
                          kp[64*a:64*(a+1), p*128:(p+1)*128],
                          qp[64*a:64*(a+1), p*128:(p+1)*128],
                          start=True, stop=True, tile_position=(64*a, 0),
                          after=aft)
                tpesc = PE.tok()
                tmsk = V.tensor_mul(out=scm[:], in0=PS_B.ap()[:, 0:128], in1=trim[:],
                                    after=[tpesc])
                war_sc = tmsk
                aftc = [tmsk, tvv] + ([war_cn] if (a == 0 and war_cn) else [])
                PE.matmul(PS_B.ap()[64*a:64*(a+1), 128:256],
                          vtok[:, p*128 + 64*a: p*128 + 64*(a+1)],
                          scm[:], start=True, stop=False,
                          tile_position=(0, 64*a), after=aftc)
                PE.matmul(PS_B.ap()[64*a:64*(a+1), 128:256],
                          S_bf[64*a:64*(a+1), p*64:(p+1)*64],
                          qp[64*a:64*(a+1), p*128:(p+1)*128],
                          start=False, stop=True, tile_position=(64*a, 64*a))
                PE.matmul(PS_B.ap()[64*a:64*a+1, 256:384],
                          ocol[:], scm[:], start=True, stop=False,
                          tile_position=(0, 64*a))
                PE.matmul(PS_B.ap()[64*a:64*a+1, 256:384],
                          ks_bf[64*a:64*(a+1), p:p+1],
                          qp[64*a:64*(a+1), p*128:(p+1)*128],
                          start=False, stop=True, tile_position=(64*a, 64*a))
                tctx.append(PE.tok())
            tn1 = V.tensor_scalar(out=rnrow[0:1, :], in0=PS_B.ap()[0:1, 256:384],
                                  scalar1=ATT_EPS, scalar2=None,
                                  op0=ALU.add, after=[tctx[0]])
            tn1 = V.reciprocal(out=rnrow[0:1, :], in_=rnrow[0:1, :])
            tn2 = V.tensor_scalar(out=rnrow[64:65, :], in0=PS_B.ap()[64:65, 256:384],
                                  scalar1=ATT_EPS, scalar2=None,
                                  op0=ALU.add, after=[tctx[1]])
            tn2 = V.reciprocal(out=rnrow[64:65, :], in_=rnrow[64:65, :])
            tcv = V.tensor_copy(out=ctxs[:], in_=PS_B.ap()[:, 128:256])
            war_cn = tcv
            for a in range(2):
                aftr = [tn1 if a == 0 else tn2] + ([war_rb] if (a == 0 and war_rb) else [])
                PE.matmul(PS_V.ap()[64*a:64*(a+1), 128:256],
                          onesf[64*a:64*a+1, :],
                          rnrow[64*a:64*a+1, :], start=True, stop=True,
                          tile_position=(64*a, 64*a), after=aftr)
            tper = PE.tok()
            war_rb = V.tensor_tensor(out=afm[:, p*128:(p+1)*128], in0=ctxs[:],
                                     in1=PS_V.ap()[:, 128:256], op=ALU.mult,
                                     after=[tper, tcv])
            state_update(p, tk4, tvv, use_gate=False)
        war_psv = [war_rb] if war_rb else []
        war_psv.append(war_sd)
        for f in range(2):
            for j in range(8):
                aft = ([war_psa, two, war_rb] if (f == 0 and j == 0) else [])
                PE.matmul(PS_A.ap()[:, f*512:(f+1)*512],
                          afm[:, j*128:(j+1)*128],
                          wo_b[:, j*1024 + f*512: j*1024 + (f+1)*512],
                          start=(j == 0), stop=False, after=aft)
            PE.matmul(PS_A.ap()[:, f*512:(f+1)*512], orow[:],
                      bo_r[0:1, f*512:(f+1)*512],
                      start=False, stop=True, after=([tb5] if f == 0 else []))
        tpeo = PE.tok()
        aft_ev = [tpeo] + ([war_oacc] if war_oacc else [])
        tv0 = V.tensor_copy(out=oacc[:], in_=PS_A.ap()[:], after=aft_ev)
        war_psa = tv0
        aft_ln2 = [war_sq] if war_sq else []
        e_ln(P, oacc[:], st, C, eps_ln, negm[:, 0:1], rstd[:, 0:1], after=aft_ln2)
        tv = V.tensor_scalar(out=relw[:], in0=oacc[:], scalar1=negm[:, 0:1],
                             scalar2=rstd[:, 0:1], op0=ALU.add, op1=ALU.mult)
        tv = V.tensor_add(out=oacc[:], in0=relw[:], in1=xt[:])
        war_xt = tv
        tst = GP.dma(xo_t[i], oacc[:], after=[tv])
        war_oacc = tst
        war_sq = None


def build_attn0(tsh=TSH, tpre=TSH):
    P = Prog()
    io = {
        "xpre": P.din("xpre", [tpre, C]),
        "xown": P.din("xown", [tsh, C]),
        "gate": P.din("gate", [128, 1], F32),
        "wq": P.din("wq", [C, C]),
        "wk": P.din("wk", [C, C]),
        "wv": P.din("wv", [C, C]),
        "wo": P.din("wo", [C, C]),
        "bq": P.din("bq", [1, C]),
        "bk": P.din("bk", [1, C]),
        "bv": P.din("bv", [1, C]),
        "bo": P.din("bo", [1, C]),
        "ones_row": P.din("ones_row", [1, 128], BF16),
        "ones_col": P.din("ones_col", [128, 1], BF16),
        "tri": P.din("tri", [128, 128], F32),
        "ident": P.din("ident", [128, 128], BF16),
        "consts": P.din("consts", [128, NCONST]),
        "xout": P.dout("xout", [tsh, C]),
    }
    P.begin()
    emit_attn0(P, io, tsh=tsh, tpre=tpre)
    nc = P.finish()
    return nc, P




def emit_mamba_a(P, io, tt=T, start_after=()):
    xin = io["xin"]
    lnw = io["lnw"]
    lnb = io["lnb"]
    win = io["win"]
    cw = io["cw"]
    cb = io["cb"]
    xpw = io["xpw"]
    ident_in = io["ident"]
    consts = io["consts"]
    xcs_o = io["xcs"]
    zs_o = io["zs"]
    dblp_o = io["dblp"]

    ngr = tt // 512
    x_t = xin.rearrange("(n p) c -> n p c", p=128)

    cst = P.sb("cst", [128, NCONST], F32)
    idb = P.sb("idb", [128, 128], BF16)
    lnw_s = P.sb("lnw_s", [128, 8], F32)
    lnb_s = P.sb("lnb_s", [128, 8], F32)
    cw_s = P.sb("cw_s", [128, 8 * D_CONV], F32)
    cb_s = P.sb("cb_s", [128, 8], F32)
    wstage = P.sb("wstage", [128, 2 * 2048], F32)
    win_b = P.sb("win_b", [128, 8 * 2048], BF16)
    xpw_b = P.sb("xpw_b", [128, 8 * 96], BF16)

    xt = P.sb("xt", [128, C], F32)
    sq = P.sb("sq", [128, C], F32)
    s1 = P.sb("s1", [128, 1], F32)
    s2 = P.sb("s2", [128, 1], F32)
    negm = P.sb("negm", [128, 1], F32)
    rstd = P.sb("rstd", [128, 1], F32)
    xlb = P.sb("xlb", [128, C], BF16)
    xlt = P.sb("xlt", [128, 8 * 512], BF16)
    xzb = P.sb("xzb", [128, 8 * 515], F32)
    convw = P.sb("convw", [128, 512], F32)
    xcs_g = P.sb("xcs_g", [128, 8 * 512], BF16)
    zs_g = P.sb("zs_g", [128, 8 * 512], BF16)
    dbl_g = P.sb("dbl_g", [96, 512], F32)

    ps_m = P.psum("ps_m", [128, 512])
    ps_m2 = P.psum("ps_m2", [128, 512])
    ps_t = P.psum("ps_t", [128, 128], BF16)
    ps_x = P.psum("ps_x", [96, 512])

    st = {"sq": sq, "s1": s1, "s2": s2}

    SY, GP, V, SC, PE = P.SY, P.GP, P.V, P.SC, P.PE

    t_c = SY.dma(cst[:], consts)
    t_i = SY.dma(idb[:], ident_in)
    t1 = SY.dma(lnw_s[:], lnw)
    t2 = SY.dma(lnb_s[:], lnb)
    t3 = SY.dma(cw_s[:], cw)
    t4 = SY.dma(cb_s[:], cb)
    SC.copy(out=s1[:, 0:1], in_=cst[:, 0:1], after=[t_c, t1, t2, t3, t4, t_i])
    twi = e_load_weight_bf16(P, win, wstage,
                             lambda j: win_b[:, j * 2048:(j + 1) * 2048], 8, 2048)
    twx = e_load_weight_bf16(P, xpw, wstage,
                             lambda j: xpw_b[:, j * 96:(j + 1) * 96], 8, 96)
    eps_ln = cst[:, K_LN_EPS:K_LN_EPS + 1]

    war_xt = None
    war_sq = None
    war_psm = [None, None]
    war_psx = None
    war_xlt = None
    war_convw = None
    t_hist = V.memset(xzb[:], 0.0)

    for g in range(ngr):
        for k in range(4):
            i = g * 4 + k
            aft_x = ([war_xt] if war_xt else []) + list(start_after)
            tx = SY.dma(xt[:], x_t[i], after=aft_x)
            aft = [tx] + ([war_sq] if war_sq else [])
            e_ln(P, xt[:], st, C, eps_ln, negm[:, 0:1], rstd[:, 0:1], after=aft)
            war_sq = None
            txl = V.tensor_scalar(out=xlb[:], in0=xt[:], scalar1=negm[:, 0:1],
                                  scalar2=rstd[:, 0:1], op0=ALU.add, op1=ALU.mult)
            war_xt = txl
            aft_t = [txl, t_i] + ([war_xlt] if (k == 0 and war_xlt) else [])
            war_xlt = None
            tt_tok = e_transpose_row(
                P, lambda j: xlb[:, j * 128:(j + 1) * 128],
                lambda j, k=k: xlt[:, j * 512 + k * 128: j * 512 + (k + 1) * 128],
                8, ps_t.ap()[:, 0:128], idb, after=aft_t, key="pst_a")
        tg = None
        for j in range(8):
            tg = V.tensor_scalar(out=xlt[:, j * 512:(j + 1) * 512],
                                 in0=xlt[:, j * 512:(j + 1) * 512],
                                 scalar1=lnw_s[:, j:j + 1], scalar2=lnb_s[:, j:j + 1],
                                 op0=ALU.mult, op1=ALU.add)
        for r in range(16):
            pb = r % 2
            psm = (ps_m, ps_m2)[pb]
            for j in range(8):
                aft = []
                if j == 0:
                    aft = [tg, twi]
                    if war_psm[pb]:
                        aft.append(war_psm[pb])
                PE.matmul(psm.ap()[:], win_b[:, j * 2048 + r * 128: j * 2048 + (r + 1) * 128],
                          xlt[:, j * 512:(j + 1) * 512],
                          start=(j == 0), stop=(j == 7), after=aft)
            tpe = PE.tok()
            if r < 8:
                tcp = V.tensor_copy(out=xzb[:, r * 515 + 3: r * 515 + 515],
                                    in_=psm.ap()[:], after=[tpe] +
                                    ([t_hist] if (t_hist and r == 0) else []))
                war_psm[pb] = tcp
                tv = V.tensor_scalar(out=convw[:], in0=xzb[:, r * 515: r * 515 + 512],
                                     scalar1=cw_s[:, r * 4:r * 4 + 1], scalar2=None,
                                     op0=ALU.mult,
                                     after=([war_convw] if war_convw else []))
                for kk in range(1, 4):
                    tv = V.scalar_tensor_tensor(
                        out=convw[:], in0=xzb[:, r * 515 + kk: r * 515 + kk + 512],
                        scalar=cw_s[:, r * 4 + kk:r * 4 + kk + 1], in1=convw[:],
                        op0=ALU.mult, op1=ALU.add)
                tsl = SC.activation(out=xcs_g[:, r * 512:(r + 1) * 512], in_=convw[:],
                                    func=AF.Silu, bias=cb_s[:, r:r + 1],
                                    after=[tv] + ([war_psm[0]] if (r == 0 and war_psm[0]) else []))
                war_convw = tsl
            else:
                rr = r - 8
                aft_z = [tpe]
                tsl = SC.activation(out=zs_g[:, rr * 512:(rr + 1) * 512],
                                    in_=psm.ap()[:], func=AF.Silu, after=aft_z)
                war_psm[pb] = tsl
        t_xc_done = tsl
        th = None
        for r in range(8):
            th = V.tensor_copy(out=xzb[:, r * 515: r * 515 + 3],
                               in_=xzb[:, r * 515 + 512: r * 515 + 515])
        t_hist = th
        for j in range(8):
            aft = []
            if j == 0:
                aft = [t_xc_done, twx]
                if war_psx:
                    aft.append(war_psx)
            PE.matmul(ps_x.ap()[:], xpw_b[:, j * 96:(j + 1) * 96],
                      xcs_g[:, j * 512:(j + 1) * 512],
                      start=(j == 0), stop=(j == 7), after=aft)
        tpx = PE.tok()
        war_psx = V.tensor_copy(out=dbl_g[:], in_=ps_x.ap()[:], after=[tpx])
        xcs_ot = xcs_o.rearrange("(r p) t -> r p t", p=128)
        zs_ot = zs_o.rearrange("(r p) t -> r p t", p=128)
        ts1_ = ts2_ = None
        for r in range(8):
            ts1_ = GP.dma(xcs_ot[r][:, g * 512:(g + 1) * 512],
                          xcs_g[:, r * 512:(r + 1) * 512], after=[t_xc_done])
            ts2_ = GP.dma(zs_ot[r][:, g * 512:(g + 1) * 512],
                          zs_g[:, r * 512:(r + 1) * 512])
        ts3_ = GP.dma(dblp_o[:, g * 512:(g + 1) * 512], dbl_g[:], after=[war_psx])
        war_psm = [ts2_, ts3_]
        war_xlt = tpe


def build_mamba_a(tt=T):
    P = Prog()
    io = {
        "xin": P.din("xin", [tt, C]),
        "lnw": P.din("lnw", [128, 8]),
        "lnb": P.din("lnb", [128, 8]),
        "win": P.din("win", [C, 2048]),
        "cw": P.din("cw", [128, 8 * D_CONV]),
        "cb": P.din("cb", [128, 8]),
        "xpw": P.din("xpw", [C, 96]),
        "ident": P.din("ident", [128, 128], BF16),
        "consts": P.din("consts", [128, NCONST]),
        "xcs": P.dout("xcs", [C, tt], BF16),
        "zs": P.dout("zs", [C, tt], BF16),
        "dblp": P.dout("dblp", [96, tt]),
    }
    P.begin()
    emit_mamba_a(P, io, tt=tt)
    nc = P.finish()
    return nc, P


def emit_mamba_b(P, io, tt=T, lc=1024, upto=99, sub=9, nosync=0, start_after=()):
    xcs_i = io["xcs"]
    zs_i = io["zs"]
    dbl0 = io["dbl0"]
    dbl1 = io["dbl1"]
    dtw = io["dtw"]
    dtb = io["dtb"]
    alog = io["alog"]
    dsk = io["dsk"]
    selbc = io["selbc"]
    ident_in = io["ident"]
    consts = io["consts"]
    yg_o = io["yg"]

    nlc = tt // lc

    cst = P.sb("cst", [128, NCONST], F32)
    idb = P.sb("idb", [128, 128], BF16)
    dtb_s = P.sb("dtb_s", [128, 8], F32)
    acol = P.sb("acol", [128, 8 * D_ST], F32)
    dsk_s = P.sb("dsk_s", [128, 8], F32)
    selb = P.sb("selb", [32, 2 * D_ST * 128], BF16)
    wstage = P.sb("wstage", [128, 1024], F32)
    dtw_b = P.sb("dtw_b", [DT_RANK, C], BF16)
    hst = P.sb("hst", [128, 8 * D_ST], F32)

    dtr = P.sb("dtr", [DT_RANK, lc], BF16)
    bcr = P.sb("bcr", [32, lc], BF16)
    dt_f = P.sb("dt_f", [128, 8 * lc], F32)
    u_f = P.sb("u_f", [128, 8 * lc], BF16)
    xc_f = P.sb("xc_f", [128, 8 * lc], BF16)
    zs_f = P.sb("zs_f", [128, 8 * lc], BF16)
    Ball = P.sb("Ball", [128, D_ST * lc], BF16)
    Call = P.sb("Call", [128, D_ST * lc], BF16)
    dA = P.sb("dA", [128, 2 * lc], F32)
    dBu = P.sb("dBu", [128, 2 * lc], BF16)
    hbuf = P.sb("hbuf", [128, 2 * lc], BF16)
    prod = P.sb("prod", [128, 2 * lc], BF16)
    yv = P.sb("yv", [128, lc], F32)
    ygb = P.sb("ygb", [128, lc], BF16)
    sp1 = P.sb("sp1", [128, lc], F32)
    sp2 = P.sb("sp2", [128, lc], F32)

    ps_d = P.psum("ps_d", [128, 512])
    ps_b = P.psum("ps_b", [128, 1024])
    ps_y = P.psum("ps_y", [128, 1024])

    SY, GP, V, SC, PE = P.SY, P.GP, P.V, P.SC, P.PE
    if nosync:
        V.selfsync = False
        SC.selfsync = False

    t_c = SY.dma(cst[:], consts)
    t_i = SY.dma(idb[:], ident_in)
    t1 = SY.dma(dtb_s[:], dtb)
    t2 = SY.dma(acol[:], alog)
    t3 = SY.dma(dsk_s[:], dsk)
    t4 = SY.dma(selb[:], selbc)
    tf = SC.copy(out=hst[:, 0:1], in_=cst[:, 0:1], after=[t_c, t1, t2, t3, t4, t_i])
    ta = SC.activation(out=acol[:], in_=acol[:], func=AF.Exp, after=[t2])
    ta = V.tensor_scalar(out=acol[:], in0=acol[:], scalar1=-1.0, scalar2=None,
                         op0=ALU.mult, after=[ta])
    t5 = SY.dma(wstage[0:DT_RANK, 0:C], dtw)
    t6 = SC.copy(out=dtw_b[:], in_=wstage[0:DT_RANK, 0:C], after=[t5])
    tz = V.memset(hst[:], 0.0, after=[tf])
    war = {}
    war["stage_d"] = t6

    xcs_t = xcs_i.rearrange("(r p) t -> r p t", p=128)
    zs_t = zs_i.rearrange("(r p) t -> r p t", p=128)
    yg_t = yg_o.rearrange("(r p) t -> r p t", p=128)

    def W(key):
        if key not in war:
            return []
        v = war[key]
        return list(v) if isinstance(v, list) else [v]

    for li in range(nlc):
        t0 = li * lc
        tld = []
        for r in range(8):
            tld.append(SY.dma(xc_f[:, r * lc:(r + 1) * lc], xcs_t[r][:, t0:t0 + lc],
                              after=W("xc_f")))
            tld.append(SY.dma(zs_f[:, r * lc:(r + 1) * lc], zs_t[r][:, t0:t0 + lc],
                              after=W("zs_f")))
        tda = SY.dma(wstage[0:DT_RANK, 0:lc], dbl0[0:DT_RANK, t0:t0 + lc],
                     after=W("stage_d") + list(start_after))
        tdb = SY.dma(sp2[0:DT_RANK, 0:lc], dbl1[0:DT_RANK, t0:t0 + lc],
                     after=W("sp2"))
        tdc = V.tensor_add(out=sp1[0:DT_RANK, 0:lc], in0=wstage[0:DT_RANK, 0:lc],
                           in1=sp2[0:DT_RANK, 0:lc], after=[tda, tdb] + W("sp1"))
        tdd = V.tensor_copy(out=dtr[:], in_=sp1[0:DT_RANK, 0:lc], after=W("dtr"))
        tba_ = SY.dma(wstage[0:32, 0:lc], dbl0[DT_RANK:96, t0:t0 + lc],
                      after=[tdc])
        tbb = SY.dma(sp2[0:32, 0:lc], dbl1[DT_RANK:96, t0:t0 + lc], after=[tdc])
        war["stage_d"] = tbc = V.tensor_add(out=sp1[0:32, 0:lc], in0=wstage[0:32, 0:lc],
                                            in1=sp2[0:32, 0:lc],
                                            after=[tba_, tbb])
        war["sp2"] = tbc
        tbd = V.tensor_copy(out=bcr[:], in_=sp1[0:32, 0:lc], after=W("bcr"))
        if upto < 1:
            continue
        tev = None
        for n in range(2 * D_ST):
            dst = Ball if n < D_ST else Call
            nn = n % D_ST
            for h in range(lc // 512):
                aft = [tbd] if h == 0 else []
                if n == 0 and h == 0:
                    aft += W("ps_b")
                if h == 0 and tev is not None:
                    aft.append(tev)
                PE.matmul(ps_b.ap()[:, h * 512:(h + 1) * 512],
                          selb[:, n * 128:(n + 1) * 128],
                          bcr[:, h * 512:(h + 1) * 512],
                          start=True, stop=True, after=aft)
            tpb = PE.tok()
            war["bcr"] = tpb
            tev = SC.copy(out=dst[:, nn * lc:(nn + 1) * lc], in_=ps_b.ap()[:, 0:lc],
                          after=[tpb] + ((W("Ball") + W("Call")) if n == 0 else []))
        war["ps_b"] = tev
        t_bc = tev
        if upto < 2:
            continue
        import os
        rlim = int(os.environ.get("R_LIMIT", "8"))
        tdt = None
        for r in range(rlim):
            for h in range(lc // 512):
                aft = W("ps_d") + W("sp1")
                if r == 0 and h == 0:
                    aft += [tdd, t6]
                PE.matmul(ps_d.ap()[:],
                          dtw_b[:, r * 128:(r + 1) * 128],
                          dtr[:, h * 512:(h + 1) * 512],
                          start=True, stop=True, after=aft)
                tpd = PE.tok()
                sl = slice(r * lc + h * 512, r * lc + (h + 1) * 512)
                war["dtr"] = tpd
                if sub == -10:
                    tso = SC.activation(out=sp1[:, 0:512], in_=ps_d.ap()[:], func=AF.Abs,
                                        after=[tpd] + W("sp1"))
                    trl = V.tensor_scalar(out=sp2[:, 0:512], in0=ps_d.ap()[:],
                                          scalar1=dtb_s[:, r:r + 1], scalar2=0.0,
                                          op0=ALU.add, op1=ALU.max)
                    war["ps_d"] = trl
                    tdt = GP.tensor_add(out=dt_f[:, sl], in0=sp2[:, 0:512],
                                        in1=sp1[:, 0:512], after=[tso, trl])
                    war["sp1"] = tdt
                    continue
                if sub == -11:
                    tso = SC.activation(out=sp1[:, 0:512], in_=ps_d.ap()[:], func=AF.Copy,
                                        bias=dtb_s[:, r:r + 1], after=[tpd] + W("sp1"))
                    trl = V.tensor_scalar(out=sp2[:, 0:512], in0=ps_d.ap()[:],
                                          scalar1=dtb_s[:, r:r + 1], scalar2=0.0,
                                          op0=ALU.add, op1=ALU.max)
                    war["ps_d"] = trl
                    tdt = GP.tensor_add(out=dt_f[:, sl], in0=sp2[:, 0:512],
                                        in1=sp1[:, 0:512], after=[tso, trl])
                    war["sp1"] = tdt
                    continue
                if sub == -8:
                    tso = SC.copy(out=sp1[:, 0:512], in_=ps_d.ap()[:],
                                  after=[tpd] + W("sp1"))
                    trl = V.tensor_scalar(out=sp2[:, 0:512], in0=ps_d.ap()[:],
                                          scalar1=dtb_s[:, r:r + 1], scalar2=0.0,
                                          op0=ALU.add, op1=ALU.max)
                    war["ps_d"] = trl
                    tdt = GP.tensor_add(out=dt_f[:, sl], in0=sp2[:, 0:512],
                                        in1=sp1[:, 0:512], after=[tso, trl])
                    war["sp1"] = tdt
                    continue
                if sub == -9:
                    tso = SC.copy(out=sp1[:, 0:512], in_=ps_d.ap()[:],
                                  after=[tpd] + W("sp1"))
                    trl = V.tensor_scalar(out=sp2[:, 0:512], in0=ps_d.ap()[:],
                                          scalar1=dtb_s[:, r:r + 1], scalar2=0.0,
                                          op0=ALU.add, op1=ALU.max)
                    war["ps_d"] = trl
                    tdt = V.tensor_add(out=sp2[:, 0:512], in0=sp2[:, 0:512],
                                       in1=sp1[:, 0:512])
                    war["sp1"] = tdt
                    continue
                if sub == -7:
                    tso = SC.copy(out=sp1[:, 0:512], in_=ps_d.ap()[:],
                                  after=[tpd] + W("sp1"))
                    trl = V.tensor_scalar(out=sp2[:, 0:512], in0=ps_d.ap()[:],
                                          scalar1=dtb_s[:, r:r + 1], scalar2=0.0,
                                          op0=ALU.add, op1=ALU.max)
                    war["ps_d"] = trl
                    war["sp1"] = tso
                    tdt = trl
                    continue
                if sub <= -3 and sub not in (-5, -6):
                    continue
                if sub == -2:
                    trl = V.tensor_scalar(out=sp2[:, 0:512], in0=ps_d.ap()[:],
                                          scalar1=dtb_s[:, r:r + 1], scalar2=0.0,
                                          op0=ALU.add, op1=ALU.max)
                    war["ps_d"] = trl
                    tdt = trl
                    continue
                if sub == -1:
                    tso = SC.copy(out=sp1[:, 0:512], in_=ps_d.ap()[:],
                                  after=[tpd] + W("sp1"))
                    war["ps_d"] = tso
                    war["sp1"] = tso
                    tdt = tso
                    continue
                if sub == -5:
                    tso = SC.copy(out=sp1[:, 0:512], in_=ps_d.ap()[:],
                                  after=[tpd] + W("sp1"))
                    trl = V.tensor_scalar(out=sp2[:, 0:512], in0=ps_d.ap()[:],
                                          scalar1=dtb_s[:, r:r + 1], scalar2=0.0,
                                          op0=ALU.add, op1=ALU.max)
                    war["ps_d"] = trl
                    tdt = V.tensor_add(out=sp2[:, 0:512], in0=sp2[:, 0:512],
                                       in1=sp1[:, 0:512], after=[tso])
                    war["sp1"] = tdt
                    continue
                if sub == -6:
                    tso = SC.copy(out=sp1[:, 0:512], in_=ps_d.ap()[:],
                                  after=[tpd] + W("sp1"))
                    trl = V.tensor_scalar(out=sp2[:, 0:512], in0=ps_d.ap()[:],
                                          scalar1=dtb_s[:, r:r + 1], scalar2=0.0,
                                          op0=ALU.add, op1=ALU.max)
                    war["ps_d"] = trl
                    tdt = V.tensor_copy(out=dt_f[:, sl], in_=sp2[:, 0:512])
                    war["sp1"] = tso
                    continue
                tpre = V.tensor_scalar(out=sp1[:, 0:512], in0=ps_d.ap()[:],
                                       scalar1=dtb_s[:, r:r + 1], scalar2=None,
                                       op0=ALU.add, after=[tpd] + W("sp1"))
                trl = V.tensor_scalar(out=sp2[:, 0:512], in0=ps_d.ap()[:],
                                      scalar1=dtb_s[:, r:r + 1], scalar2=0.0,
                                      op0=ALU.add, op1=ALU.max)
                war["ps_d"] = trl
                tso = SC.activation(out=sp1[:, 0:512], in_=sp1[:, 0:512], func=AF.Abs,
                                    after=[tpre])
                tso = SC.activation(out=sp1[:, 0:512], in_=sp1[:, 0:512], func=AF.Exp,
                                    scale=cst[:, K_NEG1:K_NEG1 + 1])
                tso = SC.activation(out=sp1[:, 0:512], in_=sp1[:, 0:512], func=AF.Ln,
                                    bias=cst[:, K_ONE:K_ONE + 1])
                tdt = GP.tensor_add(out=dt_f[:, sl], in0=sp2[:, 0:512],
                                    in1=sp1[:, 0:512], after=[tso, trl])
                war["sp1"] = tdt
                war["sp2"] = tdt
        if upto < 3:
            continue
        tu = None
        for r in range(8):
            tu = V.tensor_mul(out=u_f[:, r * lc:(r + 1) * lc],
                              in0=dt_f[:, r * lc:(r + 1) * lc],
                              in1=xc_f[:, r * lc:(r + 1) * lc],
                              after=[tld[2 * r], tdt] + W("u_f"))
        if upto < 4:
            continue
        for r in range(8):
            for n in range(D_ST):
                s = n % 2
                sl_s = slice(s * lc, (s + 1) * lc)
                tda_ = SC.activation(out=dA[:, sl_s], in_=dt_f[:, r * lc:(r + 1) * lc],
                                     func=AF.Exp,
                                     scale=acol[:, r * D_ST + n:r * D_ST + n + 1],
                                     after=[tdt] + W(f"dA{s}"))
                tdbu = V.tensor_mul(out=dBu[:, sl_s], in0=u_f[:, r * lc:(r + 1) * lc],
                                    in1=Ball[:, n * lc:(n + 1) * lc],
                                    after=[tu, t_bc] + W(f"dBu{s}"))
                tsc = V.tensor_tensor_scan(out=hbuf[:, sl_s], data0=dA[:, sl_s],
                                           data1=dBu[:, sl_s],
                                           initial=hst[:, r * D_ST + n:r * D_ST + n + 1],
                                           op0=ALU.mult, op1=ALU.add,
                                           after=[tda_, tz] + W(f"hbuf{s}"))
                war[f"dA{s}"] = tsc
                war[f"dBu{s}"] = tsc
                ths = V.tensor_copy(out=hst[:, r * D_ST + n:r * D_ST + n + 1],
                                     in_=hbuf[:, s * lc + lc - 1: s * lc + lc],
                                     after=[tsc])
                tpr = GP.tensor_mul(out=prod[:, sl_s], in0=hbuf[:, sl_s],
                                    in1=Call[:, n * lc:(n + 1) * lc],
                                    after=[tsc] + W(f"prod{s}"))
                war[f"hbuf{s}"] = [tpr, ths]
                war["Ball"] = tdbu
                war["Call"] = tpr
                for h in range(lc // 512):
                    aft = [tpr]
                    if n == 0 and h == 0:
                        aft += W("ps_y")
                    PE.matmul(ps_y.ap()[:, h * 512:(h + 1) * 512],
                              idb[:], prod[:, s * lc + h * 512: s * lc + (h + 1) * 512],
                              start=(n == 0), stop=(n == D_ST - 1), after=aft)
                war[f"prod{s}"] = PE.tok()
            tpy = PE.tok()
            ty1 = V.scalar_tensor_tensor(out=yv[:], in0=xc_f[:, r * lc:(r + 1) * lc],
                                         scalar=dsk_s[:, r:r + 1], in1=ps_y.ap()[:, 0:lc],
                                         op0=ALU.mult, op1=ALU.add, after=[tpy])
            war["ps_y"] = ty1
            ty2 = V.tensor_mul(out=ygb[:], in0=yv[:], in1=zs_f[:, r * lc:(r + 1) * lc],
                               after=[tld[2 * r + 1]] + W("ygb"))
            war["ygb"] = tst = GP.dma(yg_t[r][:, t0:t0 + lc], ygb[:], after=[ty2])
        war["xc_f"] = ty1
        war["zs_f"] = ty2


def build_mamba_b(tt=T, lc=1024, upto=99, sub=9, nosync=0):
    P = Prog()
    io = {
        "xcs": P.din("xcs", [C, tt], BF16),
        "zs": P.din("zs", [C, tt], BF16),
        "dbl0": P.din("dbl0", [96, tt]),
        "dbl1": P.din("dbl1", [96, tt]),
        "dtw": P.din("dtw", [DT_RANK, C]),
        "dtb": P.din("dtb", [128, 8]),
        "alog": P.din("alog", [128, 8 * D_ST]),
        "dsk": P.din("dsk", [128, 8]),
        "selbc": P.din("selbc", [32, 2 * D_ST * 128], BF16),
        "ident": P.din("ident", [128, 128], BF16),
        "consts": P.din("consts", [128, NCONST]),
        "yg": P.dout("yg", [C, tt], BF16),
    }
    P.begin()
    emit_mamba_b(P, io, tt=tt, lc=lc, upto=upto, sub=sub, nosync=nosync)
    nc = P.finish()
    return nc, P


def emit_mamba_c(P, io, tsh=TSH, final=False, start_after=(), yg_colofs=0):
    yg = io["yg"]
    xres_in = io["xres"]
    wout = io["wout"]
    ident_in = io["ident"]
    consts = io["consts"]
    if final:
        nfw = io["nfw"]
        nfb = io["nfb"]
        xout2 = io["xout2"]
    x_out = io["xout"]

    ntt = tsh // 128
    xr_t = xres_in.rearrange("(n p) c -> n p c", p=128)
    xo_t = x_out.rearrange("(n p) c -> n p c", p=128)
    if final:
        x2_t = xout2.rearrange("(n p) c -> n p c", p=128)
    yg_t = yg.rearrange("(r p) t -> r p t", p=128)

    cst = P.sb("cst", [128, NCONST], F32)
    wstage = P.sb("wstage", [128, 2 * 1024], F32)
    wo_b = P.sb("wo_b", [128, 16 * 1024], BF16)
    ygs = P.sb("ygs", [128, 16 * 128], BF16)
    xt = P.sb("xt", [128, C], F32)
    ot = P.sb("ot", [128, C], F32)
    sq = P.sb("sq", [128, C], F32)
    s1 = P.sb("s1", [128, 1], F32)
    s2 = P.sb("s2", [128, 1], F32)
    negm = P.sb("negm", [128, 1], F32)
    rstd = P.sb("rstd", [128, 1], F32)
    if final:
        nfw_s = P.sb("nfw_s", [128, C], F32)
        nfb_s = P.sb("nfb_s", [128, C], F32)
        x2t = P.sb("x2t", [128, C], F32)
    ps_o = P.psum("ps_o", [128, 1024])

    st = {"sq": sq, "s1": s1, "s2": s2}

    SY, GP, V, SC, PE = P.SY, P.GP, P.V, P.SC, P.PE
    t_c = SY.dma(cst[:], consts)
    SC.copy(out=s1[:, 0:1], in_=cst[:, 0:1], after=[t_c])
    if final:
        tn1 = SY.dma(nfw_s[:], nfw)
        tn2 = SY.dma(nfb_s[:], nfb)
        V.tensor_copy(out=s2[:, 0:1], in_=cst[:, 0:1], after=[tn1, tn2])
    two = e_load_weight_bf16(P, wout, wstage,
                             lambda j: wo_b[:, j * 1024:(j + 1) * 1024], 16, 1024)
    eps_ln = cst[:, K_LN_EPS:K_LN_EPS + 1]

    war = {}

    def W(key):
        return [war[key]] if key in war else []

    for i in range(ntt):
        c0 = yg_colofs + i * 128
        tld = [SY.dma(ygs[:].rearrange("p (r t) -> p r t", r=16),
                      yg.rearrange("(r p) t -> p r t", p=128)[:, :, c0:c0 + 128],
                      after=W("ygs") + (list(start_after) if i == 0 else []))]
        tx = SY.dma(xt[:], xr_t[i], after=W("xt"))
        if final:
            tx2 = SY.dma(x2t[:], x2_t[i], after=W("x2t"))
        for f in range(2):
            for j in range(16):
                aft = []
                if f == 0 and j == 0:
                    aft = [tld[0], two] + W("ps_o")
                PE.matmul(ps_o.ap()[:, f * 512:(f + 1) * 512],
                          ygs[:, j * 128:(j + 1) * 128],
                          wo_b[:, j * 1024 + f * 512: j * 1024 + (f + 1) * 512],
                          start=(j == 0), stop=(j == 15), after=aft)
        tpe = PE.tok()
        war["ygs"] = tpe
        tv = V.tensor_add(out=xt[:], in0=ps_o.ap()[:], in1=xt[:], after=[tpe, tx])
        war["ps_o"] = tv
        if not final:
            war["xt"] = GP.dma(xo_t[i], xt[:], after=[tv])
        else:
            aft = W("sq")
            e_ln(P, xt[:], st, C, eps_ln, negm[:, 0:1], rstd[:, 0:1], after=aft)
            tv = V.tensor_scalar(out=ot[:], in0=xt[:], scalar1=negm[:, 0:1],
                                 scalar2=rstd[:, 0:1], op0=ALU.add, op1=ALU.mult)
            war["xt"] = tv
            tv = V.tensor_mul(out=ot[:], in0=ot[:], in1=nfw_s[:])
            tv = V.tensor_add(out=ot[:], in0=ot[:], in1=nfb_s[:])
            e_ln(P, ot[:], st, C, eps_ln, negm[:, 0:1], rstd[:, 0:1])
            tv = V.tensor_scalar(out=ot[:], in0=ot[:], scalar1=negm[:, 0:1],
                                 scalar2=rstd[:, 0:1], op0=ALU.add, op1=ALU.mult)
            tv = V.tensor_add(out=ot[:], in0=ot[:], in1=x2t[:], after=[tx2])
            war["x2t"] = tv
            war["sq"] = tv
            war["xt"] = GP.dma(xo_t[i], ot[:], after=[tv])


def build_mamba_c(tsh=TSH, final=False):
    P = Prog()
    io = {
        "yg": P.din("yg", [D_INNER, tsh], BF16),
        "xres": P.din("xres", [tsh, C]),
        "wout": P.din("wout", [D_INNER, C]),
        "ident": P.din("ident", [128, 128], BF16),
        "consts": P.din("consts", [128, NCONST]),
        "xout": P.dout("xout", [tsh, C]),
    }
    if final:
        io["nfw"] = P.din("nfw", [128, C])
        io["nfb"] = P.din("nfb", [128, C])
        io["xout2"] = P.din("xout2", [tsh, C])
    P.begin()
    emit_mamba_c(P, io, tsh=tsh, final=final)
    nc = P.finish()
    return nc, P




def _emit_scores(P, V, SC, PE, xlt, wgt_b, lb_s, g8, gm, s1, s2, tt_tok, war_psg,
                 ps_g):
    for j in range(8):
        aft = [tt_tok] + ([war_psg] if war_psg else []) if j == 0 else []
        PE.matmul(ps_g[:], xlt[:, j * 128:(j + 1) * 128],
                  wgt_b[:, j * MOE_E:(j + 1) * MOE_E],
                  start=(j == 0), stop=(j == 7), after=aft)
    tpe_g = PE.tok()
    tl = V.tensor_add(out=g8[:], in0=ps_g[:], in1=lb_s[:], after=[tpe_g])
    tsg = SC.activation(out=g8[:], in_=g8[:], func=AF.Sigmoid, after=[tl])
    t1 = V.reduce_max(out=s1[:, 0:1], in_=g8[:], axis=AX.X, after=[tsg])
    t2 = V.tensor_scalar(out=gm[:], in0=g8[:], scalar1=s1[:, 0:1], scalar2=None,
                         op0=ALU.is_ge)
    t3 = V.tensor_scalar(out=gm[:], in0=gm[:], scalar1=-1.0, scalar2=1.0,
                         op0=ALU.mult, op1=ALU.add)
    t4 = V.tensor_mul(out=gm[:], in0=g8[:], in1=gm[:])
    t5 = V.tensor_reduce(out=s2[:, 0:1], in_=gm[:], axis=AX.X, op=ALU.max)
    t6 = V.tensor_scalar(out=gm[:], in0=g8[:], scalar1=s2[:, 0:1], scalar2=None,
                         op0=ALU.is_ge)
    t7 = V.tensor_add(out=s2[:, 0:1], in0=s2[:, 0:1], in1=s1[:, 0:1])
    t8 = V.tensor_scalar(out=s2[:, 0:1], in0=s2[:, 0:1], scalar1=1e-8, scalar2=None,
                         op0=ALU.add)
    t9 = V.reciprocal(out=s2[:, 0:1], in_=s2[:, 0:1])
    return t6, t9, tpe_g


def emit_ffn1a(P, io, tsh=TSH):
    x_in = io["x"]
    wgt = io["wgt"]
    lb = io["lb"]
    ones_col = io["ones_col"]
    ident_in = io["ident"]
    consts = io["consts"]
    part_o = io["part"]

    ntt = tsh // 128
    x_t = x_in.rearrange("(n p) c -> n p c", p=128)

    cst = P.sb("cst", [128, NCONST], F32)
    idb = P.sb("idb", [128, 128], BF16)
    ocol = P.sb("ocol", [128, 1], BF16)
    wstage = P.sb("wstage", [128, 2 * MOE_E], F32)
    wgt_b = P.sb("wgt_b", [128, 8 * MOE_E], BF16)
    lb_s = P.sb("lb_s", [128, MOE_E], F32)
    xt = P.sb("xt", [128, C], F32)
    sq = P.sb("sq", [128, C], F32)
    s1 = P.sb("s1", [128, 1], F32)
    s2 = P.sb("s2", [128, 1], F32)
    negm = P.sb("negm", [128, 1], F32)
    rstd = P.sb("rstd", [128, 1], F32)
    xlb = P.sb("xlb", [128, C], BF16)
    xlt = P.sb("xlt", [128, C], BF16)
    g8 = P.sb("g8", [128, MOE_E], F32)
    gm = P.sb("gm", [128, MOE_E], F32)
    stk = P.sb("stk", [128, 2 * MOE_E], BF16)
    pres = P.sb("pres", [16, 1], F32)
    ps_g = P.psum("ps_g", [128, MOE_E])
    ps_t = P.psum("ps_t", [128, 128], BF16)
    ps_p = P.psum("ps_p", [16, 1])
    st = {"sq": sq, "s1": s1, "s2": s2}

    SY, GP, V, SC, PE = P.SY, P.GP, P.V, P.SC, P.PE
    t_c = SY.dma(cst[:], consts)
    t_i = SY.dma(idb[:], ident_in)
    t_o = SY.dma(ocol[:], ones_col)
    t_l = SY.dma(lb_s[:], lb)
    SC.copy(out=s1[:, 0:1], in_=cst[:, 0:1], after=[t_c, t_i, t_o, t_l])
    twg = e_load_weight_bf16(P, wgt, wstage,
                             lambda j: wgt_b[:, j * MOE_E:(j + 1) * MOE_E], 8, MOE_E)
    eps_ln = cst[:, K_LN_EPS:K_LN_EPS + 1]

    war_xt = None
    war_sq = None
    war_psg = None
    war_stk = None
    for i in range(ntt):
        aft_x = [war_xt] if war_xt else []
        tx = SY.dma(xt[:], x_t[i], after=aft_x)
        aft = [tx] + ([war_sq] if war_sq else [])
        e_ln(P, xt[:], st, C, eps_ln, negm[:, 0:1], rstd[:, 0:1], after=aft)
        txl = V.tensor_scalar(out=xlb[:], in0=xt[:], scalar1=negm[:, 0:1],
                              scalar2=rstd[:, 0:1], op0=ALU.add, op1=ALU.mult)
        war_xt = txl
        tt_tok = e_transpose_row(P, lambda j: xlb[:, j * 128:(j + 1) * 128],
                                 lambda j: xlt[:, j * 128:(j + 1) * 128],
                                 8, ps_t, idb, after=[txl, t_i], key="pst")
        tm, tr, tpe_g = _emit_scores(P, V, SC, PE, xlt, wgt_b, lb_s, g8, gm,
                                     s1, s2, tt_tok, war_psg, ps_g.ap())
        war_psg = tm
        tw = V.tensor_mul(out=sq[:, 0:MOE_E], in0=g8[:], in1=gm[:], after=[tm])
        tw = V.tensor_scalar(out=stk[:, 0:MOE_E], in0=sq[:, 0:MOE_E],
                             scalar1=s2[:, 0:1], scalar2=None, op0=ALU.mult,
                             after=[tr] + ([war_stk] if war_stk else []))
        tw2 = V.tensor_copy(out=stk[:, MOE_E:2 * MOE_E], in_=gm[:])
        PE.matmul(ps_p.ap()[:], stk[:], ocol[:], start=(i == 0), stop=(i == ntt - 1),
                  after=[tw, tw2])
        war_stk = PE.tok()
        war_sq = None
    tpp = PE.tok()
    tv = V.tensor_copy(out=pres[:], in_=ps_p.ap()[:], after=[tpp])
    GP.dma(part_o, pres[:], after=[tv])


def build_ffn1a(tsh=TSH):
    P = Prog()
    io = {
        "x": P.din("x", [tsh, C]),
        "wgt": P.din("wgt", [C, MOE_E]),
        "lb": P.din("lb", [128, MOE_E]),
        "ones_col": P.din("ones_col", [128, 1], BF16),
        "ident": P.din("ident", [128, 128], BF16),
        "consts": P.din("consts", [128, NCONST]),
        "part": P.dout("part", [16, 1]),
    }
    P.begin()
    emit_ffn1a(P, io, tsh=tsh)
    nc = P.finish()
    return nc, P


def build_ffn1b(tsh=TSH):
    P = Prog()
    x_in = P.din("x", [tsh, C])
    pw_t = P.din("pw_t", [MOE_E, 8])
    pc_t = P.din("pc_t", [MOE_E, 8])
    wgt = P.din("wgt", [C, MOE_E])
    lb = P.din("lb", [128, MOE_E])
    sgw = P.din("sgw", [C, 1024])
    suw = P.din("suw", [C, 1024])
    sdw = P.din("sdw", [1024, C])
    egw = P.din("egw", [MOE_E, C, 512])
    euw = P.din("euw", [MOE_E, C, 512])
    edw = P.din("edw", [MOE_E, 512, C])
    ident_in = P.din("ident", [128, 128], BF16)
    identf = P.din("identf", [8, 8])
    consts = P.din("consts", [128, NCONST])
    x_out = P.dout("xout", [tsh, C])

    ntt = tsh // 128
    QT = min(8, ntt)
    NQ = ntt // QT
    x_t = x_in.rearrange("(n p) c -> n p c", p=128)
    xo_t = x_out.rearrange("(n p) c -> n p c", p=128)

    cst = P.sb("cst", [128, NCONST], F32)
    idb = P.sb("idb", [128, 128], BF16)
    idf = P.sb("idf", [8, 8], F32)
    wstage = P.sb("wstage", [128, 2 * 1024], F32)
    wgt_b = P.sb("wgt_b", [128, 8 * MOE_E], BF16)
    lb_s = P.sb("lb_s", [128, MOE_E], F32)
    sgw_b = P.sb("sgw_b", [128, 8 * 1024], BF16)
    suw_b = P.sb("suw_b", [128, 8 * 1024], BF16)
    sdw_b = P.sb("sdw_b", [128, 8 * 1024], BF16)
    egw_b = P.sb("egw_b", [128, 2 * 8 * 512], BF16)
    euw_b = P.sb("euw_b", [128, 2 * 8 * 512], BF16)
    edw_b = P.sb("edw_b", [128, 2 * 4 * 1024], BF16)
    pstk = P.sb("pstk", [MOE_E, 20], F32)
    wm_s = P.sb("wm_s", [128, MOE_E], F32)
    onesf = P.sb("onesf", [128, 128], F32)

    xlt_q = P.sb("xlt_q", [128, QT * 1024], BF16)
    acc = P.sb("acc", [128, QT * 1024], F32)
    xres = P.sb("xres", [128, QT * 1024], F32)
    coefs = P.sb("coefs", [128, QT * MOE_E], F32)

    xt = P.sb("xt", [128, C], F32)
    sq = P.sb("sq", [128, 2048], F32)
    s1 = P.sb("s1", [128, 1], F32)
    s2 = P.sb("s2", [128, 1], F32)
    negm = P.sb("negm", [128, 1], F32)
    rstd = P.sb("rstd", [128, 1], F32)
    xlb = P.sb("xlb", [128, C], BF16)
    g8 = P.sb("g8", [128, MOE_E], F32)
    gm = P.sb("gm", [128, MOE_E], F32)
    gcol = P.sb("gcol", [128, 1], F32)
    siln = P.sb("siln", [128, 1024], BF16)
    mid = P.sb("mid", [128, 1024], BF16)
    midt = P.sb("midt", [128, 1024], BF16)

    ps_u = P.psum("ps_u", [128, 2048])
    ps_o = P.psum("ps_o", [128, 1024])
    ps_t = P.psum("ps_t", [128, 128], BF16)
    ps_g = P.psum("ps_g", [128, MOE_E])
    st = {"sq": sq, "s1": s1, "s2": s2}

    P.begin()
    SY, GP, V, SC, PE = P.SY, P.GP, P.V, P.SC, P.PE
    t_c = SY.dma(cst[:], consts)
    t_i = SY.dma(idb[:], ident_in)
    t_if = SY.dma(idf[:], identf)
    t_l = SY.dma(lb_s[:], lb)
    t_p1 = SY.dma(pstk[:, 0:8], pw_t)
    t_p2 = SY.dma(pstk[:, 8:16], pc_t)
    SC.copy(out=s1[:, 0:1], in_=cst[:, 0:1], after=[t_c, t_i, t_if, t_l, t_p1, t_p2])
    tz = V.memset(onesf[:], 1.0)
    tg1 = V.tensor_reduce(out=pstk[:, 16:17], in_=pstk[:, 0:8], axis=AX.X, op=ALU.add,
                          after=[t_p1])
    tg2 = V.tensor_reduce(out=pstk[:, 17:18], in_=pstk[:, 8:16], axis=AX.X, op=ALU.add,
                          after=[t_p2])
    tg3 = V.tensor_scalar(out=pstk[:, 17:18], in0=pstk[:, 17:18], scalar1=1.0,
                          scalar2=None, op0=ALU.max)
    tg4 = V.reciprocal(out=pstk[:, 17:18], in_=pstk[:, 17:18])
    tg5 = V.tensor_mul(out=pstk[:, 18:19], in0=pstk[:, 16:17], in1=pstk[:, 17:18])
    PE.matmul(ps_g.ap()[0:1, 0:MOE_E], pstk[:, 18:19], idf[:], start=True, stop=True,
              after=[tg5, t_if])
    tpr = PE.tok()
    tg6 = V.tensor_copy(out=wm_s[0:1, :], in_=ps_g.ap()[0:1, 0:MOE_E], after=[tpr])
    PE.matmul(ps_g.ap()[:, 0:MOE_E], onesf[0:1, 0:128], wm_s[0:1, :],
              start=True, stop=True, after=[tg6, tz])
    tpr2 = PE.tok()
    tg7 = V.tensor_copy(out=wm_s[:], in_=ps_g.ap()[:, 0:MOE_E], after=[tpr2])
    war_psg = tg7

    twg = e_load_weight_bf16(P, wgt, wstage,
                             lambda j: wgt_b[:, j * MOE_E:(j + 1) * MOE_E], 8, MOE_E)
    tw1 = e_load_weight_bf16(P, sgw, wstage,
                             lambda j: sgw_b[:, j * 1024:(j + 1) * 1024], 8, 1024)
    tw2 = e_load_weight_bf16(P, suw, wstage,
                             lambda j: suw_b[:, j * 1024:(j + 1) * 1024], 8, 1024)
    tw3 = e_load_weight_bf16(P, sdw, wstage,
                             lambda j: sdw_b[:, j * 1024:(j + 1) * 1024], 8, 1024)
    eps_ln = cst[:, K_LN_EPS:K_LN_EPS + 1]

    war_xt = None
    war_sq = None
    war_psu = []
    war_pso = None
    t_we_mm = None
    t_xlt_last = None

    for q in range(NQ):
        for it in range(QT):
            i = q * QT + it
            aft_x = ([war_xt] if war_xt else [])
            tx = SY.dma(xt[:], x_t[i], after=aft_x)
            tres = V.tensor_copy(out=xres[:, it * 1024:(it + 1) * 1024], in_=xt[:],
                                 after=[tx])
            aft = [war_sq] if war_sq else []
            e_ln(P, xt[:], st, C, eps_ln, negm[:, 0:1], rstd[:, 0:1], after=aft)
            war_sq = None
            txl = V.tensor_scalar(out=xlb[:], in0=xt[:], scalar1=negm[:, 0:1],
                                  scalar2=rstd[:, 0:1], op0=ALU.add, op1=ALU.mult)
            war_xt = txl
            aft_t = [txl, t_i] + ([t_xlt_last] if (it == 0 and t_xlt_last) else [])
            tt_tok = e_transpose_row(
                P, lambda j: xlb[:, j * 128:(j + 1) * 128],
                lambda j, it=it: xlt_q[:, it * 1024 + j * 128: it * 1024 + (j + 1) * 128],
                8, ps_t, idb, after=aft_t, key="pst")
            xlt_ap = xlt_q[:, it * 1024:(it + 1) * 1024]
            tm, tr, tpe_g = _emit_scores(P, V, SC, PE, xlt_ap, wgt_b, lb_s, g8, gm,
                                         s1, s2, tt_tok, war_psg, ps_g.ap())
            war_psg = tm
            tcf = V.tensor_mul(out=coefs[:, it * MOE_E:(it + 1) * MOE_E],
                               in0=gm[:], in1=wm_s[:], after=[tm, tg7])
            for br, wb in ((0, sgw_b), (1, suw_b)):
                for f in range(2):
                    for j in range(8):
                        aft = []
                        if br == 0 and f == 0 and j == 0:
                            aft = [tt_tok, tw1, tw2] + war_psu
                        PE.matmul(ps_u.ap()[:, br * 1024 + f * 512: br * 1024 + (f + 1) * 512],
                                  xlt_ap[:, j * 128:(j + 1) * 128],
                                  wb[:, j * 1024 + f * 512: j * 1024 + (f + 1) * 512],
                                  start=(j == 0), stop=(j == 7), after=aft)
            tpe_u = PE.tok()
            ts1 = SC.activation(out=siln[:], in_=ps_u.ap()[:, 0:1024], func=AF.Silu,
                                after=[tpe_u])
            tm1 = V.tensor_mul(out=mid[:], in0=ps_u.ap()[:, 1024:2048], in1=siln[:],
                               after=[ts1, tpe_u])
            war_psu = [ts1, tm1]
            ttm = e_transpose_row(P, lambda j: mid[:, j * 128:(j + 1) * 128],
                                  lambda j: midt[:, j * 128:(j + 1) * 128],
                                  8, ps_t, idb, after=[tm1], key="pst")
            for f in range(2):
                for j in range(8):
                    aft = ([ttm, tw3] + ([war_pso] if war_pso else [])) \
                        if (f == 0 and j == 0) else []
                    PE.matmul(ps_o.ap()[:, f * 512:(f + 1) * 512],
                              midt[:, j * 128:(j + 1) * 128],
                              sdw_b[:, j * 1024 + f * 512: j * 1024 + (f + 1) * 512],
                              start=(j == 0), stop=(j == 7), after=aft)
            tpe_o = PE.tok()
            war_pso = V.tensor_copy(out=acc[:, it * 1024:(it + 1) * 1024],
                                    in_=ps_o.ap()[:], after=[tpe_o])
        t_xlt_last = None

        we_hist = [None, None]
        for e in range(MOE_E):
            eb5 = (e % 2) * 4096
            eb10 = (e % 2) * 4096
            caft = [we_hist[e % 2]] if we_hist[e % 2] else []
            tg_ = e_load_weight_bf16(P, egw[e], wstage,
                                     lambda j, eb5=eb5: egw_b[:, eb5 + j * 512: eb5 + (j + 1) * 512],
                                     8, 512, conv_after=caft)
            tu_ = e_load_weight_bf16(P, euw[e], wstage,
                                     lambda j, eb5=eb5: euw_b[:, eb5 + j * 512: eb5 + (j + 1) * 512],
                                     8, 512, conv_after=caft)
            td_ = e_load_weight_bf16(P, edw[e], wstage,
                                     lambda j, eb10=eb10: edw_b[:, eb10 + j * 1024: eb10 + (j + 1) * 1024],
                                     4, 1024, conv_after=caft)
            for it in range(QT):
                xlt_ap = xlt_q[:, it * 1024:(it + 1) * 1024]
                for br, wb in ((0, egw_b), (1, euw_b)):
                    for j in range(8):
                        aft = []
                        if br == 0 and j == 0:
                            aft = [tg_, tu_] + war_psu
                        PE.matmul(ps_u.ap()[:, br * 512:(br + 1) * 512],
                                  xlt_ap[:, j * 128:(j + 1) * 128],
                                  wb[:, eb5 + j * 512: eb5 + (j + 1) * 512],
                                  start=(j == 0), stop=(j == 7), after=aft)
                tpe_u = PE.tok()
                ts1 = SC.activation(out=siln[:, 0:512], in_=ps_u.ap()[:, 0:512],
                                    func=AF.Silu, after=[tpe_u])
                tm1 = V.tensor_mul(out=mid[:, 0:512], in0=ps_u.ap()[:, 512:1024],
                                   in1=siln[:, 0:512], after=[ts1, tpe_u])
                war_psu = [ts1, tm1]
                ttm = e_transpose_row(P, lambda j: mid[:, j * 128:(j + 1) * 128],
                                      lambda j: midt[:, j * 128:(j + 1) * 128],
                                      4, ps_t, idb, after=[tm1], key="pst")
                for f in range(2):
                    for j in range(4):
                        aft = ([ttm, td_] + ([war_pso] if war_pso else [])) \
                            if (f == 0 and j == 0) else []
                        PE.matmul(ps_o.ap()[:, f * 512:(f + 1) * 512],
                                  midt[:, j * 128:(j + 1) * 128],
                                  edw_b[:, eb10 + j * 1024 + f * 512: eb10 + j * 1024 + (f + 1) * 512],
                                  start=(j == 0), stop=(j == 3), after=aft)
                t_we_mm = PE.tok()
                we_hist[e % 2] = t_we_mm
                war_pso = V.scalar_tensor_tensor(
                    out=acc[:, it * 1024:(it + 1) * 1024],
                    in0=ps_o.ap()[:], scalar=coefs[:, it * MOE_E + e: it * MOE_E + e + 1],
                    in1=acc[:, it * 1024:(it + 1) * 1024],
                    op0=ALU.mult, op1=ALU.add, after=[t_we_mm])
        t_xlt_last = t_we_mm

        for it in range(QT):
            i = q * QT + it
            a_ap = acc[:, it * 1024:(it + 1) * 1024]
            r_ap = xres[:, it * 1024:(it + 1) * 1024]
            aft_ln = ([war_sq] if war_sq else []) + ([war_xt] if war_xt else [])
            tln = e_ln(P, a_ap, st, C, eps_ln, negm[:, 0:1], rstd[:, 0:1],
                       after=aft_ln)
            tv = V.tensor_scalar(out=xt[:], in0=a_ap, scalar1=negm[:, 0:1],
                                 scalar2=rstd[:, 0:1], op0=ALU.add, op1=ALU.mult)
            tv = V.tensor_add(out=xt[:], in0=xt[:], in1=r_ap)
            tst = GP.dma(xo_t[i], xt[:], after=[tv])
            war_xt = tst
            war_sq = None

    nc = P.finish()
    return nc, P





MAGIC = 0x51CAFE77


def e_roles(P, magic_in):
    nc = P.nc
    g = nc.gpsimd
    P.magic_sb = P.sb("magic_sb", [1, 4], U32)
    P.flagrd = P.sb("flagrd", [1, 1], U32)
    t = P.SY.dma(P.magic_sb[:], magic_in)
    P.rh = g.alloc_register("rh")
    pid = nc.partition_id(engines=[g.engine])
    g.reg_alu(P.rh, pid, 2, ALU.mod)
    P._npoll = 0
    return t


def e_publish(P, src_ap, dst_h0, dst_h1, sh_flag, widx, after=()):
    g = P.nc.gpsimd
    GP = P.GP
    GP._waits(list(after))
    dsem = P.sems[GP.dname]
    base = P.cnt[GP.dname]
    with g.If_eq(P.rh, 0):
        g.dma_start(out=dst_h0, in_=src_ap).then_inc(dsem, 16)
        g.wait_ge(dsem, base + 16)
        g.dma_start(out=sh_flag[0:1, widx:widx + 1],
                    in_=P.magic_sb[0:1, 0:1]).then_inc(dsem, 16)
    with g.Else():
        g.dma_start(out=dst_h1, in_=src_ap).then_inc(dsem, 16)
        g.wait_ge(dsem, base + 16)
        g.dma_start(out=sh_flag[1:2, widx:widx + 1],
                    in_=P.magic_sb[0:1, 0:1]).then_inc(dsem, 16)
    g.end_ifs()
    P.cnt[GP.dname] += 32
    return (GP.dname, P.cnt[GP.dname])


def e_poll(P, sh_flag, widx):
    nc = P.nc
    g = nc.gpsimd
    P.GP._waits([])
    sem = P.sem(f"q_poll{P._npoll}")
    loop = f"poll{P._npoll}_loop"
    done = f"poll{P._npoll}_done"
    P._npoll += 1
    with g.register(f"exp{P._npoll}") as exp, g.register(f"rfv{P._npoll}") as rfv:
        g.reg_mov(exp, 0)
        g.br(loop)
        with nc.body(loop, valid_engines=[g.engine]):
            for _ in range(40):
                g.nop(cycle_cnt=250)
            with g.If_eq(P.rh, 0):
                g.dma_start(out=P.flagrd[0:1, 0:1],
                            in_=sh_flag[1:2, widx:widx + 1]).then_inc(sem, 16)
            with g.Else():
                g.dma_start(out=P.flagrd[0:1, 0:1],
                            in_=sh_flag[0:1, widx:widx + 1]).then_inc(sem, 16)
            g.end_ifs()
            g.reg_add(exp, exp, 16)
            g.wait_ge(sem, exp)
            g.reg_load(rfv, P.flagrd[0:1, 0:1])
            g.br_cmp(rfv, MAGIC, on_true=done, on_false=loop, comp_op="IS_EQ")
        nc.switch_body(done)
    return P.GP.engine_nop()


def e_ifelse_copy(P, dst_ap, src_h0, src_h1, after=()):
    g = P.nc.gpsimd
    GP = P.GP
    GP._waits(list(after))
    dsem = P.sems[GP.dname]
    with g.If_eq(P.rh, 0):
        g.dma_start(out=dst_ap, in_=src_h0).then_inc(dsem, 16)
    with g.Else():
        g.dma_start(out=dst_ap, in_=src_h1).then_inc(dsem, 16)
    g.end_ifs()
    P.cnt[GP.dname] += 16
    return (GP.dname, P.cnt[GP.dname])


def build_launchA():
    P = Prog()
    io = {}
    d = io
    d["xpre"] = P.din("xpre", [TSH, C])
    d["xown"] = P.din("xown", [TSH, C])
    d["gate"] = P.din("gate", [128, 1], F32)
    d["wq"] = P.din("wq", [C, C]); d["wk"] = P.din("wk", [C, C])
    d["wv"] = P.din("wv", [C, C]); d["wo"] = P.din("wo", [C, C])
    d["bq"] = P.din("bq", [1, C]); d["bk"] = P.din("bk", [1, C])
    d["bv"] = P.din("bv", [1, C]); d["bo"] = P.din("bo", [1, C])
    d["ones_row"] = P.din("ones_row", [1, 128], BF16)
    d["ones_col"] = P.din("ones_col", [128, 1], BF16)
    d["tri"] = P.din("tri", [128, 128], F32)
    d["ident"] = P.din("ident", [128, 128], BF16)
    d["consts"] = P.din("consts", [128, NCONST])
    d["magic"] = P.din("magic", [1, 4], U32)
    d["wsi"] = P.din("wsi", [C, 2048]); d["wso"] = P.din("wso", [C, C])
    d["w1"] = P.din("w1", [C, 2048]); d["w2"] = P.din("w2", [MOE_E, C, C])
    d["wg"] = P.din("wg", [C, MOE_E]); d["gb"] = P.din("gb", [128, MOE_E])
    for l in range(2):
        d[f"lnw{l}"] = P.din(f"lnw{l}", [128, 8])
        d[f"lnb{l}"] = P.din(f"lnb{l}", [128, 8])
        d[f"win{l}"] = P.din(f"win{l}", [C, 2048])
        d[f"cw{l}"] = P.din(f"cw{l}", [128, 8 * D_CONV])
        d[f"cb{l}"] = P.din(f"cb{l}", [128, 8])
        d[f"xpw{l}"] = P.din(f"xpw{l}", [C, 96])
        d[f"dtw{l}"] = P.din(f"dtw{l}", [DT_RANK, C])
        d[f"dtb{l}"] = P.din(f"dtb{l}", [128, 8])
        d[f"alog{l}"] = P.din(f"alog{l}", [128, 8 * D_ST])
        d[f"dsk{l}"] = P.din(f"dsk{l}", [128, 8])
        d[f"wout{l}"] = P.din(f"wout{l}", [D_INNER, C])
    d["selbc"] = P.din("selbc", [32, 2 * D_ST * 128], BF16)
    d["nfw"] = P.din("nfw", [128, C]); d["nfb"] = P.din("nfb", [128, C])
    d["wgt"] = P.din("wgt", [C, MOE_E]); d["lb"] = P.din("lb", [128, MOE_E])
    x3_out = P.dout("xout3", [TSH, C])
    part_out = P.dout("part", [16, 1])

    x1 = P.dtmp("x1s", [TSH, C])
    x2 = P.dtmp("x2s", [TSH, C])
    xin_a = P.dtmp("xina", [TSH, C])
    xin_b = P.dtmp("xinb", [TSH, C])
    xcs_l = P.dtmp("xcsl", [C, T], BF16)
    zs_l = P.dtmp("zsl", [C, T], BF16)
    dblp_l = P.dtmp("dblpl", [96, T])
    yg_mine = P.dtmp("ygmine", [C, T], BF16)
    yg_loc = P.dtmp("ygloc", [D_INNER, TSH], BF16)
    sh_xin = [P.dtmp(f"sh_xin{l}", [T, C], shared=True) for l in range(2)]
    sh_dbl = [P.dtmp(f"sh_dbl{l}", [2, 96, T], shared=True) for l in range(2)]
    sh_yg = [P.dtmp(f"sh_yg{l}", [D_INNER, T], BF16, shared=True) for l in range(2)]
    sh_flag = P.dtmp("sh_flag", [2, 64], U32, shared=True)

    P.begin()
    e_roles(P, d["magic"])

    with P.phase("attn"):
        emit_attn0(P, {**io, "xout": x1}, tsh=TSH, tpre=TSH)
    with P.phase("ffn0"):
        emit_ffn0(P, {**io, "x": x1, "xout": x2, "xin0": xin_a},
                  tsh=TSH, emit_xin0=True)

    xin_cur = xin_a
    xin_nxt = xin_b
    for l in range(2):
        tpub = e_publish(P, xin_cur, sh_xin[l][0:TSH, :], sh_xin[l][TSH:T, :],
                         sh_flag, 3 * l + 0)
        tok = e_poll(P, sh_flag, 3 * l + 0)
        with P.phase(f"ma{l}"):
            emit_mamba_a(P, {
                "xin": sh_xin[l], "lnw": d[f"lnw{l}"], "lnb": d[f"lnb{l}"],
                "win": d[f"win{l}"], "cw": d[f"cw{l}"], "cb": d[f"cb{l}"],
                "xpw": d[f"xpw{l}"], "ident": d["ident"], "consts": d["consts"],
                "xcs": xcs_l, "zs": zs_l, "dblp": dblp_l,
            }, tt=T, start_after=[tpub, tok])
        tpub = e_publish(P, dblp_l, sh_dbl[l][0], sh_dbl[l][1], sh_flag, 3 * l + 1)
        tok = e_poll(P, sh_flag, 3 * l + 1)
        with P.phase(f"mb{l}"):
            emit_mamba_b(P, {
                "xcs": xcs_l, "zs": zs_l,
                "dbl0": sh_dbl[l][0], "dbl1": sh_dbl[l][1],
                "dtw": d[f"dtw{l}"], "dtb": d[f"dtb{l}"],
                "alog": d[f"alog{l}"], "dsk": d[f"dsk{l}"],
                "selbc": d["selbc"], "ident": d["ident"], "consts": d["consts"],
                "yg": yg_mine,
            }, tt=T, lc=1024, start_after=[tpub, tok])
        tpub = e_publish(P, yg_mine, sh_yg[l][0:C, :], sh_yg[l][C:D_INNER, :],
                         sh_flag, 3 * l + 2)
        tok = e_poll(P, sh_flag, 3 * l + 2)
        tcp = e_ifelse_copy(P, yg_loc, sh_yg[l][:, 0:TSH], sh_yg[l][:, TSH:T],
                            after=[tpub, tok])
        with P.phase(f"mc{l}"):
            mio = {
                "yg": yg_loc, "xres": xin_cur, "wout": d[f"wout{l}"],
                "ident": d["ident"], "consts": d["consts"],
                "xout": (xin_nxt if l == 0 else x3_out),
            }
            if l == 1:
                mio["nfw"] = d["nfw"]; mio["nfb"] = d["nfb"]; mio["xout2"] = x2
            emit_mamba_c(P, mio, tsh=TSH, final=(l == 1), start_after=[tcp])
        xin_cur, xin_nxt = xin_nxt, xin_cur

    with P.phase("f1a"):
        emit_ffn1a(P, {**io, "x": x3_out, "part": part_out}, tsh=TSH)

    nc = P.finish()
    return nc, P



def _consts_pack():
    import ml_dtypes
    BF = ml_dtypes.bfloat16
    return {
        "consts": make_consts(),
        "ident": np.eye(128, dtype=BF),
        "ones_row": np.ones((1, 128), BF),
        "ones_col": np.ones((128, 1), BF),
        "tri": np.triu(np.ones((128, 128), np.float32)),
        "identf": np.eye(8, dtype=np.float32),
    }


def _col8(v):
    return np.ascontiguousarray(v.reshape(8, 128).T)


def _selbc():
    import ml_dtypes
    sel = np.zeros((32, 2 * D_ST * 128), np.float32)
    for n in range(D_ST):
        sel[n, n * 128:(n + 1) * 128] = 1.0
        sel[D_ST + n, (D_ST + n) * 128:(D_ST + n + 1) * 128] = 1.0
    return sel.astype(ml_dtypes.bfloat16)


_BUILD_CACHE = {}


def _get(name, fn):
    if name not in _BUILD_CACHE:
        _BUILD_CACHE[name] = fn()
    return _BUILD_CACHE[name]


def _launch(nc, in_maps):
    res = run_bass_kernel_spmd(nc, in_maps, core_ids=list(range(NCORES)))
    return res.results


def kernel_multi(**inp):
    CP = _consts_pack()
    x0 = np.ascontiguousarray(inp["L0_attn_x"])

    def shard(c):
        b, h = c // 2, c % 2
        return b, h

    wqkv = inp["L0_attn_w_qkv"]; bqkv = inp["L0_attn_b_qkv"]
    wq_t = np.ascontiguousarray(wqkv[0:C].T)
    wk_t = np.ascontiguousarray(wqkv[C:2 * C].T)
    wv_t = np.ascontiguousarray(wqkv[2 * C:3 * C].T)
    wo_t = np.ascontiguousarray(inp["L0_attn_w_out"].T)
    nc_at, _ = _get("attn0", lambda: build_attn0(tsh=TSH, tpre=TSH))
    maps = []
    for c in range(NCORES):
        b, h = shard(c)
        maps.append({
            "xpre": np.ascontiguousarray(x0[b, 0:TSH]),
            "xown": np.ascontiguousarray(x0[b, h * TSH:(h + 1) * TSH]),
            "gate": np.full((128, 1), float(h), np.float32),
            "wq": wq_t, "wk": wk_t, "wv": wv_t, "wo": wo_t,
            "bq": bqkv[None, 0:C].astype(np.float32),
            "bk": bqkv[None, C:2 * C].astype(np.float32),
            "bv": bqkv[None, 2 * C:3 * C].astype(np.float32),
            "bo": inp["L0_attn_b_out"][None, :].astype(np.float32),
            "ones_row": CP["ones_row"], "ones_col": CP["ones_col"],
            "tri": CP["tri"], "ident": CP["ident"], "consts": CP["consts"],
        })
    r = _launch(nc_at, maps)
    x1 = [r[c]["xout"] for c in range(NCORES)]

    nc_f0, _ = _get("ffn0", lambda: build_ffn0(tsh=TSH, emit_xin0=True))
    gbv = np.tile(inp["L0_ffn_gate_bias"][None], (128, 1)).astype(np.float32)
    maps = []
    for c in range(NCORES):
        maps.append({
            "x": x1[c],
            "wsi": inp["L0_ffn_w_shared_in"], "wso": inp["L0_ffn_w_shared_out"],
            "w1": inp["L0_ffn_w1_shared"], "w2": inp["L0_ffn_w2"],
            "wg": inp["L0_ffn_w_gate"], "gb": gbv,
            "ident": CP["ident"], "consts": CP["consts"],
        })
    r = _launch(nc_f0, maps)
    x2 = [r[c]["xout"] for c in range(NCORES)]
    xinner = [r[c]["xin0"] for c in range(NCORES)]

    in_w = inp["L1_attn_in_proj_w"]; conv_w = inp["L1_attn_conv_w"]
    conv_b = inp["L1_attn_conv_b"]; xp_w = inp["L1_attn_x_proj_w"]
    dtp_w = inp["L1_attn_dt_proj_w"]; dtp_b = inp["L1_attn_dt_proj_b"]
    A_log = inp["L1_attn_A_log"]; D_skip = inp["L1_attn_D_skip"]
    out_w = inp["L1_attn_out_proj_w"]
    ln_w = inp["L1_attn_ln_w"]; ln_b = inp["L1_attn_ln_b"]
    selbc = _selbc()

    nc_ma, _ = _get("mamba_a", lambda: build_mamba_a(tt=T))
    nc_mb, _ = _get("mamba_b", lambda: build_mamba_b(tt=T, lc=1024))
    nc_mc0, _ = _get("mamba_c0", lambda: build_mamba_c(tsh=TSH, final=False))
    nc_mc1, _ = _get("mamba_c1", lambda: build_mamba_c(tsh=TSH, final=True))

    for l in range(2):
        xin_full = [np.concatenate([xinner[2 * b], xinner[2 * b + 1]], axis=0)
                    for b in range(B)]
        maps = []
        for c in range(NCORES):
            b, h = shard(c)
            chs = slice(h * 1024, (h + 1) * 1024)
            win = np.concatenate([in_w[l][chs],
                                  in_w[l][D_INNER + h * 1024: D_INNER + (h + 1) * 1024]],
                                 axis=0).T
            maps.append({
                "xin": xin_full[b],
                "lnw": _col8(ln_w[l]), "lnb": _col8(ln_b[l]),
                "win": np.ascontiguousarray(win),
                "cw": np.ascontiguousarray(
                    conv_w[l][chs].reshape(8, 128, 4).transpose(1, 0, 2).reshape(128, 32)),
                "cb": _col8(conv_b[l][chs]),
                "xpw": np.ascontiguousarray(xp_w[l][:, chs].T),
                "ident": CP["ident"], "consts": CP["consts"],
            })
        ra = _launch(nc_ma, maps)
        maps = []
        for c in range(NCORES):
            b, h = shard(c)
            chs = slice(h * 1024, (h + 1) * 1024)
            maps.append({
                "xcs": ra[c]["xcs"], "zs": ra[c]["zs"],
                "dbl0": ra[2 * b]["dblp"], "dbl1": ra[2 * b + 1]["dblp"],
                "dtw": np.ascontiguousarray(dtp_w[l][chs].T),
                "dtb": _col8(dtp_b[l][chs]),
                "alog": np.ascontiguousarray(
                    A_log[l][chs].reshape(8, 128, 16).transpose(1, 0, 2).reshape(128, 128)),
                "dsk": _col8(D_skip[l][chs]),
                "selbc": selbc, "ident": CP["ident"], "consts": CP["consts"],
            })
        rb = _launch(nc_mb, maps)
        yg_full = [np.concatenate([rb[2 * b]["yg"], rb[2 * b + 1]["yg"]], axis=0)
                   for b in range(B)]
        maps = []
        for c in range(NCORES):
            b, h = shard(c)
            m = {
                "yg": np.ascontiguousarray(yg_full[b][:, h * TSH:(h + 1) * TSH]),
                "xres": xinner[c],
                "wout": np.ascontiguousarray(out_w[l].T),
                "ident": CP["ident"], "consts": CP["consts"],
            }
            if l == 1:
                m["nfw"] = np.tile(inp["L1_attn_norm_f_w"][None], (128, 1)).astype(np.float32)
                m["nfb"] = np.tile(inp["L1_attn_norm_f_b"][None], (128, 1)).astype(np.float32)
                m["xout2"] = x2[c]
            maps.append(m)
        rc = _launch(nc_mc1 if l == 1 else nc_mc0, maps)
        xinner = [rc[c]["xout"] for c in range(NCORES)]
    x3 = xinner

    wgt = np.ascontiguousarray(inp["L1_ffn_gate_w"].T)
    lbv = np.tile(inp["L1_ffn_lb_bias"][None], (128, 1)).astype(np.float32)
    nc_fa, _ = _get("ffn1a", lambda: build_ffn1a(tsh=TSH))
    maps = []
    for c in range(NCORES):
        maps.append({
            "x": x3[c], "wgt": wgt, "lb": lbv,
            "ones_col": CP["ones_col"], "ident": CP["ident"], "consts": CP["consts"],
        })
    r = _launch(nc_fa, maps)
    parts = np.stack([r[c]["part"][:, 0] for c in range(NCORES)])
    pw_t = np.ascontiguousarray(parts[:, 0:MOE_E].T)
    pc_t = np.ascontiguousarray(parts[:, MOE_E:2 * MOE_E].T)

    nc_fb, _ = _get("ffn1b", lambda: build_ffn1b(tsh=TSH))
    sgw_t = np.ascontiguousarray(inp["L1_ffn_shared_gate_w"].T)
    suw_t = np.ascontiguousarray(inp["L1_ffn_shared_up_w"].T)
    sdw_t = np.ascontiguousarray(inp["L1_ffn_shared_down_w"].T)
    egw_t = np.ascontiguousarray(inp["L1_ffn_expert_gate_w"].transpose(0, 2, 1))
    euw_t = np.ascontiguousarray(inp["L1_ffn_expert_up_w"].transpose(0, 2, 1))
    edw_t = np.ascontiguousarray(inp["L1_ffn_expert_down_w"].transpose(0, 2, 1))
    maps = []
    for c in range(NCORES):
        maps.append({
            "x": x3[c], "pw_t": pw_t, "pc_t": pc_t,
            "wgt": wgt, "lb": lbv,
            "sgw": sgw_t, "suw": suw_t, "sdw": sdw_t,
            "egw": egw_t, "euw": euw_t, "edw": edw_t,
            "ident": CP["ident"], "identf": CP["identf"], "consts": CP["consts"],
        })
    r = _launch(nc_fb, maps)
    out = np.empty((B, T, C), np.float32)
    for c in range(NCORES):
        b, h = shard(c)
        out[b, h * TSH:(h + 1) * TSH] = r[c]["xout"]
    return out


if __name__ == "__main__":
    pass


def _prep_common(inp):
    CP = _consts_pack()
    CP["magic"] = np.full((1, 4), MAGIC, np.uint32)
    wqkv = inp["L0_attn_w_qkv"]
    pre = {
        "wq": np.ascontiguousarray(wqkv[0:C].T),
        "wk": np.ascontiguousarray(wqkv[C:2 * C].T),
        "wv": np.ascontiguousarray(wqkv[2 * C:3 * C].T),
        "wo": np.ascontiguousarray(inp["L0_attn_w_out"].T),
        "bq": wqkv[None, 0, 0:0].copy(),
    }
    bqkv = inp["L0_attn_b_qkv"]
    pre["bq"] = bqkv[None, 0:C].astype(np.float32)
    pre["bk"] = bqkv[None, C:2 * C].astype(np.float32)
    pre["bv"] = bqkv[None, 2 * C:3 * C].astype(np.float32)
    pre["bo"] = inp["L0_attn_b_out"][None, :].astype(np.float32)
    pre["wsi"] = inp["L0_ffn_w_shared_in"]
    pre["wso"] = inp["L0_ffn_w_shared_out"]
    pre["w1"] = inp["L0_ffn_w1_shared"]
    pre["w2"] = inp["L0_ffn_w2"]
    pre["wg"] = inp["L0_ffn_w_gate"]
    pre["gb"] = np.tile(inp["L0_ffn_gate_bias"][None], (128, 1)).astype(np.float32)
    pre["selbc"] = _selbc()
    pre["nfw"] = np.tile(inp["L1_attn_norm_f_w"][None], (128, 1)).astype(np.float32)
    pre["nfb"] = np.tile(inp["L1_attn_norm_f_b"][None], (128, 1)).astype(np.float32)
    pre["wgt"] = np.ascontiguousarray(inp["L1_ffn_gate_w"].T)
    pre["lb"] = np.tile(inp["L1_ffn_lb_bias"][None], (128, 1)).astype(np.float32)
    return CP, pre


def kernel(**inp):
    CP, pre = _prep_common(inp)
    x0 = np.ascontiguousarray(inp["L0_attn_x"])
    in_w = inp["L1_attn_in_proj_w"]; conv_w = inp["L1_attn_conv_w"]
    conv_b = inp["L1_attn_conv_b"]; xp_w = inp["L1_attn_x_proj_w"]
    dtp_w = inp["L1_attn_dt_proj_w"]; dtp_b = inp["L1_attn_dt_proj_b"]
    A_log = inp["L1_attn_A_log"]; D_skip = inp["L1_attn_D_skip"]
    out_w = inp["L1_attn_out_proj_w"]
    ln_w = inp["L1_attn_ln_w"]; ln_b = inp["L1_attn_ln_b"]

    nc_a, _ = _get("launchA", build_launchA)
    maps = []
    for c in range(NCORES):
        b, h = c // 2, c % 2
        m = {
            "xpre": np.ascontiguousarray(x0[b, 0:TSH]),
            "xown": np.ascontiguousarray(x0[b, h * TSH:(h + 1) * TSH]),
            "gate": np.full((128, 1), float(h), np.float32),
            "ones_row": CP["ones_row"], "ones_col": CP["ones_col"],
            "tri": CP["tri"], "ident": CP["ident"], "consts": CP["consts"],
            "identf": CP["identf"], "magic": CP["magic"],
        }
        m.pop("identf")
        for k in ("wq", "wk", "wv", "wo", "bq", "bk", "bv", "bo", "wsi", "wso",
                  "w1", "w2", "wg", "gb", "selbc", "nfw", "nfb", "wgt", "lb"):
            m[k] = pre[k]
        for l in range(2):
            chs = slice(h * 1024, (h + 1) * 1024)
            win = np.concatenate(
                [in_w[l][chs], in_w[l][D_INNER + h * 1024: D_INNER + (h + 1) * 1024]],
                axis=0).T
            m[f"lnw{l}"] = _col8(ln_w[l])
            m[f"lnb{l}"] = _col8(ln_b[l])
            m[f"win{l}"] = np.ascontiguousarray(win)
            m[f"cw{l}"] = np.ascontiguousarray(
                conv_w[l][chs].reshape(8, 128, 4).transpose(1, 0, 2).reshape(128, 32))
            m[f"cb{l}"] = _col8(conv_b[l][chs])
            m[f"xpw{l}"] = np.ascontiguousarray(xp_w[l][:, chs].T)
            m[f"dtw{l}"] = np.ascontiguousarray(dtp_w[l][chs].T)
            m[f"dtb{l}"] = _col8(dtp_b[l][chs])
            m[f"alog{l}"] = np.ascontiguousarray(
                A_log[l][chs].reshape(8, 128, 16).transpose(1, 0, 2).reshape(128, 128))
            m[f"dsk{l}"] = _col8(D_skip[l][chs])
            m[f"wout{l}"] = np.ascontiguousarray(out_w[l].T)
        maps.append(m)
    ra = _launch(nc_a, maps)
    x3 = [ra[c]["xout3"] for c in range(NCORES)]
    parts = np.stack([ra[c]["part"][:, 0] for c in range(NCORES)])
    pw_t = np.ascontiguousarray(parts[:, 0:MOE_E].T)
    pc_t = np.ascontiguousarray(parts[:, MOE_E:2 * MOE_E].T)

    nc_fb, _ = _get("ffn1b", lambda: build_ffn1b(tsh=TSH))
    sgw_t = np.ascontiguousarray(inp["L1_ffn_shared_gate_w"].T)
    suw_t = np.ascontiguousarray(inp["L1_ffn_shared_up_w"].T)
    sdw_t = np.ascontiguousarray(inp["L1_ffn_shared_down_w"].T)
    egw_t = np.ascontiguousarray(inp["L1_ffn_expert_gate_w"].transpose(0, 2, 1))
    euw_t = np.ascontiguousarray(inp["L1_ffn_expert_up_w"].transpose(0, 2, 1))
    edw_t = np.ascontiguousarray(inp["L1_ffn_expert_down_w"].transpose(0, 2, 1))
    maps = []
    for c in range(NCORES):
        maps.append({
            "x": x3[c], "pw_t": pw_t, "pc_t": pc_t,
            "wgt": pre["wgt"], "lb": pre["lb"],
            "sgw": sgw_t, "suw": suw_t, "sdw": sdw_t,
            "egw": egw_t, "euw": euw_t, "edw": edw_t,
            "ident": CP["ident"], "identf": CP["identf"], "consts": CP["consts"],
        })
    r = _launch(nc_fb, maps)
    out = np.empty((B, T, C), np.float32)
    for c in range(NCORES):
        b, h = c // 2, c % 2
        out[b, h * TSH:(h + 1) * TSH] = r[c]["xout"]
    return out


# version 20
# speedup vs baseline: 1.0072x; 1.0072x over previous
import numpy as np
from contextlib import ExitStack
import concourse.bass as bass
import concourse.mybir as mybir
from concourse.bass_utils import run_bass_kernel_spmd

F32 = mybir.dt.float32
BF16 = mybir.dt.bfloat16
U32 = mybir.dt.uint32
AF = mybir.ActivationFunctionType
ALU = mybir.AluOpType
AX = mybir.AxisListType

B, T, C = 4, 8192, 1024
NCORES = 8
TSH = T // 2
LN_EPS = 1e-5
ATT_EPS = 1e-6
HEADS, DH = 16, 64
D_INNER, D_ST, DT_RANK, D_CONV = 2048, 16, 64, 4
MOE_E = 8

K_LN_EPS, K_ATT_EPS, K_ONE, K_NEG1 = 0, 1, 2, 3
NCONST = 4


def make_consts():
    v = np.zeros((1, NCONST), np.float32)
    v[0, K_LN_EPS] = LN_EPS
    v[0, K_ATT_EPS] = ATT_EPS
    v[0, K_ONE] = 1.0
    v[0, K_NEG1] = -1.0
    return np.repeat(v, 128, axis=0)



class EQ:

    def __init__(self, prog, q, name, dma_inc=16):
        self.p = prog
        self.q = q
        self.name = name
        self.dname = name + "_d"
        self.sem = prog.sem(name)
        self.dma_inc = dma_inc
        self.selfsync = True
        self.lazy_inc = False
        self._pending = None

    def _waits(self, after):
        p = self.p
        if self.selfsync:
            if p.cnt[self.name] > 0:
                self.q.wait_ge(self.sem, p.cnt[self.name])
            if p.cnt.get(self.dname, 0) > 0:
                self.q.wait_ge(p.sems[self.dname], p.cnt[self.dname])
        if after:
            for (sname, val) in after:
                if sname == self.name:
                    continue
                if val > 0:
                    self.q.wait_ge(p.sems[sname], val)

    def tok(self):
        if self._pending is not None:
            self._pending.then_inc(self.sem, 1)
            self.p.cnt[self.name] += 1
            self._pending = None
        return (self.name, self.p.cnt[self.name])

    def _emit(self, fname, a, kw, inc, sem=None, name=None):
        after = kw.pop("after", None)
        self._waits(after)
        ins = getattr(self.q, fname)(*a, **kw)
        if self.lazy_inc and sem is None:
            self._pending = ins
            return (self.name, self.p.cnt[self.name])
        ins.then_inc(sem or self.sem, inc)
        self.p.cnt[name or self.name] += inc
        return (name or self.name, self.p.cnt[name or self.name])

    def dma(self, out, in_, **kw):
        dsem = self.p.sem(self.dname)
        return self._emit("dma_start", (), dict(out=out, in_=in_, **kw),
                          self.dma_inc, sem=dsem, name=self.dname)

    def __getattr__(self, fname):
        def call(*a, **kw):
            return self._emit(fname, a, kw, 1)
        return call


class Prog:
    def __init__(self):
        self.nc = bass.Bass("TRN2", target_bir_lowering=False)
        self.stack = ExitStack()
        self.cnt = {}
        self.sems = {}
        self.inputs = {}
        self.outputs = {}
        self._phase = None
        self._phase_stack = None
        self._nphase = 0

    def phase(self, name):
        from contextlib import contextmanager

        @contextmanager
        def _ph():
            self._phase = f"{self._nphase}_{name}"
            self._nphase += 1
            self._phase_stack = ExitStack()
            try:
                yield self
            finally:
                self.barrier()
                self._phase_stack.close()
                self._phase_stack = None
                self._phase = None
        return _ph()

    def barrier(self):
        for eq in (self.PE, self.V, self.SC, self.GP, self.SY):
            if getattr(eq, "_pending", None) is not None:
                eq.tok()
        nc = self.nc
        engines = [nc.sync, nc.gpsimd, nc.vector, nc.scalar, nc.tensor]
        for q in engines:
            for name, cval in self.cnt.items():
                if cval > 0:
                    q.wait_ge(self.sems[name], cval)

    def din(self, name, shape, dtype=F32):
        ap = self.nc.dram_tensor(name, list(shape), dtype, kind="ExternalInput").ap()
        self.inputs[name] = (tuple(shape), dtype)
        return ap

    def dout(self, name, shape, dtype=F32):
        ap = self.nc.dram_tensor(name, list(shape), dtype, kind="ExternalOutput").ap()
        self.outputs[name] = (tuple(shape), dtype)
        return ap

    def dtmp(self, name, shape, dtype=F32, shared=False):
        return self.nc.dram_tensor(name, list(shape), dtype,
                                   addr_space=("Shared" if shared else "Local")).ap()

    def sb(self, name, shape, dtype):
        if self._phase is not None:
            name = f"{self._phase}_{name}"
            return self._phase_stack.enter_context(
                self.nc.sbuf_tensor(name, list(shape), dtype))
        return self.stack.enter_context(self.nc.sbuf_tensor(name, list(shape), dtype))

    def psum(self, name, shape, dtype=F32):
        if self._phase is not None:
            name = f"{self._phase}_{name}"
            return self._phase_stack.enter_context(
                self.nc.psum_tensor(name, list(shape), dtype))
        return self.stack.enter_context(self.nc.psum_tensor(name, list(shape), dtype))

    def sem(self, name):
        if name not in self.sems:
            self.sems[name] = self.stack.enter_context(self.nc.semaphore(name))
            self.cnt[name] = 0
        return self.sems[name]

    def begin(self):
        self.block = self.stack.enter_context(self.nc.Block())
        nc = self.nc
        self.SY = EQ(self, nc.sync, "q_sy")
        self.GP = EQ(self, nc.gpsimd, "q_gp")
        self.V = EQ(self, nc.vector, "q_v")
        self.SC = EQ(self, nc.scalar, "q_sc")
        self.PE = EQ(self, nc.tensor, "q_pe")
        self.PE.selfsync = True
        self.PE.lazy_inc = True

        return self

    def finish(self):
        for eq in (self.PE, self.V, self.SC, self.GP, self.SY):
            if getattr(eq, "_pending", None) is not None:
                eq.tok()
        nc = self.nc
        for name in list(self.cnt):
            if self.cnt[name] > 0:
                nc.gpsimd.wait_ge(self.sems[name], self.cnt[name])
        self.stack.close()
        return self.nc



def e_ln(P, x_ap, st, n_feat, eps_col, out_negm, out_rstd, after=()):
    V, SC = P.V, P.SC
    t = V.scalar_tensor_tensor(out=st["sq"][:, 0:n_feat], in0=x_ap, scalar=0.0,
                               in1=x_ap, op0=ALU.add, op1=ALU.mult,
                               accum_out=st["s2"][:, 0:1], after=list(after))
    V.tensor_reduce(out=st["s1"][:, 0:1], in_=x_ap, axis=AX.X, op=ALU.add)
    V.tensor_scalar(out=out_negm, in0=st["s1"][:, 0:1], scalar1=-1.0 / n_feat,
                    scalar2=None, op0=ALU.mult)
    V.tensor_mul(out=st["s1"][:, 0:1], in0=out_negm, in1=out_negm)
    tv = V.scalar_tensor_tensor(out=out_rstd, in0=st["s2"][:, 0:1], scalar=1.0 / n_feat,
                                in1=st["s1"][:, 0:1], op0=ALU.mult, op1=ALU.subtract)
    ts = SC.activation(out=out_rstd, in_=out_rstd, func=AF.Sqrt, bias=eps_col,
                       after=[tv])
    tr = V.reciprocal(out=out_rstd, in_=out_rstd, after=[ts])
    return tr


def e_transpose_row(P, src_bf16_ap_fn, dst_bf16_ap_fn, nchunks, ps_t, ident,
                    after=(), evac_q=None, key=None):
    V = evac_q or P.V
    PE = P.PE
    if not hasattr(P, "_pst_last"):
        P._pst_last = {}
    if key is None:
        key = id(ps_t)
    ps_full = ps_t if isinstance(ps_t, bass.AP) else ps_t[:]
    nslot = 2 if ps_full.shape[1] >= 256 else 1
    slots = [ps_full[:, 128 * s:128 * (s + 1)] for s in range(nslot)]
    tprev = P._pst_last.get(key)
    if not isinstance(tprev, list):
        tprev = [tprev] * nslot
    tok = None
    for j in range(nchunks):
        s = j % nslot
        aft = list(after) if j == 0 else []
        if tprev[s] is not None:
            aft.append(tprev[s])
        PE.transpose(slots[s], src_bf16_ap_fn(j), ident[:], after=aft)
        tok_pe = PE.tok()
        tok = V.tensor_copy(out=dst_bf16_ap_fn(j), in_=slots[s], after=[tok_pe])
        tprev[s] = tok
    P._pst_last[key] = tprev
    return tok


def e_load_weight_bf16(P, dram_ap, sb_stage, sb_dst_fn, nchunks, fwidth,
                       conv_q=None, conv_after=(), pre_sliced=False):
    SY = P.SY
    CV = conv_q or P.SC
    if not hasattr(P, "_stage_last"):
        P._stage_last = {}
    key = id(sb_stage)
    toks = P._stage_last.get(key, [None, None])
    prev_both = [t for t in toks if t is not None]
    srcv = dram_ap if pre_sliced else dram_ap.rearrange("(j p) f -> j p f", p=128)
    tok = None
    for j in range(nchunks):
        half = j % 2
        if j < 2:
            aft = list(prev_both)
        else:
            aft = []
            if toks[half] is not None:
                aft.append(toks[half])
        tl = SY.dma(sb_stage[:, half * fwidth:(half + 1) * fwidth], srcv[j], after=aft)
        caft = [tl] + (list(conv_after) if j == 0 else [])
        tok = CV.copy(out=sb_dst_fn(j), in_=sb_stage[:, half * fwidth:(half + 1) * fwidth],
                      after=caft)
        toks[half] = tok
    P._stage_last[key] = toks
    return tok



def emit_ffn0(P, io, tsh=TSH, emit_xin0=False):
    x_in = io["x"]
    wsi = io["wsi"]
    wso = io["wso"]
    w1 = io["w1"]
    w2 = io["w2"]
    wg = io["wg"]
    gb = io["gb"]
    ident_in = io["ident"]
    consts = io["consts"]
    x_out = io["xout"]
    xin0_out = io.get("xin0")

    ntt = tsh // 128
    QT = min(4, ntt)
    NQ = ntt // QT

    x_t = x_in.rearrange("(n p) c -> n p c", p=128)
    xo_t = x_out.rearrange("(n p) c -> n p c", p=128)
    if emit_xin0:
        xi_t = xin0_out.rearrange("(n p) c -> n p c", p=128)

    cst = P.sb("cst", [128, NCONST], F32)
    idb = P.sb("idb", [128, 128], BF16)
    wstage = P.sb("wstage", [128, 2 * 2048], F32)
    wsi_b = P.sb("wsi_b", [128, 8 * 2048], BF16)
    w1_b = P.sb("w1_b", [128, 8 * 2048], BF16)
    wso_b = P.sb("wso_b", [128, 8 * 1024], BF16)
    wg_b = P.sb("wg_b", [128, 8 * MOE_E], BF16)
    gb_s = P.sb("gb_s", [128, MOE_E], F32)
    w2_b = P.sb("w2_b", [128, 2 * 8 * 1024], BF16)

    hidt = P.sb("hidt", [128, QT * 1024], BF16)
    acc = P.sb("acc", [128, QT * 1024], F32)
    xres = P.sb("xres", [128, QT * 1024], F32)
    wdns = P.sb("wdns", [128, QT * MOE_E], F32)

    xt = P.sb("xt", [128, C], F32)
    sq = P.sb("sq", [128, 2048], F32)
    s1 = P.sb("s1", [128, 1], F32)
    s2 = P.sb("s2", [128, 1], F32)
    negm = P.sb("negm", [128, 1], F32)
    rstd = P.sb("rstd", [128, 1], F32)
    xlb = P.sb("xlb", [128, C], BF16)
    xlt = P.sb("xlt", [128, C], BF16)
    siln = P.sb("siln", [128, 1024], BF16)
    mid = P.sb("mid", [128, 1024], BF16)
    midt = P.sb("midt", [128, 1024], BF16)
    g8 = P.sb("g8", [128, MOE_E], F32)
    gm = P.sb("gm", [128, MOE_E], F32)
    gcol = P.sb("gcol", [128, 1], F32)

    ps_u = P.psum("ps_u", [128, 2048])
    ps_o = P.psum("ps_o", [128, 1024])
    ps_t = P.psum("ps_t", [128, 128], BF16)
    ps_g = P.psum("ps_g", [128, MOE_E])

    st = {"sq": sq, "s1": s1, "s2": s2}

    SY, GP, V, SC, PE = P.SY, P.GP, P.V, P.SC, P.PE

    t_c = SY.dma(cst[:], consts)
    t_i = SY.dma(idb[:], ident_in)
    t_g = SY.dma(gb_s[:], gb)
    SC.copy(out=s1[:, 0:1], in_=cst[:, 0:1], after=[t_c, t_g])
    tw1 = e_load_weight_bf16(P, wsi, wstage,
                             lambda j: wsi_b[:, j * 2048:(j + 1) * 2048], 8, 2048)
    tw2 = e_load_weight_bf16(P, w1, wstage,
                             lambda j: w1_b[:, j * 2048:(j + 1) * 2048], 8, 2048)
    tw3 = e_load_weight_bf16(P, wso, wstage,
                             lambda j: wso_b[:, j * 1024:(j + 1) * 1024], 8, 1024)
    tw4 = e_load_weight_bf16(P, wg, wstage,
                             lambda j: wg_b[:, j * MOE_E:(j + 1) * MOE_E], 8, MOE_E)
    eps_ln = cst[:, K_LN_EPS:K_LN_EPS + 1]

    war_xt = None
    war_sq = None
    war_psg = None
    war_psu = []
    war_pso = None
    t_hid_last = None
    t_we_mm = None
    we_hist = [None, None]

    for q in range(NQ):
        for it in range(QT):
            i = q * QT + it
            aft_x = [tw4]
            if war_xt is not None:
                aft_x.append(war_xt)
            tx = SY.dma(xt[:], x_t[i], after=aft_x)
            tres = V.tensor_copy(out=xres[:, it * 1024:(it + 1) * 1024], in_=xt[:],
                                 after=[tx])
            aft_ln = [tx] + ([war_sq] if war_sq else [])
            tln = e_ln(P, xt[:], st, C, eps_ln, negm[:, 0:1], rstd[:, 0:1],
                       after=aft_ln)
            txl = V.tensor_scalar(out=xlb[:], in0=xt[:], scalar1=negm[:, 0:1],
                                  scalar2=rstd[:, 0:1], op0=ALU.add, op1=ALU.mult)
            war_xt = txl
            tt = e_transpose_row(P, lambda j: xlb[:, j * 128:(j + 1) * 128],
                                 lambda j: xlt[:, j * 128:(j + 1) * 128],
                                 8, ps_t, idb, after=[txl, t_i])
            for j in range(8):
                aft = [tt] + ([war_psg] if war_psg else []) if j == 0 else []
                PE.matmul(ps_g[:], xlt[:, j * 128:(j + 1) * 128],
                          wg_b[:, j * MOE_E:(j + 1) * MOE_E],
                          start=(j == 0), stop=(j == 7), after=aft)
            tpe_g = PE.tok()
            V.tensor_add(out=g8[:], in0=ps_g[:], in1=gb_s[:], after=[tpe_g])
            V.reduce_max(out=s1[:, 0:1], in_=g8[:], axis=AX.X)
            tneg = V.tensor_scalar(out=s1[:, 0:1], in0=s1[:, 0:1], scalar1=-1.0,
                                   scalar2=None, op0=ALU.mult)
            tex = SC.activation(out=g8[:], in_=g8[:], func=AF.Exp, bias=s1[:, 0:1],
                                after=[tneg])
            V.tensor_reduce(out=s2[:, 0:1], in_=g8[:], axis=AX.X, op=ALU.add,
                            after=[tex])
            V.reciprocal(out=s2[:, 0:1], in_=s2[:, 0:1])
            V.tensor_scalar(out=g8[:], in0=g8[:], scalar1=s2[:, 0:1], scalar2=None,
                            op0=ALU.mult)
            V.reduce_max(out=s1[:, 0:1], in_=g8[:], axis=AX.X)
            V.tensor_scalar(out=gm[:], in0=g8[:], scalar1=s1[:, 0:1], scalar2=None,
                            op0=ALU.is_ge)
            V.tensor_scalar(out=gm[:], in0=gm[:], scalar1=-1.0, scalar2=1.0,
                            op0=ALU.mult, op1=ALU.add)
            V.tensor_mul(out=gm[:], in0=g8[:], in1=gm[:])
            V.reduce_max(out=s2[:, 0:1], in_=gm[:], axis=AX.X)
            V.tensor_scalar(out=gm[:], in0=g8[:], scalar1=s2[:, 0:1], scalar2=None,
                            op0=ALU.is_ge)
            war_psg = V.tensor_mul(out=wdns[:, it * MOE_E:(it + 1) * MOE_E],
                                   in0=g8[:], in1=gm[:])

            for f in range(4):
                for j in range(8):
                    aft = ([tw1, tt] + war_psu) if (f == 0 and j == 0) else []
                    PE.matmul(ps_u[:, f * 512:(f + 1) * 512],
                              xlt[:, j * 128:(j + 1) * 128],
                              wsi_b[:, j * 2048 + f * 512: j * 2048 + (f + 1) * 512],
                              start=(j == 0), stop=(j == 7), after=aft)
            tpe_u = PE.tok()
            ts1 = SC.activation(out=siln[:], in_=ps_u[:, 0:1024], func=AF.Silu,
                                after=[tpe_u])
            tm1 = V.tensor_mul(out=mid[:], in0=ps_u[:, 1024:2048], in1=siln[:],
                               after=[ts1, tpe_u])
            war_psu = [ts1, tm1]
            ttm = e_transpose_row(P, lambda j: mid[:, j * 128:(j + 1) * 128],
                                  lambda j: midt[:, j * 128:(j + 1) * 128],
                                  8, ps_t, idb, after=[tm1])
            for f in range(2):
                for j in range(8):
                    aft = ([ttm, tw3] + ([war_pso] if war_pso else [])) \
                        if (f == 0 and j == 0) else []
                    PE.matmul(ps_o[:, f * 512:(f + 1) * 512],
                              midt[:, j * 128:(j + 1) * 128],
                              wso_b[:, j * 1024 + f * 512: j * 1024 + (f + 1) * 512],
                              start=(j == 0), stop=(j == 7), after=aft)
            tpe_o = PE.tok()
            war_pso = V.tensor_copy(out=acc[:, it * 1024:(it + 1) * 1024], in_=ps_o[:],
                                    after=[tpe_o])

            for f in range(4):
                for j in range(8):
                    aft = ([tw2] + war_psu) if (f == 0 and j == 0) else []
                    PE.matmul(ps_u[:, f * 512:(f + 1) * 512],
                              xlt[:, j * 128:(j + 1) * 128],
                              w1_b[:, j * 2048 + f * 512: j * 2048 + (f + 1) * 512],
                              start=(j == 0), stop=(j == 7), after=aft)
            tpe_h = PE.tok()
            ts2 = SC.activation(out=siln[:], in_=ps_u[:, 0:1024], func=AF.Silu,
                                after=[tpe_h])
            tm2 = V.tensor_mul(out=mid[:], in0=ps_u[:, 1024:2048], in1=siln[:],
                               after=[ts2, tpe_h])
            war_psu = [ts2, tm2]
            t_hid_last = e_transpose_row(
                P, lambda j: mid[:, j * 128:(j + 1) * 128],
                lambda j: hidt[:, it * 1024 + j * 128: it * 1024 + (j + 1) * 128],
                8, ps_t, idb, after=[tm2])

        war_po = [([war_pso] if war_pso else []), list(war_psu)]
        kk = 0

        def po_ap(b, c0, c1):
            return ps_o[:, c0:c1] if b == 0 else ps_u[:, c0:c1]

        for e in range(MOE_E):
            eb = (e % 2) * 8192
            caft = [we_hist[e % 2]] if we_hist[e % 2] else []
            twe = e_load_weight_bf16(P, w2[e], wstage,
                                     lambda j, eb=eb: w2_b[:, eb + j * 1024: eb + (j + 1) * 1024],
                                     8, 1024, conv_after=caft)
            for it in range(QT):
                b = kk % 2
                kk += 1
                for f in range(2):
                    for j in range(8):
                        aft = ([twe, t_hid_last] + war_po[b]) \
                            if (f == 0 and j == 0) else []
                        PE.matmul(po_ap(b, f * 512, (f + 1) * 512),
                                  hidt[:, it * 1024 + j * 128: it * 1024 + (j + 1) * 128],
                                  w2_b[:, eb + j * 1024 + f * 512: eb + j * 1024 + (f + 1) * 512],
                                  start=(j == 0), stop=(j == 7), after=aft)
                t_we_mm = PE.tok()
                we_hist[e % 2] = t_we_mm
                tst_ = V.scalar_tensor_tensor(
                    out=acc[:, it * 1024:(it + 1) * 1024],
                    in0=po_ap(b, 0, 1024),
                    scalar=wdns[:, it * MOE_E + e: it * MOE_E + e + 1],
                    in1=acc[:, it * 1024:(it + 1) * 1024],
                    op0=ALU.mult, op1=ALU.add, after=[t_we_mm])
                war_po[b] = [tst_]
        war_pso = war_po[0][0] if war_po[0] else None
        war_psu = war_po[1]

        for it in range(QT):
            i = q * QT + it
            a_ap = acc[:, it * 1024:(it + 1) * 1024]
            r_ap = xres[:, it * 1024:(it + 1) * 1024]
            aft_ln = [war_sq] if war_sq else []
            if war_xt is not None:
                aft_ln.append(war_xt)
            tln = e_ln(P, a_ap, st, C, eps_ln, negm[:, 0:1], rstd[:, 0:1],
                       after=aft_ln)
            tv = V.tensor_scalar(out=xt[:], in0=a_ap, scalar1=negm[:, 0:1],
                                 scalar2=rstd[:, 0:1], op0=ALU.add, op1=ALU.mult)
            tv = V.tensor_add(out=xt[:], in0=xt[:], in1=r_ap)
            tst = GP.dma(xo_t[i], xt[:], after=[tv])
            war_xt = tst
            war_sq = None
            if emit_xin0:
                tln2 = e_ln(P, xt[:], st, C, eps_ln, negm[:, 0:1], rstd[:, 0:1])
                tv2 = V.tensor_scalar(out=sq[:, 0:1024], in0=xt[:],
                                      scalar1=negm[:, 0:1], scalar2=rstd[:, 0:1],
                                      op0=ALU.add, op1=ALU.mult)
                war_sq = GP.dma(xi_t[i], sq[:, 0:1024], after=[tv2])


def build_ffn0(tsh=TSH, emit_xin0=False):
    P = Prog()
    io = {
        "x": P.din("x", [tsh, C]),
        "wsi": P.din("wsi", [C, 2048]),
        "wso": P.din("wso", [C, C]),
        "w1": P.din("w1", [C, 2048]),
        "w2": P.din("w2", [MOE_E, C, C]),
        "wg": P.din("wg", [C, MOE_E]),
        "gb": P.din("gb", [128, MOE_E]),
        "ident": P.din("ident", [128, 128], BF16),
        "consts": P.din("consts", [128, NCONST]),
        "xout": P.dout("xout", [tsh, C]),
    }
    if emit_xin0:
        io["xin0"] = P.dout("xin0", [tsh, C])
    P.begin()
    emit_ffn0(P, io, tsh=tsh, emit_xin0=emit_xin0)
    nc = P.finish()
    return nc, P




def emit_attn0(P, io, tsh=TSH, tpre=TSH):
    xpre = io["xpre"]
    xown = io["xown"]
    gate = io["gate"]
    wq = io["wq"]
    wk = io["wk"]
    wv = io["wv"]
    wo = io["wo"]
    bq_in = io["bq"]
    bk_in = io["bk"]
    bv_in = io["bv"]
    bo_in = io["bo"]
    ones_row = io["ones_row"]
    ones_col = io["ones_col"]
    tri = io["tri"]
    ident_in = io["ident"]
    consts = io["consts"]
    x_out = io["xout"]

    npre = tpre // 128
    ntt = tsh // 128
    xp_t = xpre.rearrange("(n p) c -> n p c", p=128)
    xo_in_t = xown.rearrange("(n p) c -> n p c", p=128)
    xo_t = x_out.rearrange("(n p) c -> n p c", p=128)

    cst = P.sb("cst", [128, NCONST], F32)
    idb = P.sb("idb", [128, 128], BF16)
    orow = P.sb("orow", [1, 128], BF16)
    ocol = P.sb("ocol", [128, 1], BF16)
    gcol = P.sb("gcol", [128, 1], F32)
    gcol_b = P.sb("gcol_b", [128, 1], BF16)
    trim = P.sb("trim", [128, 128], F32)
    wstage = P.sb("wstage", [128, 2 * 2048], F32)
    wq_b = P.sb("wq_b", [128, 8 * 1024], BF16)
    wk_b = P.sb("wk_b", [128, 8 * 1024], BF16)
    wv_b = P.sb("wv_b", [128, 8 * 1024], BF16)
    wo_b = P.sb("wo_b", [128, 8 * 1024], BF16)
    brows = P.sb("brows", [1, 4 * C], F32)
    bq_r = P.sb("bq_r", [1, C], BF16)
    bk_r = P.sb("bk_r", [1, C], BF16)
    bv_r = P.sb("bv_r", [1, C], BF16)
    bo_r = P.sb("bo_r", [1, C], BF16)

    S_sb = P.sb("S_sb", [128, 8 * 64], F32)
    S_bf = P.sb("S_bf", [128, 8 * 64], BF16)
    ks_sb = P.sb("ks_sb", [128, 8], F32)
    ks_bf = P.sb("ks_bf", [128, 8], BF16)

    xt = P.sb("xt", [128, C], F32)
    sq = P.sb("sq", [128, C], F32)
    s1 = P.sb("s1", [128, 1], F32)
    s2 = P.sb("s2", [128, 1], F32)
    negm = P.sb("negm", [128, 1], F32)
    rstd = P.sb("rstd", [128, 1], F32)
    xlb = P.sb("xlb", [128, C], BF16)
    xlt = P.sb("xlt", [128, C], BF16)
    eluw = P.sb("eluw", [128, C], F32)
    relw = P.sb("relw", [128, C], F32)
    qp = P.sb("qp", [128, C], BF16)
    kp = P.sb("kp", [128, C], BF16)
    vtok = P.sb("vtok", [128, C], BF16)
    scm = P.sb("scm", [128, 128], BF16)
    ktr = P.sb("ktr", [128, 128], BF16)
    ctxs = P.sb("ctxs", [128, 128], BF16)
    rnrow = P.sb("rnrow", [128, 128], F32)
    onesf = P.sb("onesf", [128, 64], F32)
    afm = P.sb("afm", [128, 8 * 128], BF16)
    oacc = P.sb("oacc", [128, C], F32)

    PS_A = P.psum("PS_A", [128, 1024])
    PS_B = P.psum("PS_B", [128, 512])
    PS_K = P.psum("PS_K", [128, 1024])
    PS_V = P.psum("PS_V", [128, 1024])
    PS_D = P.psum("PS_D", [128, 256], BF16)

    st = {"sq": sq, "s1": s1, "s2": s2}

    SY, GP, V, SC, PE = P.SY, P.GP, P.V, P.SC, P.PE

    t_c = SY.dma(cst[:], consts)
    t_i = SY.dma(idb[:], ident_in)
    t_or = SY.dma(orow[:], ones_row)
    t_oc = SY.dma(ocol[:], ones_col)
    t_gt = SY.dma(gcol[:], gate)
    t_tr = SY.dma(trim[:], tri)
    SC.copy(out=s1[:, 0:1], in_=cst[:, 0:1], after=[t_c, t_gt, t_tr, t_or, t_oc])
    t_gtb = SC.copy(out=gcol_b[:], in_=gcol[:])
    twq = e_load_weight_bf16(P, wq, wstage, lambda j: wq_b[:, j*1024:(j+1)*1024], 8, 1024)
    twk = e_load_weight_bf16(P, wk, wstage, lambda j: wk_b[:, j*1024:(j+1)*1024], 8, 1024)
    twv = e_load_weight_bf16(P, wv, wstage, lambda j: wv_b[:, j*1024:(j+1)*1024], 8, 1024)
    two = e_load_weight_bf16(P, wo, wstage, lambda j: wo_b[:, j*1024:(j+1)*1024], 8, 1024)
    tba = SY.dma(brows[0:1, 0:C], bq_in)
    tbb = SY.dma(brows[0:1, C:2 * C], bk_in)
    tbc = SY.dma(brows[0:1, 2 * C:3 * C], bv_in)
    tbd = SY.dma(brows[0:1, 3 * C:4 * C], bo_in)
    tb1 = SC.copy(out=bq_r[:], in_=brows[0:1, 0:C], after=[tba])
    tb2 = SC.copy(out=bk_r[:], in_=brows[0:1, C:2 * C], after=[tbb])
    tb4 = SC.copy(out=bv_r[:], in_=brows[0:1, 2 * C:3 * C], after=[tbc])
    tb5 = SC.copy(out=bo_r[:], in_=brows[0:1, 3 * C:4 * C], after=[tbd])
    tz1 = V.memset(S_sb[:], 0.0)
    tz2 = V.memset(S_bf[:], 0.0)
    tz3 = V.memset(ks_sb[:], 0.0)
    tz4 = V.memset(ks_bf[:], 0.0)
    tz5 = V.memset(onesf[:], 1.0)
    eps_ln = cst[:, K_LN_EPS:K_LN_EPS + 1]
    eps_at = cst[:, K_ATT_EPS:K_ATT_EPS + 1]

    war_xt = None
    war_sq = None
    war_psa = None
    war_psk = None
    war_psv = None
    war_sc = None
    war_cn = None
    war_sd = None
    war_rb = None
    war_oacc = None

    def qkv_common(i, x_tiles, phase2):
        nonlocal war_xt, war_sq, war_psa, war_psk, war_psv
        aft_x = []
        if war_xt is not None:
            aft_x.append(war_xt)
        tx = SY.dma(xt[:], x_tiles[i], after=aft_x)
        aft_ln = [tx] + ([war_sq] if war_sq else [])
        e_ln(P, xt[:], st, C, eps_ln, negm[:, 0:1], rstd[:, 0:1], after=aft_ln)
        war_sq = None
        txl = V.tensor_scalar(out=xlb[:], in0=xt[:], scalar1=negm[:, 0:1],
                              scalar2=rstd[:, 0:1], op0=ALU.add, op1=ALU.mult)
        if not phase2:
            war_xt = txl
        tt = e_transpose_row(P, lambda j: xlb[:, j*128:(j+1)*128],
                             lambda j: xlt[:, j*128:(j+1)*128],
                             8, PS_D.ap()[:, 0:128], idb, after=[txl, t_i],
                             key="psd_xlt")
        for p in range(8):
            for j in range(8):
                aft = ([tt, twk] + ([war_psk] if war_psk else [])) if (p == 0 and j == 0) else []
                PE.matmul(PS_K.ap()[:, p*128:(p+1)*128],
                          wk_b[:, j*1024 + p*128: j*1024 + (p+1)*128],
                          xlt[:, j*128:(j+1)*128],
                          start=(j == 0), stop=False, after=aft)
            PE.matmul(PS_K.ap()[:, p*128:(p+1)*128],
                      bk_r[0:1, p*128:(p+1)*128], orow[:],
                      start=False, stop=True, after=([tb2] if p == 0 else []))
        tpek = PE.tok()
        tk1 = V.tensor_scalar(out=eluw[:], in0=PS_K.ap()[:], scalar1=0.0,
                              scalar2=None, op0=ALU.min, after=[tpek])
        tk2 = SC.activation(out=eluw[:], in_=eluw[:], func=AF.Exp, after=[tk1])
        tk3 = V.tensor_scalar(out=relw[:], in0=PS_K.ap()[:], scalar1=0.0,
                              scalar2=None, op0=ALU.max)
        war_psk = tk3
        tk4 = V.tensor_add(out=kp[:], in0=relw[:], in1=eluw[:], after=[tk2])
        for f in range(2):
            for j in range(8):
                aft = ([tt, twv] + (war_psv if isinstance(war_psv, list)
                                     else ([war_psv] if war_psv else []))) \
                    if (f == 0 and j == 0) else []
                PE.matmul(PS_V.ap()[:, f*512:(f+1)*512],
                          xlt[:, j*128:(j+1)*128],
                          wv_b[:, j*1024 + f*512: j*1024 + (f+1)*512],
                          start=(j == 0), stop=False, after=aft)
            PE.matmul(PS_V.ap()[:, f*512:(f+1)*512], orow[:],
                      bv_r[0:1, f*512:(f+1)*512],
                      start=False, stop=True, after=([tb4] if f == 0 else []))
        tpev = PE.tok()
        if phase2:
            tvv = V.tensor_copy(out=vtok[:], in_=PS_V.ap()[:], after=[tpev])
        else:
            tvv = V.tensor_scalar(out=vtok[:], in0=PS_V.ap()[:], scalar1=gcol[:, 0:1],
                                  scalar2=None, op0=ALU.mult, after=[tpev])
        war_psv = tvv
        tq4 = None
        if phase2:
            for p in range(8):
                for j in range(8):
                    aft = ([tt, twq] + ([war_psa] if war_psa else [])) if (p == 0 and j == 0) else []
                    PE.matmul(PS_A.ap()[:, p*128:(p+1)*128],
                              wq_b[:, j*1024 + p*128: j*1024 + (p+1)*128],
                              xlt[:, j*128:(j+1)*128],
                              start=(j == 0), stop=False, after=aft)
                PE.matmul(PS_A.ap()[:, p*128:(p+1)*128],
                          bq_r[0:1, p*128:(p+1)*128], orow[:],
                          start=False, stop=True, after=([tb1] if p == 0 else []))
            tpeq = PE.tok()
            tq1 = V.tensor_scalar(out=eluw[:], in0=PS_A.ap()[:], scalar1=0.0,
                                  scalar2=None, op0=ALU.min, after=[tpeq])
            tq2 = SC.activation(out=eluw[:], in_=eluw[:], func=AF.Exp, after=[tq1])
            tq3 = V.tensor_scalar(out=relw[:], in0=PS_A.ap()[:], scalar1=0.0,
                                  scalar2=None, op0=ALU.max)
            war_psa = tq3
            tq4 = V.tensor_add(out=qp[:], in0=relw[:], in1=eluw[:], after=[tq2])
        return tk4, tvv, tq4

    def state_update(p, tk4, tvv, use_gate):
        nonlocal war_sd
        td_prev = P._pst_last.get("psd_ktr")
        for a in range(2):
            aft = [tk4] + ([td_prev] if td_prev else [])
            PE.transpose(PS_D.ap()[:, 128 + 64*a:128 + 64*(a+1)],
                         kp[64*a:64*(a+1), p*128:(p+1)*128],
                         idb[64*a:64*(a+1), 64*a:64*(a+1)],
                         tile_position=(64*a, 0), after=aft)
            td_prev = None
        tpe = PE.tok()
        tkt = V.tensor_copy(out=ktr[:], in_=PS_D.ap()[:, 128:256], after=[tpe])
        P._pst_last["psd_ktr"] = tkt
        for a in range(2):
            aft0 = [tkt, tvv] + ([war_sd] if (a == 0 and war_sd) else [])
            PE.matmul(PS_V.ap()[64*a:64*(a+1), 0:64],
                      ktr[:, 64*a:64*(a+1)],
                      vtok[:, p*128 + 64*a: p*128 + 64*(a+1)],
                      start=True, stop=True, tile_position=(0, 64*a), after=aft0)
            PE.matmul(PS_V.ap()[64*a:64*(a+1), 64:65],
                      ktr[:, 64*a:64*(a+1)],
                      (gcol_b[:, 0:1] if use_gate else ocol[:]),
                      start=True, stop=True, tile_position=(0, 64*a),
                      after=([t_oc, t_gt] if a == 0 else []))
        tpes = PE.tok()
        V.tensor_add(out=S_sb[:, p*64:(p+1)*64], in0=S_sb[:, p*64:(p+1)*64],
                     in1=PS_V.ap()[:, 0:64], after=[tpes])
        V.tensor_copy(out=S_bf[:, p*64:(p+1)*64], in_=S_sb[:, p*64:(p+1)*64])
        V.tensor_add(out=ks_sb[:, p:p+1], in0=ks_sb[:, p:p+1],
                     in1=PS_V.ap()[:, 64:65])
        war_sd = V.tensor_copy(out=ks_bf[:, p:p+1], in_=ks_sb[:, p:p+1])
        return war_sd

    for i in range(npre):
        tk4, tvv, _ = qkv_common(i, xp_t, phase2=False)
        for p in range(8):
            state_update(p, tk4, tvv, use_gate=True)

    for i in range(ntt):
        tk4, tvv, tq4 = qkv_common(i, xo_in_t, phase2=True)
        for p in range(8):
            tctx = []
            for a in range(2):
                aft = [tq4, tk4] + ([war_sc] if war_sc else [])
                PE.matmul(PS_B.ap()[:, 0:128],
                          kp[64*a:64*(a+1), p*128:(p+1)*128],
                          qp[64*a:64*(a+1), p*128:(p+1)*128],
                          start=True, stop=True, tile_position=(64*a, 0),
                          after=aft)
                tpesc = PE.tok()
                tmsk = V.tensor_mul(out=scm[:], in0=PS_B.ap()[:, 0:128], in1=trim[:],
                                    after=[tpesc])
                war_sc = tmsk
                aftc = [tmsk, tvv] + ([war_cn] if (a == 0 and war_cn) else [])
                PE.matmul(PS_B.ap()[64*a:64*(a+1), 128:256],
                          vtok[:, p*128 + 64*a: p*128 + 64*(a+1)],
                          scm[:], start=True, stop=False,
                          tile_position=(0, 64*a), after=aftc)
                PE.matmul(PS_B.ap()[64*a:64*(a+1), 128:256],
                          S_bf[64*a:64*(a+1), p*64:(p+1)*64],
                          qp[64*a:64*(a+1), p*128:(p+1)*128],
                          start=False, stop=True, tile_position=(64*a, 64*a))
                PE.matmul(PS_B.ap()[64*a:64*a+1, 256:384],
                          ocol[:], scm[:], start=True, stop=False,
                          tile_position=(0, 64*a))
                PE.matmul(PS_B.ap()[64*a:64*a+1, 256:384],
                          ks_bf[64*a:64*(a+1), p:p+1],
                          qp[64*a:64*(a+1), p*128:(p+1)*128],
                          start=False, stop=True, tile_position=(64*a, 64*a))
                tctx.append(PE.tok())
            tn1 = V.tensor_scalar(out=rnrow[0:1, :], in0=PS_B.ap()[0:1, 256:384],
                                  scalar1=ATT_EPS, scalar2=None,
                                  op0=ALU.add, after=[tctx[0]])
            tn1 = V.reciprocal(out=rnrow[0:1, :], in_=rnrow[0:1, :])
            tn2 = V.tensor_scalar(out=rnrow[64:65, :], in0=PS_B.ap()[64:65, 256:384],
                                  scalar1=ATT_EPS, scalar2=None,
                                  op0=ALU.add, after=[tctx[1]])
            tn2 = V.reciprocal(out=rnrow[64:65, :], in_=rnrow[64:65, :])
            tcv = V.tensor_copy(out=ctxs[:], in_=PS_B.ap()[:, 128:256])
            war_cn = tcv
            for a in range(2):
                aftr = [tn1 if a == 0 else tn2] + ([war_rb] if (a == 0 and war_rb) else [])
                PE.matmul(PS_V.ap()[64*a:64*(a+1), 128:256],
                          onesf[64*a:64*a+1, :],
                          rnrow[64*a:64*a+1, :], start=True, stop=True,
                          tile_position=(64*a, 64*a), after=aftr)
            tper = PE.tok()
            war_rb = V.tensor_tensor(out=afm[:, p*128:(p+1)*128], in0=ctxs[:],
                                     in1=PS_V.ap()[:, 128:256], op=ALU.mult,
                                     after=[tper, tcv])
            state_update(p, tk4, tvv, use_gate=False)
        war_psv = [war_rb] if war_rb else []
        war_psv.append(war_sd)
        for f in range(2):
            for j in range(8):
                aft = ([war_psa, two, war_rb] if (f == 0 and j == 0) else [])
                PE.matmul(PS_A.ap()[:, f*512:(f+1)*512],
                          afm[:, j*128:(j+1)*128],
                          wo_b[:, j*1024 + f*512: j*1024 + (f+1)*512],
                          start=(j == 0), stop=False, after=aft)
            PE.matmul(PS_A.ap()[:, f*512:(f+1)*512], orow[:],
                      bo_r[0:1, f*512:(f+1)*512],
                      start=False, stop=True, after=([tb5] if f == 0 else []))
        tpeo = PE.tok()
        aft_ev = [tpeo] + ([war_oacc] if war_oacc else [])
        tv0 = V.tensor_copy(out=oacc[:], in_=PS_A.ap()[:], after=aft_ev)
        war_psa = tv0
        aft_ln2 = [war_sq] if war_sq else []
        e_ln(P, oacc[:], st, C, eps_ln, negm[:, 0:1], rstd[:, 0:1], after=aft_ln2)
        tv = V.tensor_scalar(out=relw[:], in0=oacc[:], scalar1=negm[:, 0:1],
                             scalar2=rstd[:, 0:1], op0=ALU.add, op1=ALU.mult)
        tv = V.tensor_add(out=oacc[:], in0=relw[:], in1=xt[:])
        war_xt = tv
        tst = GP.dma(xo_t[i], oacc[:], after=[tv])
        war_oacc = tst
        war_sq = None


def build_attn0(tsh=TSH, tpre=TSH):
    P = Prog()
    io = {
        "xpre": P.din("xpre", [tpre, C]),
        "xown": P.din("xown", [tsh, C]),
        "gate": P.din("gate", [128, 1], F32),
        "wq": P.din("wq", [C, C]),
        "wk": P.din("wk", [C, C]),
        "wv": P.din("wv", [C, C]),
        "wo": P.din("wo", [C, C]),
        "bq": P.din("bq", [1, C]),
        "bk": P.din("bk", [1, C]),
        "bv": P.din("bv", [1, C]),
        "bo": P.din("bo", [1, C]),
        "ones_row": P.din("ones_row", [1, 128], BF16),
        "ones_col": P.din("ones_col", [128, 1], BF16),
        "tri": P.din("tri", [128, 128], F32),
        "ident": P.din("ident", [128, 128], BF16),
        "consts": P.din("consts", [128, NCONST]),
        "xout": P.dout("xout", [tsh, C]),
    }
    P.begin()
    emit_attn0(P, io, tsh=tsh, tpre=tpre)
    nc = P.finish()
    return nc, P




def emit_mamba_a(P, io, tt=T, start_after=()):
    xin = io["xin"]
    lnw = io["lnw"]
    lnb = io["lnb"]
    win = io["win"]
    cw = io["cw"]
    cb = io["cb"]
    xpw = io["xpw"]
    ident_in = io["ident"]
    consts = io["consts"]
    xcs_o = io["xcs"]
    zs_o = io["zs"]
    dblp_o = io["dblp"]

    ngr = tt // 512
    x_t = xin.rearrange("(n p) c -> n p c", p=128)

    cst = P.sb("cst", [128, NCONST], F32)
    idb = P.sb("idb", [128, 128], BF16)
    lnw_s = P.sb("lnw_s", [128, 8], F32)
    lnb_s = P.sb("lnb_s", [128, 8], F32)
    cw_s = P.sb("cw_s", [128, 8 * D_CONV], F32)
    cb_s = P.sb("cb_s", [128, 8], F32)
    wstage = P.sb("wstage", [128, 2 * 2048], F32)
    win_b = P.sb("win_b", [128, 8 * 2048], BF16)
    xpw_b = P.sb("xpw_b", [128, 8 * 96], BF16)

    xt = P.sb("xt", [128, C], F32)
    sq = P.sb("sq", [128, C], F32)
    s1 = P.sb("s1", [128, 1], F32)
    s2 = P.sb("s2", [128, 1], F32)
    negm = P.sb("negm", [128, 1], F32)
    rstd = P.sb("rstd", [128, 1], F32)
    xlb = P.sb("xlb", [128, C], BF16)
    xlt = P.sb("xlt", [128, 8 * 512], BF16)
    xzb = P.sb("xzb", [128, 8 * 515], F32)
    convw = P.sb("convw", [128, 512], F32)
    xcs_g = P.sb("xcs_g", [128, 8 * 512], BF16)
    zs_g = P.sb("zs_g", [128, 8 * 512], BF16)
    dbl_g = P.sb("dbl_g", [96, 512], F32)

    ps_m = P.psum("ps_m", [128, 512])
    ps_m2 = P.psum("ps_m2", [128, 512])
    ps_t = P.psum("ps_t", [128, 128], BF16)
    ps_x = P.psum("ps_x", [96, 512])

    st = {"sq": sq, "s1": s1, "s2": s2}

    SY, GP, V, SC, PE = P.SY, P.GP, P.V, P.SC, P.PE

    t_c = SY.dma(cst[:], consts)
    t_i = SY.dma(idb[:], ident_in)
    t1 = SY.dma(lnw_s[:], lnw)
    t2 = SY.dma(lnb_s[:], lnb)
    t3 = SY.dma(cw_s[:], cw)
    t4 = SY.dma(cb_s[:], cb)
    SC.copy(out=s1[:, 0:1], in_=cst[:, 0:1], after=[t_c, t1, t2, t3, t4, t_i])
    twi = e_load_weight_bf16(P, win, wstage,
                             lambda j: win_b[:, j * 2048:(j + 1) * 2048], 8, 2048)
    twx = e_load_weight_bf16(P, xpw, wstage,
                             lambda j: xpw_b[:, j * 96:(j + 1) * 96], 8, 96)
    eps_ln = cst[:, K_LN_EPS:K_LN_EPS + 1]

    war_xt = None
    war_sq = None
    war_psm = [None, None]
    war_psx = None
    war_xlt = None
    war_convw = None
    t_hist = V.memset(xzb[:], 0.0)

    for g in range(ngr):
        for k in range(4):
            i = g * 4 + k
            aft_x = ([war_xt] if war_xt else []) + list(start_after)
            tx = SY.dma(xt[:], x_t[i], after=aft_x)
            aft = [tx] + ([war_sq] if war_sq else [])
            e_ln(P, xt[:], st, C, eps_ln, negm[:, 0:1], rstd[:, 0:1], after=aft)
            war_sq = None
            txl = V.tensor_scalar(out=xlb[:], in0=xt[:], scalar1=negm[:, 0:1],
                                  scalar2=rstd[:, 0:1], op0=ALU.add, op1=ALU.mult)
            war_xt = txl
            aft_t = [txl, t_i] + ([war_xlt] if (k == 0 and war_xlt) else [])
            war_xlt = None
            tt_tok = e_transpose_row(
                P, lambda j: xlb[:, j * 128:(j + 1) * 128],
                lambda j, k=k: xlt[:, j * 512 + k * 128: j * 512 + (k + 1) * 128],
                8, ps_t.ap()[:, 0:128], idb, after=aft_t, key="pst_a")
        tg = None
        for j in range(8):
            tg = V.tensor_scalar(out=xlt[:, j * 512:(j + 1) * 512],
                                 in0=xlt[:, j * 512:(j + 1) * 512],
                                 scalar1=lnw_s[:, j:j + 1], scalar2=lnb_s[:, j:j + 1],
                                 op0=ALU.mult, op1=ALU.add)
        for r in range(16):
            pb = r % 2
            psm = (ps_m, ps_m2)[pb]
            for j in range(8):
                aft = []
                if j == 0:
                    aft = [tg, twi]
                    if war_psm[pb]:
                        aft.append(war_psm[pb])
                PE.matmul(psm.ap()[:], win_b[:, j * 2048 + r * 128: j * 2048 + (r + 1) * 128],
                          xlt[:, j * 512:(j + 1) * 512],
                          start=(j == 0), stop=(j == 7), after=aft)
            tpe = PE.tok()
            if r < 8:
                tcp = V.tensor_copy(out=xzb[:, r * 515 + 3: r * 515 + 515],
                                    in_=psm.ap()[:], after=[tpe] +
                                    ([t_hist] if (t_hist and r == 0) else []))
                war_psm[pb] = tcp
                tv = V.tensor_scalar(out=convw[:], in0=xzb[:, r * 515: r * 515 + 512],
                                     scalar1=cw_s[:, r * 4:r * 4 + 1], scalar2=None,
                                     op0=ALU.mult,
                                     after=([war_convw] if war_convw else []))
                for kk in range(1, 4):
                    tv = V.scalar_tensor_tensor(
                        out=convw[:], in0=xzb[:, r * 515 + kk: r * 515 + kk + 512],
                        scalar=cw_s[:, r * 4 + kk:r * 4 + kk + 1], in1=convw[:],
                        op0=ALU.mult, op1=ALU.add)
                tsl = SC.activation(out=xcs_g[:, r * 512:(r + 1) * 512], in_=convw[:],
                                    func=AF.Silu, bias=cb_s[:, r:r + 1],
                                    after=[tv] + ([war_psm[0]] if (r == 0 and war_psm[0]) else []))
                war_convw = tsl
            else:
                rr = r - 8
                aft_z = [tpe]
                tsl = SC.activation(out=zs_g[:, rr * 512:(rr + 1) * 512],
                                    in_=psm.ap()[:], func=AF.Silu, after=aft_z)
                war_psm[pb] = tsl
        t_xc_done = tsl
        th = None
        for r in range(8):
            th = V.tensor_copy(out=xzb[:, r * 515: r * 515 + 3],
                               in_=xzb[:, r * 515 + 512: r * 515 + 515])
        t_hist = th
        for j in range(8):
            aft = []
            if j == 0:
                aft = [t_xc_done, twx]
                if war_psx:
                    aft.append(war_psx)
            PE.matmul(ps_x.ap()[:], xpw_b[:, j * 96:(j + 1) * 96],
                      xcs_g[:, j * 512:(j + 1) * 512],
                      start=(j == 0), stop=(j == 7), after=aft)
        tpx = PE.tok()
        war_psx = V.tensor_copy(out=dbl_g[:], in_=ps_x.ap()[:], after=[tpx])
        xcs_ot = xcs_o.rearrange("(r p) t -> r p t", p=128)
        zs_ot = zs_o.rearrange("(r p) t -> r p t", p=128)
        ts1_ = ts2_ = None
        for r in range(8):
            ts1_ = GP.dma(xcs_ot[r][:, g * 512:(g + 1) * 512],
                          xcs_g[:, r * 512:(r + 1) * 512], after=[t_xc_done])
            ts2_ = GP.dma(zs_ot[r][:, g * 512:(g + 1) * 512],
                          zs_g[:, r * 512:(r + 1) * 512])
        ts3_ = GP.dma(dblp_o[:, g * 512:(g + 1) * 512], dbl_g[:], after=[war_psx])
        war_psm = [ts2_, ts3_]
        war_xlt = tpe


def build_mamba_a(tt=T):
    P = Prog()
    io = {
        "xin": P.din("xin", [tt, C]),
        "lnw": P.din("lnw", [128, 8]),
        "lnb": P.din("lnb", [128, 8]),
        "win": P.din("win", [C, 2048]),
        "cw": P.din("cw", [128, 8 * D_CONV]),
        "cb": P.din("cb", [128, 8]),
        "xpw": P.din("xpw", [C, 96]),
        "ident": P.din("ident", [128, 128], BF16),
        "consts": P.din("consts", [128, NCONST]),
        "xcs": P.dout("xcs", [C, tt], BF16),
        "zs": P.dout("zs", [C, tt], BF16),
        "dblp": P.dout("dblp", [96, tt]),
    }
    P.begin()
    emit_mamba_a(P, io, tt=tt)
    nc = P.finish()
    return nc, P


def emit_mamba_b(P, io, tt=T, lc=1024, upto=99, sub=9, nosync=0, start_after=()):
    xcs_i = io["xcs"]
    zs_i = io["zs"]
    dbl0 = io["dbl0"]
    dbl1 = io["dbl1"]
    dtw = io["dtw"]
    dtb = io["dtb"]
    alog = io["alog"]
    dsk = io["dsk"]
    selbc = io["selbc"]
    ident_in = io["ident"]
    consts = io["consts"]
    yg_o = io["yg"]

    nlc = tt // lc

    cst = P.sb("cst", [128, NCONST], F32)
    idb = P.sb("idb", [128, 128], BF16)
    dtb_s = P.sb("dtb_s", [128, 8], F32)
    acol = P.sb("acol", [128, 8 * D_ST], F32)
    dsk_s = P.sb("dsk_s", [128, 8], F32)
    selb = P.sb("selb", [32, 2 * D_ST * 128], BF16)
    wstage = P.sb("wstage", [128, 1024], F32)
    dtw_b = P.sb("dtw_b", [DT_RANK, C], BF16)
    hst = P.sb("hst", [128, 8 * D_ST], F32)

    dtr = P.sb("dtr", [DT_RANK, lc], BF16)
    bcr = P.sb("bcr", [32, lc], BF16)
    dt_f = P.sb("dt_f", [128, 8 * lc], F32)
    u_f = P.sb("u_f", [128, 8 * lc], BF16)
    xc_f = P.sb("xc_f", [128, 8 * lc], BF16)
    zs_f = P.sb("zs_f", [128, 8 * lc], BF16)
    Ball = P.sb("Ball", [128, D_ST * lc], BF16)
    Call = P.sb("Call", [128, D_ST * lc], BF16)
    dA = P.sb("dA", [128, 2 * lc], F32)
    dBu = P.sb("dBu", [128, 2 * lc], BF16)
    hbuf = P.sb("hbuf", [128, 2 * lc], BF16)
    prod = P.sb("prod", [128, 2 * lc], BF16)
    yv = P.sb("yv", [128, lc], F32)
    ygb = P.sb("ygb", [128, lc], BF16)
    sp1 = P.sb("sp1", [128, lc], F32)
    sp2 = P.sb("sp2", [128, lc], F32)

    ps_d = P.psum("ps_d", [128, 512])
    ps_b = P.psum("ps_b", [128, 1024])
    ps_y = P.psum("ps_y", [128, 1024])

    SY, GP, V, SC, PE = P.SY, P.GP, P.V, P.SC, P.PE
    if nosync:
        V.selfsync = False
        SC.selfsync = False

    t_c = SY.dma(cst[:], consts)
    t_i = SY.dma(idb[:], ident_in)
    t1 = SY.dma(dtb_s[:], dtb)
    t2 = SY.dma(acol[:], alog)
    t3 = SY.dma(dsk_s[:], dsk)
    t4 = SY.dma(selb[:], selbc)
    tf = SC.copy(out=hst[:, 0:1], in_=cst[:, 0:1], after=[t_c, t1, t2, t3, t4, t_i])
    ta = SC.activation(out=acol[:], in_=acol[:], func=AF.Exp, after=[t2])
    ta = V.tensor_scalar(out=acol[:], in0=acol[:], scalar1=-1.0, scalar2=None,
                         op0=ALU.mult, after=[ta])
    t5 = SY.dma(wstage[0:DT_RANK, 0:C], dtw)
    t6 = SC.copy(out=dtw_b[:], in_=wstage[0:DT_RANK, 0:C], after=[t5])
    tz = V.memset(hst[:], 0.0, after=[tf])
    war = {}
    war["stage_d"] = t6

    xcs_t = xcs_i.rearrange("(r p) t -> r p t", p=128)
    zs_t = zs_i.rearrange("(r p) t -> r p t", p=128)
    yg_t = yg_o.rearrange("(r p) t -> r p t", p=128)

    def W(key):
        if key not in war:
            return []
        v = war[key]
        return list(v) if isinstance(v, list) else [v]

    for li in range(nlc):
        t0 = li * lc
        tld = []
        for r in range(8):
            tld.append(SY.dma(xc_f[:, r * lc:(r + 1) * lc], xcs_t[r][:, t0:t0 + lc],
                              after=W("xc_f")))
            tld.append(SY.dma(zs_f[:, r * lc:(r + 1) * lc], zs_t[r][:, t0:t0 + lc],
                              after=W("zs_f")))
        tda = SY.dma(wstage[0:DT_RANK, 0:lc], dbl0[0:DT_RANK, t0:t0 + lc],
                     after=W("stage_d") + list(start_after))
        tdb = SY.dma(sp2[0:DT_RANK, 0:lc], dbl1[0:DT_RANK, t0:t0 + lc],
                     after=W("sp2"))
        tdc = V.tensor_add(out=sp1[0:DT_RANK, 0:lc], in0=wstage[0:DT_RANK, 0:lc],
                           in1=sp2[0:DT_RANK, 0:lc], after=[tda, tdb] + W("sp1"))
        tdd = V.tensor_copy(out=dtr[:], in_=sp1[0:DT_RANK, 0:lc], after=W("dtr"))
        tba_ = SY.dma(wstage[0:32, 0:lc], dbl0[DT_RANK:96, t0:t0 + lc],
                      after=[tdc])
        tbb = SY.dma(sp2[0:32, 0:lc], dbl1[DT_RANK:96, t0:t0 + lc], after=[tdc])
        war["stage_d"] = tbc = V.tensor_add(out=sp1[0:32, 0:lc], in0=wstage[0:32, 0:lc],
                                            in1=sp2[0:32, 0:lc],
                                            after=[tba_, tbb])
        war["sp2"] = tbc
        tbd = V.tensor_copy(out=bcr[:], in_=sp1[0:32, 0:lc], after=W("bcr"))
        if upto < 1:
            continue
        tev = None
        for n in range(2 * D_ST):
            dst = Ball if n < D_ST else Call
            nn = n % D_ST
            for h in range(lc // 512):
                aft = [tbd] if h == 0 else []
                if n == 0 and h == 0:
                    aft += W("ps_b")
                if h == 0 and tev is not None:
                    aft.append(tev)
                PE.matmul(ps_b.ap()[:, h * 512:(h + 1) * 512],
                          selb[:, n * 128:(n + 1) * 128],
                          bcr[:, h * 512:(h + 1) * 512],
                          start=True, stop=True, after=aft)
            tpb = PE.tok()
            war["bcr"] = tpb
            tev = SC.copy(out=dst[:, nn * lc:(nn + 1) * lc], in_=ps_b.ap()[:, 0:lc],
                          after=[tpb] + ((W("Ball") + W("Call")) if n == 0 else []))
        war["ps_b"] = tev
        t_bc = tev
        if upto < 2:
            continue
        import os
        rlim = int(os.environ.get("R_LIMIT", "8"))
        tdt = None
        for r in range(rlim):
            for h in range(lc // 512):
                aft = W("ps_d") + W("sp1")
                if r == 0 and h == 0:
                    aft += [tdd, t6]
                PE.matmul(ps_d.ap()[:],
                          dtw_b[:, r * 128:(r + 1) * 128],
                          dtr[:, h * 512:(h + 1) * 512],
                          start=True, stop=True, after=aft)
                tpd = PE.tok()
                sl = slice(r * lc + h * 512, r * lc + (h + 1) * 512)
                war["dtr"] = tpd
                if sub == -10:
                    tso = SC.activation(out=sp1[:, 0:512], in_=ps_d.ap()[:], func=AF.Abs,
                                        after=[tpd] + W("sp1"))
                    trl = V.tensor_scalar(out=sp2[:, 0:512], in0=ps_d.ap()[:],
                                          scalar1=dtb_s[:, r:r + 1], scalar2=0.0,
                                          op0=ALU.add, op1=ALU.max)
                    war["ps_d"] = trl
                    tdt = GP.tensor_add(out=dt_f[:, sl], in0=sp2[:, 0:512],
                                        in1=sp1[:, 0:512], after=[tso, trl])
                    war["sp1"] = tdt
                    continue
                if sub == -11:
                    tso = SC.activation(out=sp1[:, 0:512], in_=ps_d.ap()[:], func=AF.Copy,
                                        bias=dtb_s[:, r:r + 1], after=[tpd] + W("sp1"))
                    trl = V.tensor_scalar(out=sp2[:, 0:512], in0=ps_d.ap()[:],
                                          scalar1=dtb_s[:, r:r + 1], scalar2=0.0,
                                          op0=ALU.add, op1=ALU.max)
                    war["ps_d"] = trl
                    tdt = GP.tensor_add(out=dt_f[:, sl], in0=sp2[:, 0:512],
                                        in1=sp1[:, 0:512], after=[tso, trl])
                    war["sp1"] = tdt
                    continue
                if sub == -8:
                    tso = SC.copy(out=sp1[:, 0:512], in_=ps_d.ap()[:],
                                  after=[tpd] + W("sp1"))
                    trl = V.tensor_scalar(out=sp2[:, 0:512], in0=ps_d.ap()[:],
                                          scalar1=dtb_s[:, r:r + 1], scalar2=0.0,
                                          op0=ALU.add, op1=ALU.max)
                    war["ps_d"] = trl
                    tdt = GP.tensor_add(out=dt_f[:, sl], in0=sp2[:, 0:512],
                                        in1=sp1[:, 0:512], after=[tso, trl])
                    war["sp1"] = tdt
                    continue
                if sub == -9:
                    tso = SC.copy(out=sp1[:, 0:512], in_=ps_d.ap()[:],
                                  after=[tpd] + W("sp1"))
                    trl = V.tensor_scalar(out=sp2[:, 0:512], in0=ps_d.ap()[:],
                                          scalar1=dtb_s[:, r:r + 1], scalar2=0.0,
                                          op0=ALU.add, op1=ALU.max)
                    war["ps_d"] = trl
                    tdt = V.tensor_add(out=sp2[:, 0:512], in0=sp2[:, 0:512],
                                       in1=sp1[:, 0:512])
                    war["sp1"] = tdt
                    continue
                if sub == -7:
                    tso = SC.copy(out=sp1[:, 0:512], in_=ps_d.ap()[:],
                                  after=[tpd] + W("sp1"))
                    trl = V.tensor_scalar(out=sp2[:, 0:512], in0=ps_d.ap()[:],
                                          scalar1=dtb_s[:, r:r + 1], scalar2=0.0,
                                          op0=ALU.add, op1=ALU.max)
                    war["ps_d"] = trl
                    war["sp1"] = tso
                    tdt = trl
                    continue
                if sub <= -3 and sub not in (-5, -6):
                    continue
                if sub == -2:
                    trl = V.tensor_scalar(out=sp2[:, 0:512], in0=ps_d.ap()[:],
                                          scalar1=dtb_s[:, r:r + 1], scalar2=0.0,
                                          op0=ALU.add, op1=ALU.max)
                    war["ps_d"] = trl
                    tdt = trl
                    continue
                if sub == -1:
                    tso = SC.copy(out=sp1[:, 0:512], in_=ps_d.ap()[:],
                                  after=[tpd] + W("sp1"))
                    war["ps_d"] = tso
                    war["sp1"] = tso
                    tdt = tso
                    continue
                if sub == -5:
                    tso = SC.copy(out=sp1[:, 0:512], in_=ps_d.ap()[:],
                                  after=[tpd] + W("sp1"))
                    trl = V.tensor_scalar(out=sp2[:, 0:512], in0=ps_d.ap()[:],
                                          scalar1=dtb_s[:, r:r + 1], scalar2=0.0,
                                          op0=ALU.add, op1=ALU.max)
                    war["ps_d"] = trl
                    tdt = V.tensor_add(out=sp2[:, 0:512], in0=sp2[:, 0:512],
                                       in1=sp1[:, 0:512], after=[tso])
                    war["sp1"] = tdt
                    continue
                if sub == -6:
                    tso = SC.copy(out=sp1[:, 0:512], in_=ps_d.ap()[:],
                                  after=[tpd] + W("sp1"))
                    trl = V.tensor_scalar(out=sp2[:, 0:512], in0=ps_d.ap()[:],
                                          scalar1=dtb_s[:, r:r + 1], scalar2=0.0,
                                          op0=ALU.add, op1=ALU.max)
                    war["ps_d"] = trl
                    tdt = V.tensor_copy(out=dt_f[:, sl], in_=sp2[:, 0:512])
                    war["sp1"] = tso
                    continue
                tpre = V.tensor_scalar(out=sp1[:, 0:512], in0=ps_d.ap()[:],
                                       scalar1=dtb_s[:, r:r + 1], scalar2=None,
                                       op0=ALU.add, after=[tpd] + W("sp1"))
                trl = V.tensor_scalar(out=sp2[:, 0:512], in0=ps_d.ap()[:],
                                      scalar1=dtb_s[:, r:r + 1], scalar2=0.0,
                                      op0=ALU.add, op1=ALU.max)
                war["ps_d"] = trl
                tso = SC.activation(out=sp1[:, 0:512], in_=sp1[:, 0:512], func=AF.Abs,
                                    after=[tpre])
                tso = SC.activation(out=sp1[:, 0:512], in_=sp1[:, 0:512], func=AF.Exp,
                                    scale=cst[:, K_NEG1:K_NEG1 + 1])
                tso = SC.activation(out=sp1[:, 0:512], in_=sp1[:, 0:512], func=AF.Ln,
                                    bias=cst[:, K_ONE:K_ONE + 1])
                tdt = GP.tensor_add(out=dt_f[:, sl], in0=sp2[:, 0:512],
                                    in1=sp1[:, 0:512], after=[tso, trl])
                war["sp1"] = tdt
                war["sp2"] = tdt
        if upto < 3:
            continue
        tu = None
        for r in range(8):
            tu = V.tensor_mul(out=u_f[:, r * lc:(r + 1) * lc],
                              in0=dt_f[:, r * lc:(r + 1) * lc],
                              in1=xc_f[:, r * lc:(r + 1) * lc],
                              after=[tld[2 * r], tdt] + W("u_f"))
        if upto < 4:
            continue
        for r in range(8):
            for n in range(D_ST):
                s = n % 2
                sl_s = slice(s * lc, (s + 1) * lc)
                tda_ = SC.activation(out=dA[:, sl_s], in_=dt_f[:, r * lc:(r + 1) * lc],
                                     func=AF.Exp,
                                     scale=acol[:, r * D_ST + n:r * D_ST + n + 1],
                                     after=[tdt] + W(f"dA{s}"))
                tdbu = V.tensor_mul(out=dBu[:, sl_s], in0=u_f[:, r * lc:(r + 1) * lc],
                                    in1=Ball[:, n * lc:(n + 1) * lc],
                                    after=[tu, t_bc] + W(f"dBu{s}"))
                tsc = V.tensor_tensor_scan(out=hbuf[:, sl_s], data0=dA[:, sl_s],
                                           data1=dBu[:, sl_s],
                                           initial=hst[:, r * D_ST + n:r * D_ST + n + 1],
                                           op0=ALU.mult, op1=ALU.add,
                                           after=[tda_, tz] + W(f"hbuf{s}"))
                war[f"dA{s}"] = tsc
                war[f"dBu{s}"] = tsc
                ths = V.tensor_copy(out=hst[:, r * D_ST + n:r * D_ST + n + 1],
                                     in_=hbuf[:, s * lc + lc - 1: s * lc + lc],
                                     after=[tsc])
                tpr = GP.tensor_mul(out=prod[:, sl_s], in0=hbuf[:, sl_s],
                                    in1=Call[:, n * lc:(n + 1) * lc],
                                    after=[tsc] + W(f"prod{s}"))
                war[f"hbuf{s}"] = [tpr, ths]
                war["Ball"] = tdbu
                war["Call"] = tpr
                for h in range(lc // 512):
                    aft = [tpr]
                    if n == 0 and h == 0:
                        aft += W("ps_y")
                    PE.matmul(ps_y.ap()[:, h * 512:(h + 1) * 512],
                              idb[:], prod[:, s * lc + h * 512: s * lc + (h + 1) * 512],
                              start=(n == 0), stop=(n == D_ST - 1), after=aft)
                war[f"prod{s}"] = PE.tok()
            tpy = PE.tok()
            ty1 = V.scalar_tensor_tensor(out=yv[:], in0=xc_f[:, r * lc:(r + 1) * lc],
                                         scalar=dsk_s[:, r:r + 1], in1=ps_y.ap()[:, 0:lc],
                                         op0=ALU.mult, op1=ALU.add, after=[tpy])
            war["ps_y"] = ty1
            ty2 = V.tensor_mul(out=ygb[:], in0=yv[:], in1=zs_f[:, r * lc:(r + 1) * lc],
                               after=[tld[2 * r + 1]] + W("ygb"))
            war["ygb"] = tst = GP.dma(yg_t[r][:, t0:t0 + lc], ygb[:], after=[ty2])
        war["xc_f"] = ty1
        war["zs_f"] = ty2


def build_mamba_b(tt=T, lc=1024, upto=99, sub=9, nosync=0):
    P = Prog()
    io = {
        "xcs": P.din("xcs", [C, tt], BF16),
        "zs": P.din("zs", [C, tt], BF16),
        "dbl0": P.din("dbl0", [96, tt]),
        "dbl1": P.din("dbl1", [96, tt]),
        "dtw": P.din("dtw", [DT_RANK, C]),
        "dtb": P.din("dtb", [128, 8]),
        "alog": P.din("alog", [128, 8 * D_ST]),
        "dsk": P.din("dsk", [128, 8]),
        "selbc": P.din("selbc", [32, 2 * D_ST * 128], BF16),
        "ident": P.din("ident", [128, 128], BF16),
        "consts": P.din("consts", [128, NCONST]),
        "yg": P.dout("yg", [C, tt], BF16),
    }
    P.begin()
    emit_mamba_b(P, io, tt=tt, lc=lc, upto=upto, sub=sub, nosync=nosync)
    nc = P.finish()
    return nc, P


def emit_mamba_c(P, io, tsh=TSH, final=False, start_after=(), yg_colofs=0):
    yg = io["yg"]
    xres_in = io["xres"]
    wout = io["wout"]
    ident_in = io["ident"]
    consts = io["consts"]
    if final:
        nfw = io["nfw"]
        nfb = io["nfb"]
        xout2 = io["xout2"]
    x_out = io["xout"]

    ntt = tsh // 128
    xr_t = xres_in.rearrange("(n p) c -> n p c", p=128)
    xo_t = x_out.rearrange("(n p) c -> n p c", p=128)
    if final:
        x2_t = xout2.rearrange("(n p) c -> n p c", p=128)
    yg_t = yg.rearrange("(r p) t -> r p t", p=128)

    cst = P.sb("cst", [128, NCONST], F32)
    wstage = P.sb("wstage", [128, 2 * 1024], F32)
    wo_b = P.sb("wo_b", [128, 16 * 1024], BF16)
    ygs = P.sb("ygs", [128, 16 * 128], BF16)
    xt = P.sb("xt", [128, C], F32)
    ot = P.sb("ot", [128, C], F32)
    sq = P.sb("sq", [128, C], F32)
    s1 = P.sb("s1", [128, 1], F32)
    s2 = P.sb("s2", [128, 1], F32)
    negm = P.sb("negm", [128, 1], F32)
    rstd = P.sb("rstd", [128, 1], F32)
    if final:
        nfw_s = P.sb("nfw_s", [128, C], F32)
        nfb_s = P.sb("nfb_s", [128, C], F32)
        x2t = P.sb("x2t", [128, C], F32)
    ps_o = P.psum("ps_o", [128, 1024])

    st = {"sq": sq, "s1": s1, "s2": s2}

    SY, GP, V, SC, PE = P.SY, P.GP, P.V, P.SC, P.PE
    t_c = SY.dma(cst[:], consts)
    SC.copy(out=s1[:, 0:1], in_=cst[:, 0:1], after=[t_c])
    if final:
        tn1 = SY.dma(nfw_s[:], nfw)
        tn2 = SY.dma(nfb_s[:], nfb)
        V.tensor_copy(out=s2[:, 0:1], in_=cst[:, 0:1], after=[tn1, tn2])
    two = e_load_weight_bf16(P, wout, wstage,
                             lambda j: wo_b[:, j * 1024:(j + 1) * 1024], 16, 1024)
    eps_ln = cst[:, K_LN_EPS:K_LN_EPS + 1]

    war = {}

    def W(key):
        return [war[key]] if key in war else []

    for i in range(ntt):
        c0 = yg_colofs + i * 128
        tld = [SY.dma(ygs[:].rearrange("p (r t) -> p r t", r=16),
                      yg.rearrange("(r p) t -> p r t", p=128)[:, :, c0:c0 + 128],
                      after=W("ygs") + (list(start_after) if i == 0 else []))]
        tx = SY.dma(xt[:], xr_t[i], after=W("xt"))
        if final:
            tx2 = SY.dma(x2t[:], x2_t[i], after=W("x2t"))
        for f in range(2):
            for j in range(16):
                aft = []
                if f == 0 and j == 0:
                    aft = [tld[0], two] + W("ps_o")
                PE.matmul(ps_o.ap()[:, f * 512:(f + 1) * 512],
                          ygs[:, j * 128:(j + 1) * 128],
                          wo_b[:, j * 1024 + f * 512: j * 1024 + (f + 1) * 512],
                          start=(j == 0), stop=(j == 15), after=aft)
        tpe = PE.tok()
        war["ygs"] = tpe
        tv = V.tensor_add(out=xt[:], in0=ps_o.ap()[:], in1=xt[:], after=[tpe, tx])
        war["ps_o"] = tv
        if not final:
            war["xt"] = GP.dma(xo_t[i], xt[:], after=[tv])
        else:
            aft = W("sq")
            e_ln(P, xt[:], st, C, eps_ln, negm[:, 0:1], rstd[:, 0:1], after=aft)
            tv = V.tensor_scalar(out=ot[:], in0=xt[:], scalar1=negm[:, 0:1],
                                 scalar2=rstd[:, 0:1], op0=ALU.add, op1=ALU.mult)
            war["xt"] = tv
            tv = V.tensor_mul(out=ot[:], in0=ot[:], in1=nfw_s[:])
            tv = V.tensor_add(out=ot[:], in0=ot[:], in1=nfb_s[:])
            e_ln(P, ot[:], st, C, eps_ln, negm[:, 0:1], rstd[:, 0:1])
            tv = V.tensor_scalar(out=ot[:], in0=ot[:], scalar1=negm[:, 0:1],
                                 scalar2=rstd[:, 0:1], op0=ALU.add, op1=ALU.mult)
            tv = V.tensor_add(out=ot[:], in0=ot[:], in1=x2t[:], after=[tx2])
            war["x2t"] = tv
            war["sq"] = tv
            war["xt"] = GP.dma(xo_t[i], ot[:], after=[tv])


def build_mamba_c(tsh=TSH, final=False):
    P = Prog()
    io = {
        "yg": P.din("yg", [D_INNER, tsh], BF16),
        "xres": P.din("xres", [tsh, C]),
        "wout": P.din("wout", [D_INNER, C]),
        "ident": P.din("ident", [128, 128], BF16),
        "consts": P.din("consts", [128, NCONST]),
        "xout": P.dout("xout", [tsh, C]),
    }
    if final:
        io["nfw"] = P.din("nfw", [128, C])
        io["nfb"] = P.din("nfb", [128, C])
        io["xout2"] = P.din("xout2", [tsh, C])
    P.begin()
    emit_mamba_c(P, io, tsh=tsh, final=final)
    nc = P.finish()
    return nc, P




def _emit_scores(P, V, SC, PE, xlt, wgt_b, lb_s, g8, gm, s1, s2, tt_tok, war_psg,
                 ps_g):
    for j in range(8):
        aft = [tt_tok] + ([war_psg] if war_psg else []) if j == 0 else []
        PE.matmul(ps_g[:], xlt[:, j * 128:(j + 1) * 128],
                  wgt_b[:, j * MOE_E:(j + 1) * MOE_E],
                  start=(j == 0), stop=(j == 7), after=aft)
    tpe_g = PE.tok()
    tl = V.tensor_add(out=g8[:], in0=ps_g[:], in1=lb_s[:], after=[tpe_g])
    tsg = SC.activation(out=g8[:], in_=g8[:], func=AF.Sigmoid, after=[tl])
    t1 = V.reduce_max(out=s1[:, 0:1], in_=g8[:], axis=AX.X, after=[tsg])
    t2 = V.tensor_scalar(out=gm[:], in0=g8[:], scalar1=s1[:, 0:1], scalar2=None,
                         op0=ALU.is_ge)
    t3 = V.tensor_scalar(out=gm[:], in0=gm[:], scalar1=-1.0, scalar2=1.0,
                         op0=ALU.mult, op1=ALU.add)
    t4 = V.tensor_mul(out=gm[:], in0=g8[:], in1=gm[:])
    t5 = V.tensor_reduce(out=s2[:, 0:1], in_=gm[:], axis=AX.X, op=ALU.max)
    t6 = V.tensor_scalar(out=gm[:], in0=g8[:], scalar1=s2[:, 0:1], scalar2=None,
                         op0=ALU.is_ge)
    t7 = V.tensor_add(out=s2[:, 0:1], in0=s2[:, 0:1], in1=s1[:, 0:1])
    t8 = V.tensor_scalar(out=s2[:, 0:1], in0=s2[:, 0:1], scalar1=1e-8, scalar2=None,
                         op0=ALU.add)
    t9 = V.reciprocal(out=s2[:, 0:1], in_=s2[:, 0:1])
    return t6, t9, tpe_g


def emit_ffn1a(P, io, tsh=TSH):
    x_in = io["x"]
    wgt = io["wgt"]
    lb = io["lb"]
    ones_col = io["ones_col"]
    ident_in = io["ident"]
    consts = io["consts"]
    part_o = io["part"]

    ntt = tsh // 128
    x_t = x_in.rearrange("(n p) c -> n p c", p=128)

    cst = P.sb("cst", [128, NCONST], F32)
    idb = P.sb("idb", [128, 128], BF16)
    ocol = P.sb("ocol", [128, 1], BF16)
    wstage = P.sb("wstage", [128, 2 * MOE_E], F32)
    wgt_b = P.sb("wgt_b", [128, 8 * MOE_E], BF16)
    lb_s = P.sb("lb_s", [128, MOE_E], F32)
    xt = P.sb("xt", [128, C], F32)
    sq = P.sb("sq", [128, C], F32)
    s1 = P.sb("s1", [128, 1], F32)
    s2 = P.sb("s2", [128, 1], F32)
    negm = P.sb("negm", [128, 1], F32)
    rstd = P.sb("rstd", [128, 1], F32)
    xlb = P.sb("xlb", [128, C], BF16)
    xlt = P.sb("xlt", [128, C], BF16)
    g8 = P.sb("g8", [128, MOE_E], F32)
    gm = P.sb("gm", [128, MOE_E], F32)
    stk = P.sb("stk", [128, 2 * MOE_E], BF16)
    pres = P.sb("pres", [16, 1], F32)
    ps_g = P.psum("ps_g", [128, MOE_E])
    ps_t = P.psum("ps_t", [128, 128], BF16)
    ps_p = P.psum("ps_p", [16, 1])
    st = {"sq": sq, "s1": s1, "s2": s2}

    SY, GP, V, SC, PE = P.SY, P.GP, P.V, P.SC, P.PE
    t_c = SY.dma(cst[:], consts)
    t_i = SY.dma(idb[:], ident_in)
    t_o = SY.dma(ocol[:], ones_col)
    t_l = SY.dma(lb_s[:], lb)
    SC.copy(out=s1[:, 0:1], in_=cst[:, 0:1], after=[t_c, t_i, t_o, t_l])
    twg = e_load_weight_bf16(P, wgt, wstage,
                             lambda j: wgt_b[:, j * MOE_E:(j + 1) * MOE_E], 8, MOE_E)
    eps_ln = cst[:, K_LN_EPS:K_LN_EPS + 1]

    war_xt = None
    war_sq = None
    war_psg = None
    war_stk = None
    for i in range(ntt):
        aft_x = [war_xt] if war_xt else []
        tx = SY.dma(xt[:], x_t[i], after=aft_x)
        aft = [tx] + ([war_sq] if war_sq else [])
        e_ln(P, xt[:], st, C, eps_ln, negm[:, 0:1], rstd[:, 0:1], after=aft)
        txl = V.tensor_scalar(out=xlb[:], in0=xt[:], scalar1=negm[:, 0:1],
                              scalar2=rstd[:, 0:1], op0=ALU.add, op1=ALU.mult)
        war_xt = txl
        tt_tok = e_transpose_row(P, lambda j: xlb[:, j * 128:(j + 1) * 128],
                                 lambda j: xlt[:, j * 128:(j + 1) * 128],
                                 8, ps_t, idb, after=[txl, t_i], key="pst")
        tm, tr, tpe_g = _emit_scores(P, V, SC, PE, xlt, wgt_b, lb_s, g8, gm,
                                     s1, s2, tt_tok, war_psg, ps_g.ap())
        war_psg = tm
        tw = V.tensor_mul(out=sq[:, 0:MOE_E], in0=g8[:], in1=gm[:], after=[tm])
        tw = V.tensor_scalar(out=stk[:, 0:MOE_E], in0=sq[:, 0:MOE_E],
                             scalar1=s2[:, 0:1], scalar2=None, op0=ALU.mult,
                             after=[tr] + ([war_stk] if war_stk else []))
        tw2 = V.tensor_copy(out=stk[:, MOE_E:2 * MOE_E], in_=gm[:])
        PE.matmul(ps_p.ap()[:], stk[:], ocol[:], start=(i == 0), stop=(i == ntt - 1),
                  after=[tw, tw2])
        war_stk = PE.tok()
        war_sq = None
    tpp = PE.tok()
    tv = V.tensor_copy(out=pres[:], in_=ps_p.ap()[:], after=[tpp])
    GP.dma(part_o, pres[:], after=[tv])


def build_ffn1a(tsh=TSH):
    P = Prog()
    io = {
        "x": P.din("x", [tsh, C]),
        "wgt": P.din("wgt", [C, MOE_E]),
        "lb": P.din("lb", [128, MOE_E]),
        "ones_col": P.din("ones_col", [128, 1], BF16),
        "ident": P.din("ident", [128, 128], BF16),
        "consts": P.din("consts", [128, NCONST]),
        "part": P.dout("part", [16, 1]),
    }
    P.begin()
    emit_ffn1a(P, io, tsh=tsh)
    nc = P.finish()
    return nc, P


def build_ffn1b(tsh=TSH):
    P = Prog()
    x_in = P.din("x", [tsh, C])
    pw_t = P.din("pw_t", [MOE_E, 8])
    pc_t = P.din("pc_t", [MOE_E, 8])
    wgt = P.din("wgt", [C, MOE_E])
    lb = P.din("lb", [128, MOE_E])
    sgw = P.din("sgw", [C, 1024])
    suw = P.din("suw", [C, 1024])
    sdw = P.din("sdw", [1024, C])
    egw = P.din("egw", [MOE_E, C, 512])
    euw = P.din("euw", [MOE_E, C, 512])
    edw = P.din("edw", [MOE_E, 512, C])
    ident_in = P.din("ident", [128, 128], BF16)
    identf = P.din("identf", [8, 8])
    consts = P.din("consts", [128, NCONST])
    x_out = P.dout("xout", [tsh, C])

    ntt = tsh // 128
    QT = min(8, ntt)
    NQ = ntt // QT
    x_t = x_in.rearrange("(n p) c -> n p c", p=128)
    xo_t = x_out.rearrange("(n p) c -> n p c", p=128)

    cst = P.sb("cst", [128, NCONST], F32)
    idb = P.sb("idb", [128, 128], BF16)
    idf = P.sb("idf", [8, 8], F32)
    wstage = P.sb("wstage", [128, 2 * 1024], F32)
    wgt_b = P.sb("wgt_b", [128, 8 * MOE_E], BF16)
    lb_s = P.sb("lb_s", [128, MOE_E], F32)
    sgw_b = P.sb("sgw_b", [128, 8 * 1024], BF16)
    suw_b = P.sb("suw_b", [128, 8 * 1024], BF16)
    sdw_b = P.sb("sdw_b", [128, 8 * 1024], BF16)
    egw_b = P.sb("egw_b", [128, 2 * 8 * 512], BF16)
    euw_b = P.sb("euw_b", [128, 2 * 8 * 512], BF16)
    edw_b = P.sb("edw_b", [128, 2 * 4 * 1024], BF16)
    pstk = P.sb("pstk", [MOE_E, 20], F32)
    wm_s = P.sb("wm_s", [128, MOE_E], F32)
    onesf = P.sb("onesf", [128, 128], F32)

    xlt_q = P.sb("xlt_q", [128, QT * 1024], BF16)
    acc = P.sb("acc", [128, QT * 1024], F32)
    xres = P.sb("xres", [128, QT * 1024], F32)
    coefs = P.sb("coefs", [128, QT * MOE_E], F32)

    xt = P.sb("xt", [128, C], F32)
    sq = P.sb("sq", [128, 2048], F32)
    s1 = P.sb("s1", [128, 1], F32)
    s2 = P.sb("s2", [128, 1], F32)
    negm = P.sb("negm", [128, 1], F32)
    rstd = P.sb("rstd", [128, 1], F32)
    xlb = P.sb("xlb", [128, C], BF16)
    g8 = P.sb("g8", [128, MOE_E], F32)
    gm = P.sb("gm", [128, MOE_E], F32)
    gcol = P.sb("gcol", [128, 1], F32)
    siln = P.sb("siln", [128, 1024], BF16)
    mid = P.sb("mid", [128, 1024], BF16)
    midt = P.sb("midt", [128, 1024], BF16)

    ps_u = P.psum("ps_u", [128, 2048])
    ps_o = P.psum("ps_o", [128, 1024])
    ps_t = P.psum("ps_t", [128, 128], BF16)
    ps_g = P.psum("ps_g", [128, MOE_E])
    st = {"sq": sq, "s1": s1, "s2": s2}

    P.begin()
    SY, GP, V, SC, PE = P.SY, P.GP, P.V, P.SC, P.PE
    t_c = SY.dma(cst[:], consts)
    t_i = SY.dma(idb[:], ident_in)
    t_if = SY.dma(idf[:], identf)
    t_l = SY.dma(lb_s[:], lb)
    t_p1 = SY.dma(pstk[:, 0:8], pw_t)
    t_p2 = SY.dma(pstk[:, 8:16], pc_t)
    SC.copy(out=s1[:, 0:1], in_=cst[:, 0:1], after=[t_c, t_i, t_if, t_l, t_p1, t_p2])
    tz = V.memset(onesf[:], 1.0)
    tg1 = V.tensor_reduce(out=pstk[:, 16:17], in_=pstk[:, 0:8], axis=AX.X, op=ALU.add,
                          after=[t_p1])
    tg2 = V.tensor_reduce(out=pstk[:, 17:18], in_=pstk[:, 8:16], axis=AX.X, op=ALU.add,
                          after=[t_p2])
    tg3 = V.tensor_scalar(out=pstk[:, 17:18], in0=pstk[:, 17:18], scalar1=1.0,
                          scalar2=None, op0=ALU.max)
    tg4 = V.reciprocal(out=pstk[:, 17:18], in_=pstk[:, 17:18])
    tg5 = V.tensor_mul(out=pstk[:, 18:19], in0=pstk[:, 16:17], in1=pstk[:, 17:18])
    PE.matmul(ps_g.ap()[0:1, 0:MOE_E], pstk[:, 18:19], idf[:], start=True, stop=True,
              after=[tg5, t_if])
    tpr = PE.tok()
    tg6 = V.tensor_copy(out=wm_s[0:1, :], in_=ps_g.ap()[0:1, 0:MOE_E], after=[tpr])
    PE.matmul(ps_g.ap()[:, 0:MOE_E], onesf[0:1, 0:128], wm_s[0:1, :],
              start=True, stop=True, after=[tg6, tz])
    tpr2 = PE.tok()
    tg7 = V.tensor_copy(out=wm_s[:], in_=ps_g.ap()[:, 0:MOE_E], after=[tpr2])
    war_psg = tg7

    twg = e_load_weight_bf16(P, wgt, wstage,
                             lambda j: wgt_b[:, j * MOE_E:(j + 1) * MOE_E], 8, MOE_E)
    tw1 = e_load_weight_bf16(P, sgw, wstage,
                             lambda j: sgw_b[:, j * 1024:(j + 1) * 1024], 8, 1024)
    tw2 = e_load_weight_bf16(P, suw, wstage,
                             lambda j: suw_b[:, j * 1024:(j + 1) * 1024], 8, 1024)
    tw3 = e_load_weight_bf16(P, sdw, wstage,
                             lambda j: sdw_b[:, j * 1024:(j + 1) * 1024], 8, 1024)
    eps_ln = cst[:, K_LN_EPS:K_LN_EPS + 1]

    war_xt = None
    war_sq = None
    war_psu = []
    war_pso = None
    t_we_mm = None
    t_xlt_last = None

    for q in range(NQ):
        for it in range(QT):
            i = q * QT + it
            aft_x = ([war_xt] if war_xt else [])
            tx = SY.dma(xt[:], x_t[i], after=aft_x)
            tres = V.tensor_copy(out=xres[:, it * 1024:(it + 1) * 1024], in_=xt[:],
                                 after=[tx])
            aft = [war_sq] if war_sq else []
            e_ln(P, xt[:], st, C, eps_ln, negm[:, 0:1], rstd[:, 0:1], after=aft)
            war_sq = None
            txl = V.tensor_scalar(out=xlb[:], in0=xt[:], scalar1=negm[:, 0:1],
                                  scalar2=rstd[:, 0:1], op0=ALU.add, op1=ALU.mult)
            war_xt = txl
            aft_t = [txl, t_i] + ([t_xlt_last] if (it == 0 and t_xlt_last) else [])
            tt_tok = e_transpose_row(
                P, lambda j: xlb[:, j * 128:(j + 1) * 128],
                lambda j, it=it: xlt_q[:, it * 1024 + j * 128: it * 1024 + (j + 1) * 128],
                8, ps_t, idb, after=aft_t, key="pst")
            xlt_ap = xlt_q[:, it * 1024:(it + 1) * 1024]
            tm, tr, tpe_g = _emit_scores(P, V, SC, PE, xlt_ap, wgt_b, lb_s, g8, gm,
                                         s1, s2, tt_tok, war_psg, ps_g.ap())
            war_psg = tm
            tcf = V.tensor_mul(out=coefs[:, it * MOE_E:(it + 1) * MOE_E],
                               in0=gm[:], in1=wm_s[:], after=[tm, tg7])
            for br, wb in ((0, sgw_b), (1, suw_b)):
                for f in range(2):
                    for j in range(8):
                        aft = []
                        if br == 0 and f == 0 and j == 0:
                            aft = [tt_tok, tw1, tw2] + war_psu
                        PE.matmul(ps_u.ap()[:, br * 1024 + f * 512: br * 1024 + (f + 1) * 512],
                                  xlt_ap[:, j * 128:(j + 1) * 128],
                                  wb[:, j * 1024 + f * 512: j * 1024 + (f + 1) * 512],
                                  start=(j == 0), stop=(j == 7), after=aft)
            tpe_u = PE.tok()
            ts1 = SC.activation(out=siln[:], in_=ps_u.ap()[:, 0:1024], func=AF.Silu,
                                after=[tpe_u])
            tm1 = V.tensor_mul(out=mid[:], in0=ps_u.ap()[:, 1024:2048], in1=siln[:],
                               after=[ts1, tpe_u])
            war_psu = [ts1, tm1]
            ttm = e_transpose_row(P, lambda j: mid[:, j * 128:(j + 1) * 128],
                                  lambda j: midt[:, j * 128:(j + 1) * 128],
                                  8, ps_t, idb, after=[tm1], key="pst")
            for f in range(2):
                for j in range(8):
                    aft = ([ttm, tw3] + ([war_pso] if war_pso else [])) \
                        if (f == 0 and j == 0) else []
                    PE.matmul(ps_o.ap()[:, f * 512:(f + 1) * 512],
                              midt[:, j * 128:(j + 1) * 128],
                              sdw_b[:, j * 1024 + f * 512: j * 1024 + (f + 1) * 512],
                              start=(j == 0), stop=(j == 7), after=aft)
            tpe_o = PE.tok()
            war_pso = V.tensor_copy(out=acc[:, it * 1024:(it + 1) * 1024],
                                    in_=ps_o.ap()[:], after=[tpe_o])
        t_xlt_last = None

        we_hist = [None, None]
        for e in range(MOE_E):
            eb5 = (e % 2) * 4096
            eb10 = (e % 2) * 4096
            caft = [we_hist[e % 2]] if we_hist[e % 2] else []
            tg_ = e_load_weight_bf16(P, egw[e], wstage,
                                     lambda j, eb5=eb5: egw_b[:, eb5 + j * 512: eb5 + (j + 1) * 512],
                                     8, 512, conv_after=caft)
            tu_ = e_load_weight_bf16(P, euw[e], wstage,
                                     lambda j, eb5=eb5: euw_b[:, eb5 + j * 512: eb5 + (j + 1) * 512],
                                     8, 512, conv_after=caft)
            td_ = e_load_weight_bf16(P, edw[e], wstage,
                                     lambda j, eb10=eb10: edw_b[:, eb10 + j * 1024: eb10 + (j + 1) * 1024],
                                     4, 1024, conv_after=caft)
            for it in range(QT):
                xlt_ap = xlt_q[:, it * 1024:(it + 1) * 1024]
                for br, wb in ((0, egw_b), (1, euw_b)):
                    for j in range(8):
                        aft = []
                        if br == 0 and j == 0:
                            aft = [tg_, tu_] + war_psu
                        PE.matmul(ps_u.ap()[:, br * 512:(br + 1) * 512],
                                  xlt_ap[:, j * 128:(j + 1) * 128],
                                  wb[:, eb5 + j * 512: eb5 + (j + 1) * 512],
                                  start=(j == 0), stop=(j == 7), after=aft)
                tpe_u = PE.tok()
                ts1 = SC.activation(out=siln[:, 0:512], in_=ps_u.ap()[:, 0:512],
                                    func=AF.Silu, after=[tpe_u])
                tm1 = V.tensor_mul(out=mid[:, 0:512], in0=ps_u.ap()[:, 512:1024],
                                   in1=siln[:, 0:512], after=[ts1, tpe_u])
                war_psu = [ts1, tm1]
                ttm = e_transpose_row(P, lambda j: mid[:, j * 128:(j + 1) * 128],
                                      lambda j: midt[:, j * 128:(j + 1) * 128],
                                      4, ps_t, idb, after=[tm1], key="pst")
                for f in range(2):
                    for j in range(4):
                        aft = ([ttm, td_] + ([war_pso] if war_pso else [])) \
                            if (f == 0 and j == 0) else []
                        PE.matmul(ps_o.ap()[:, f * 512:(f + 1) * 512],
                                  midt[:, j * 128:(j + 1) * 128],
                                  edw_b[:, eb10 + j * 1024 + f * 512: eb10 + j * 1024 + (f + 1) * 512],
                                  start=(j == 0), stop=(j == 3), after=aft)
                t_we_mm = PE.tok()
                we_hist[e % 2] = t_we_mm
                war_pso = V.scalar_tensor_tensor(
                    out=acc[:, it * 1024:(it + 1) * 1024],
                    in0=ps_o.ap()[:], scalar=coefs[:, it * MOE_E + e: it * MOE_E + e + 1],
                    in1=acc[:, it * 1024:(it + 1) * 1024],
                    op0=ALU.mult, op1=ALU.add, after=[t_we_mm])
        t_xlt_last = t_we_mm

        for it in range(QT):
            i = q * QT + it
            a_ap = acc[:, it * 1024:(it + 1) * 1024]
            r_ap = xres[:, it * 1024:(it + 1) * 1024]
            aft_ln = ([war_sq] if war_sq else []) + ([war_xt] if war_xt else [])
            tln = e_ln(P, a_ap, st, C, eps_ln, negm[:, 0:1], rstd[:, 0:1],
                       after=aft_ln)
            tv = V.tensor_scalar(out=xt[:], in0=a_ap, scalar1=negm[:, 0:1],
                                 scalar2=rstd[:, 0:1], op0=ALU.add, op1=ALU.mult)
            tv = V.tensor_add(out=xt[:], in0=xt[:], in1=r_ap)
            tst = GP.dma(xo_t[i], xt[:], after=[tv])
            war_xt = tst
            war_sq = None

    nc = P.finish()
    return nc, P





MAGIC = 0x51CAFE77


def e_roles(P, magic_in):
    nc = P.nc
    g = nc.gpsimd
    P.magic_sb = P.sb("magic_sb", [1, 4], U32)
    P.flagrd = P.sb("flagrd", [1, 1], U32)
    t = P.SY.dma(P.magic_sb[:], magic_in)
    P.rh = g.alloc_register("rh")
    pid = nc.partition_id(engines=[g.engine])
    g.reg_alu(P.rh, pid, 2, ALU.mod)
    P._npoll = 0
    return t


def e_publish(P, src_ap, dst_h0, dst_h1, sh_flag, widx, after=()):
    g = P.nc.gpsimd
    GP = P.GP
    GP._waits(list(after))
    dsem = P.sems[GP.dname]
    base = P.cnt[GP.dname]
    with g.If_eq(P.rh, 0):
        g.dma_start(out=dst_h0, in_=src_ap).then_inc(dsem, 16)
        g.wait_ge(dsem, base + 16)
        g.dma_start(out=sh_flag[0:1, widx:widx + 1],
                    in_=P.magic_sb[0:1, 0:1]).then_inc(dsem, 16)
    with g.Else():
        g.dma_start(out=dst_h1, in_=src_ap).then_inc(dsem, 16)
        g.wait_ge(dsem, base + 16)
        g.dma_start(out=sh_flag[1:2, widx:widx + 1],
                    in_=P.magic_sb[0:1, 0:1]).then_inc(dsem, 16)
    g.end_ifs()
    P.cnt[GP.dname] += 32
    return (GP.dname, P.cnt[GP.dname])


def e_poll(P, sh_flag, widx):
    nc = P.nc
    g = nc.gpsimd
    P.GP._waits([])
    sem = P.sem(f"q_poll{P._npoll}")
    loop = f"poll{P._npoll}_loop"
    done = f"poll{P._npoll}_done"
    P._npoll += 1
    with g.register(f"exp{P._npoll}") as exp, g.register(f"rfv{P._npoll}") as rfv:
        g.reg_mov(exp, 0)
        g.br(loop)
        with nc.body(loop, valid_engines=[g.engine]):
            for _ in range(40):
                g.nop(cycle_cnt=250)
            with g.If_eq(P.rh, 0):
                g.dma_start(out=P.flagrd[0:1, 0:1],
                            in_=sh_flag[1:2, widx:widx + 1]).then_inc(sem, 16)
            with g.Else():
                g.dma_start(out=P.flagrd[0:1, 0:1],
                            in_=sh_flag[0:1, widx:widx + 1]).then_inc(sem, 16)
            g.end_ifs()
            g.reg_add(exp, exp, 16)
            g.wait_ge(sem, exp)
            g.reg_load(rfv, P.flagrd[0:1, 0:1])
            g.br_cmp(rfv, MAGIC, on_true=done, on_false=loop, comp_op="IS_EQ")
        nc.switch_body(done)
    return P.GP.engine_nop()


def e_ifelse_copy(P, dst_ap, src_h0, src_h1, after=()):
    g = P.nc.gpsimd
    GP = P.GP
    GP._waits(list(after))
    dsem = P.sems[GP.dname]
    with g.If_eq(P.rh, 0):
        g.dma_start(out=dst_ap, in_=src_h0).then_inc(dsem, 16)
    with g.Else():
        g.dma_start(out=dst_ap, in_=src_h1).then_inc(dsem, 16)
    g.end_ifs()
    P.cnt[GP.dname] += 16
    return (GP.dname, P.cnt[GP.dname])


def build_launchA():
    P = Prog()
    io = {}
    d = io
    d["xpre"] = P.din("xpre", [TSH, C])
    d["xown"] = P.din("xown", [TSH, C])
    d["gate"] = P.din("gate", [128, 1], F32)
    d["wq"] = P.din("wq", [C, C]); d["wk"] = P.din("wk", [C, C])
    d["wv"] = P.din("wv", [C, C]); d["wo"] = P.din("wo", [C, C])
    d["bq"] = P.din("bq", [1, C]); d["bk"] = P.din("bk", [1, C])
    d["bv"] = P.din("bv", [1, C]); d["bo"] = P.din("bo", [1, C])
    d["ones_row"] = P.din("ones_row", [1, 128], BF16)
    d["ones_col"] = P.din("ones_col", [128, 1], BF16)
    d["tri"] = P.din("tri", [128, 128], F32)
    d["ident"] = P.din("ident", [128, 128], BF16)
    d["consts"] = P.din("consts", [128, NCONST])
    d["magic"] = P.din("magic", [1, 4], U32)
    d["wsi"] = P.din("wsi", [C, 2048]); d["wso"] = P.din("wso", [C, C])
    d["w1"] = P.din("w1", [C, 2048]); d["w2"] = P.din("w2", [MOE_E, C, C])
    d["wg"] = P.din("wg", [C, MOE_E]); d["gb"] = P.din("gb", [128, MOE_E])
    for l in range(2):
        d[f"lnw{l}"] = P.din(f"lnw{l}", [128, 8])
        d[f"lnb{l}"] = P.din(f"lnb{l}", [128, 8])
        d[f"win{l}"] = P.din(f"win{l}", [C, 2048])
        d[f"cw{l}"] = P.din(f"cw{l}", [128, 8 * D_CONV])
        d[f"cb{l}"] = P.din(f"cb{l}", [128, 8])
        d[f"xpw{l}"] = P.din(f"xpw{l}", [C, 96])
        d[f"dtw{l}"] = P.din(f"dtw{l}", [DT_RANK, C])
        d[f"dtb{l}"] = P.din(f"dtb{l}", [128, 8])
        d[f"alog{l}"] = P.din(f"alog{l}", [128, 8 * D_ST])
        d[f"dsk{l}"] = P.din(f"dsk{l}", [128, 8])
        d[f"wout{l}"] = P.din(f"wout{l}", [D_INNER, C])
    d["selbc"] = P.din("selbc", [32, 2 * D_ST * 128], BF16)
    d["nfw"] = P.din("nfw", [128, C]); d["nfb"] = P.din("nfb", [128, C])
    d["wgt"] = P.din("wgt", [C, MOE_E]); d["lb"] = P.din("lb", [128, MOE_E])
    x3_out = P.dout("xout3", [TSH, C])
    part_out = P.dout("part", [16, 1])

    x1 = P.dtmp("x1s", [TSH, C])
    x2 = P.dtmp("x2s", [TSH, C])
    xin_a = P.dtmp("xina", [TSH, C])
    xin_b = P.dtmp("xinb", [TSH, C])
    xcs_l = P.dtmp("xcsl", [C, T], BF16)
    zs_l = P.dtmp("zsl", [C, T], BF16)
    dblp_l = P.dtmp("dblpl", [96, T])
    yg_mine = P.dtmp("ygmine", [C, T], BF16)
    yg_loc = P.dtmp("ygloc", [D_INNER, TSH], BF16)
    sh_xin = [P.dtmp(f"sh_xin{l}", [T, C], shared=True) for l in range(2)]
    sh_dbl = [P.dtmp(f"sh_dbl{l}", [2, 96, T], shared=True) for l in range(2)]
    sh_yg = [P.dtmp(f"sh_yg{l}", [D_INNER, T], BF16, shared=True) for l in range(2)]
    sh_flag = P.dtmp("sh_flag", [2, 64], U32, shared=True)

    P.begin()
    e_roles(P, d["magic"])

    with P.phase("attn"):
        emit_attn0(P, {**io, "xout": x1}, tsh=TSH, tpre=TSH)
    with P.phase("ffn0"):
        emit_ffn0(P, {**io, "x": x1, "xout": x2, "xin0": xin_a},
                  tsh=TSH, emit_xin0=True)

    xin_cur = xin_a
    xin_nxt = xin_b
    for l in range(2):
        tpub = e_publish(P, xin_cur, sh_xin[l][0:TSH, :], sh_xin[l][TSH:T, :],
                         sh_flag, 3 * l + 0)
        tok = e_poll(P, sh_flag, 3 * l + 0)
        with P.phase(f"ma{l}"):
            emit_mamba_a(P, {
                "xin": sh_xin[l], "lnw": d[f"lnw{l}"], "lnb": d[f"lnb{l}"],
                "win": d[f"win{l}"], "cw": d[f"cw{l}"], "cb": d[f"cb{l}"],
                "xpw": d[f"xpw{l}"], "ident": d["ident"], "consts": d["consts"],
                "xcs": xcs_l, "zs": zs_l, "dblp": dblp_l,
            }, tt=T, start_after=[tpub, tok])
        tpub = e_publish(P, dblp_l, sh_dbl[l][0], sh_dbl[l][1], sh_flag, 3 * l + 1)
        tok = e_poll(P, sh_flag, 3 * l + 1)
        with P.phase(f"mb{l}"):
            emit_mamba_b(P, {
                "xcs": xcs_l, "zs": zs_l,
                "dbl0": sh_dbl[l][0], "dbl1": sh_dbl[l][1],
                "dtw": d[f"dtw{l}"], "dtb": d[f"dtb{l}"],
                "alog": d[f"alog{l}"], "dsk": d[f"dsk{l}"],
                "selbc": d["selbc"], "ident": d["ident"], "consts": d["consts"],
                "yg": yg_mine,
            }, tt=T, lc=1024, start_after=[tpub, tok])
        tpub = e_publish(P, yg_mine, sh_yg[l][0:C, :], sh_yg[l][C:D_INNER, :],
                         sh_flag, 3 * l + 2)
        tok = e_poll(P, sh_flag, 3 * l + 2)
        tcp = e_ifelse_copy(P, yg_loc, sh_yg[l][:, 0:TSH], sh_yg[l][:, TSH:T],
                            after=[tpub, tok])
        with P.phase(f"mc{l}"):
            mio = {
                "yg": yg_loc, "xres": xin_cur, "wout": d[f"wout{l}"],
                "ident": d["ident"], "consts": d["consts"],
                "xout": (xin_nxt if l == 0 else x3_out),
            }
            if l == 1:
                mio["nfw"] = d["nfw"]; mio["nfb"] = d["nfb"]; mio["xout2"] = x2
            emit_mamba_c(P, mio, tsh=TSH, final=(l == 1), start_after=[tcp])
        xin_cur, xin_nxt = xin_nxt, xin_cur

    with P.phase("f1a"):
        emit_ffn1a(P, {**io, "x": x3_out, "part": part_out}, tsh=TSH)

    nc = P.finish()
    return nc, P



def _consts_pack():
    import ml_dtypes
    BF = ml_dtypes.bfloat16
    return {
        "consts": make_consts(),
        "ident": np.eye(128, dtype=BF),
        "ones_row": np.ones((1, 128), BF),
        "ones_col": np.ones((128, 1), BF),
        "tri": np.triu(np.ones((128, 128), np.float32)),
        "identf": np.eye(8, dtype=np.float32),
    }


def _col8(v):
    return np.ascontiguousarray(v.reshape(8, 128).T)


def _selbc():
    import ml_dtypes
    sel = np.zeros((32, 2 * D_ST * 128), np.float32)
    for n in range(D_ST):
        sel[n, n * 128:(n + 1) * 128] = 1.0
        sel[D_ST + n, (D_ST + n) * 128:(D_ST + n + 1) * 128] = 1.0
    return sel.astype(ml_dtypes.bfloat16)


_BUILD_CACHE = {}


def _get(name, fn):
    if name not in _BUILD_CACHE:
        _BUILD_CACHE[name] = fn()
    return _BUILD_CACHE[name]


def _launch(nc, in_maps):
    res = run_bass_kernel_spmd(nc, in_maps, core_ids=list(range(NCORES)))
    return res.results


def kernel_multi(**inp):
    CP = _consts_pack()
    x0 = np.ascontiguousarray(inp["L0_attn_x"])

    def shard(c):
        b, h = c // 2, c % 2
        return b, h

    wqkv = inp["L0_attn_w_qkv"]; bqkv = inp["L0_attn_b_qkv"]
    wq_t = np.ascontiguousarray(wqkv[0:C].T)
    wk_t = np.ascontiguousarray(wqkv[C:2 * C].T)
    wv_t = np.ascontiguousarray(wqkv[2 * C:3 * C].T)
    wo_t = np.ascontiguousarray(inp["L0_attn_w_out"].T)
    nc_at, _ = _get("attn0", lambda: build_attn0(tsh=TSH, tpre=TSH))
    maps = []
    for c in range(NCORES):
        b, h = shard(c)
        maps.append({
            "xpre": np.ascontiguousarray(x0[b, 0:TSH]),
            "xown": np.ascontiguousarray(x0[b, h * TSH:(h + 1) * TSH]),
            "gate": np.full((128, 1), float(h), np.float32),
            "wq": wq_t, "wk": wk_t, "wv": wv_t, "wo": wo_t,
            "bq": bqkv[None, 0:C].astype(np.float32),
            "bk": bqkv[None, C:2 * C].astype(np.float32),
            "bv": bqkv[None, 2 * C:3 * C].astype(np.float32),
            "bo": inp["L0_attn_b_out"][None, :].astype(np.float32),
            "ones_row": CP["ones_row"], "ones_col": CP["ones_col"],
            "tri": CP["tri"], "ident": CP["ident"], "consts": CP["consts"],
        })
    r = _launch(nc_at, maps)
    x1 = [r[c]["xout"] for c in range(NCORES)]

    nc_f0, _ = _get("ffn0", lambda: build_ffn0(tsh=TSH, emit_xin0=True))
    gbv = np.tile(inp["L0_ffn_gate_bias"][None], (128, 1)).astype(np.float32)
    maps = []
    for c in range(NCORES):
        maps.append({
            "x": x1[c],
            "wsi": inp["L0_ffn_w_shared_in"], "wso": inp["L0_ffn_w_shared_out"],
            "w1": inp["L0_ffn_w1_shared"], "w2": inp["L0_ffn_w2"],
            "wg": inp["L0_ffn_w_gate"], "gb": gbv,
            "ident": CP["ident"], "consts": CP["consts"],
        })
    r = _launch(nc_f0, maps)
    x2 = [r[c]["xout"] for c in range(NCORES)]
    xinner = [r[c]["xin0"] for c in range(NCORES)]

    in_w = inp["L1_attn_in_proj_w"]; conv_w = inp["L1_attn_conv_w"]
    conv_b = inp["L1_attn_conv_b"]; xp_w = inp["L1_attn_x_proj_w"]
    dtp_w = inp["L1_attn_dt_proj_w"]; dtp_b = inp["L1_attn_dt_proj_b"]
    A_log = inp["L1_attn_A_log"]; D_skip = inp["L1_attn_D_skip"]
    out_w = inp["L1_attn_out_proj_w"]
    ln_w = inp["L1_attn_ln_w"]; ln_b = inp["L1_attn_ln_b"]
    selbc = _selbc()

    nc_ma, _ = _get("mamba_a", lambda: build_mamba_a(tt=T))
    nc_mb, _ = _get("mamba_b", lambda: build_mamba_b(tt=T, lc=1024))
    nc_mc0, _ = _get("mamba_c0", lambda: build_mamba_c(tsh=TSH, final=False))
    nc_mc1, _ = _get("mamba_c1", lambda: build_mamba_c(tsh=TSH, final=True))

    for l in range(2):
        xin_full = [np.concatenate([xinner[2 * b], xinner[2 * b + 1]], axis=0)
                    for b in range(B)]
        maps = []
        for c in range(NCORES):
            b, h = shard(c)
            chs = slice(h * 1024, (h + 1) * 1024)
            win = np.concatenate([in_w[l][chs],
                                  in_w[l][D_INNER + h * 1024: D_INNER + (h + 1) * 1024]],
                                 axis=0).T
            maps.append({
                "xin": xin_full[b],
                "lnw": _col8(ln_w[l]), "lnb": _col8(ln_b[l]),
                "win": np.ascontiguousarray(win),
                "cw": np.ascontiguousarray(
                    conv_w[l][chs].reshape(8, 128, 4).transpose(1, 0, 2).reshape(128, 32)),
                "cb": _col8(conv_b[l][chs]),
                "xpw": np.ascontiguousarray(xp_w[l][:, chs].T),
                "ident": CP["ident"], "consts": CP["consts"],
            })
        ra = _launch(nc_ma, maps)
        maps = []
        for c in range(NCORES):
            b, h = shard(c)
            chs = slice(h * 1024, (h + 1) * 1024)
            maps.append({
                "xcs": ra[c]["xcs"], "zs": ra[c]["zs"],
                "dbl0": ra[2 * b]["dblp"], "dbl1": ra[2 * b + 1]["dblp"],
                "dtw": np.ascontiguousarray(dtp_w[l][chs].T),
                "dtb": _col8(dtp_b[l][chs]),
                "alog": np.ascontiguousarray(
                    A_log[l][chs].reshape(8, 128, 16).transpose(1, 0, 2).reshape(128, 128)),
                "dsk": _col8(D_skip[l][chs]),
                "selbc": selbc, "ident": CP["ident"], "consts": CP["consts"],
            })
        rb = _launch(nc_mb, maps)
        yg_full = [np.concatenate([rb[2 * b]["yg"], rb[2 * b + 1]["yg"]], axis=0)
                   for b in range(B)]
        maps = []
        for c in range(NCORES):
            b, h = shard(c)
            m = {
                "yg": np.ascontiguousarray(yg_full[b][:, h * TSH:(h + 1) * TSH]),
                "xres": xinner[c],
                "wout": np.ascontiguousarray(out_w[l].T),
                "ident": CP["ident"], "consts": CP["consts"],
            }
            if l == 1:
                m["nfw"] = np.tile(inp["L1_attn_norm_f_w"][None], (128, 1)).astype(np.float32)
                m["nfb"] = np.tile(inp["L1_attn_norm_f_b"][None], (128, 1)).astype(np.float32)
                m["xout2"] = x2[c]
            maps.append(m)
        rc = _launch(nc_mc1 if l == 1 else nc_mc0, maps)
        xinner = [rc[c]["xout"] for c in range(NCORES)]
    x3 = xinner

    wgt = np.ascontiguousarray(inp["L1_ffn_gate_w"].T)
    lbv = np.tile(inp["L1_ffn_lb_bias"][None], (128, 1)).astype(np.float32)
    nc_fa, _ = _get("ffn1a", lambda: build_ffn1a(tsh=TSH))
    maps = []
    for c in range(NCORES):
        maps.append({
            "x": x3[c], "wgt": wgt, "lb": lbv,
            "ones_col": CP["ones_col"], "ident": CP["ident"], "consts": CP["consts"],
        })
    r = _launch(nc_fa, maps)
    parts = np.stack([r[c]["part"][:, 0] for c in range(NCORES)])
    pw_t = np.ascontiguousarray(parts[:, 0:MOE_E].T)
    pc_t = np.ascontiguousarray(parts[:, MOE_E:2 * MOE_E].T)

    nc_fb, _ = _get("ffn1b", lambda: build_ffn1b(tsh=TSH))
    sgw_t = np.ascontiguousarray(inp["L1_ffn_shared_gate_w"].T)
    suw_t = np.ascontiguousarray(inp["L1_ffn_shared_up_w"].T)
    sdw_t = np.ascontiguousarray(inp["L1_ffn_shared_down_w"].T)
    egw_t = np.ascontiguousarray(inp["L1_ffn_expert_gate_w"].transpose(0, 2, 1))
    euw_t = np.ascontiguousarray(inp["L1_ffn_expert_up_w"].transpose(0, 2, 1))
    edw_t = np.ascontiguousarray(inp["L1_ffn_expert_down_w"].transpose(0, 2, 1))
    maps = []
    for c in range(NCORES):
        maps.append({
            "x": x3[c], "pw_t": pw_t, "pc_t": pc_t,
            "wgt": wgt, "lb": lbv,
            "sgw": sgw_t, "suw": suw_t, "sdw": sdw_t,
            "egw": egw_t, "euw": euw_t, "edw": edw_t,
            "ident": CP["ident"], "identf": CP["identf"], "consts": CP["consts"],
        })
    r = _launch(nc_fb, maps)
    out = np.empty((B, T, C), np.float32)
    for c in range(NCORES):
        b, h = shard(c)
        out[b, h * TSH:(h + 1) * TSH] = r[c]["xout"]
    return out


if __name__ == "__main__":
    pass


def _prep_common(inp):
    CP = _consts_pack()
    CP["magic"] = np.full((1, 4), MAGIC, np.uint32)
    wqkv = inp["L0_attn_w_qkv"]
    pre = {
        "wq": np.ascontiguousarray(wqkv[0:C].T),
        "wk": np.ascontiguousarray(wqkv[C:2 * C].T),
        "wv": np.ascontiguousarray(wqkv[2 * C:3 * C].T),
        "wo": np.ascontiguousarray(inp["L0_attn_w_out"].T),
        "bq": wqkv[None, 0, 0:0].copy(),
    }
    bqkv = inp["L0_attn_b_qkv"]
    pre["bq"] = bqkv[None, 0:C].astype(np.float32)
    pre["bk"] = bqkv[None, C:2 * C].astype(np.float32)
    pre["bv"] = bqkv[None, 2 * C:3 * C].astype(np.float32)
    pre["bo"] = inp["L0_attn_b_out"][None, :].astype(np.float32)
    pre["wsi"] = inp["L0_ffn_w_shared_in"]
    pre["wso"] = inp["L0_ffn_w_shared_out"]
    pre["w1"] = inp["L0_ffn_w1_shared"]
    pre["w2"] = inp["L0_ffn_w2"]
    pre["wg"] = inp["L0_ffn_w_gate"]
    pre["gb"] = np.tile(inp["L0_ffn_gate_bias"][None], (128, 1)).astype(np.float32)
    pre["selbc"] = _selbc()
    pre["nfw"] = np.tile(inp["L1_attn_norm_f_w"][None], (128, 1)).astype(np.float32)
    pre["nfb"] = np.tile(inp["L1_attn_norm_f_b"][None], (128, 1)).astype(np.float32)
    pre["wgt"] = np.ascontiguousarray(inp["L1_ffn_gate_w"].T)
    pre["lb"] = np.tile(inp["L1_ffn_lb_bias"][None], (128, 1)).astype(np.float32)
    return CP, pre


def kernel(**inp):
    CP, pre = _prep_common(inp)
    x0 = np.ascontiguousarray(inp["L0_attn_x"])
    in_w = inp["L1_attn_in_proj_w"]; conv_w = inp["L1_attn_conv_w"]
    conv_b = inp["L1_attn_conv_b"]; xp_w = inp["L1_attn_x_proj_w"]
    dtp_w = inp["L1_attn_dt_proj_w"]; dtp_b = inp["L1_attn_dt_proj_b"]
    A_log = inp["L1_attn_A_log"]; D_skip = inp["L1_attn_D_skip"]
    out_w = inp["L1_attn_out_proj_w"]
    ln_w = inp["L1_attn_ln_w"]; ln_b = inp["L1_attn_ln_b"]

    nc_a, _ = _get("launchA", build_launchA)
    maps = []
    for c in range(NCORES):
        b, h = c // 2, c % 2
        m = {
            "xpre": np.ascontiguousarray(x0[b, 0:TSH]),
            "xown": np.ascontiguousarray(x0[b, h * TSH:(h + 1) * TSH]),
            "gate": np.full((128, 1), float(h), np.float32),
            "ones_row": CP["ones_row"], "ones_col": CP["ones_col"],
            "tri": CP["tri"], "ident": CP["ident"], "consts": CP["consts"],
            "identf": CP["identf"], "magic": CP["magic"],
        }
        m.pop("identf")
        for k in ("wq", "wk", "wv", "wo", "bq", "bk", "bv", "bo", "wsi", "wso",
                  "w1", "w2", "wg", "gb", "selbc", "nfw", "nfb", "wgt", "lb"):
            m[k] = pre[k]
        for l in range(2):
            chs = slice(h * 1024, (h + 1) * 1024)
            win = np.concatenate(
                [in_w[l][chs], in_w[l][D_INNER + h * 1024: D_INNER + (h + 1) * 1024]],
                axis=0).T
            m[f"lnw{l}"] = _col8(ln_w[l])
            m[f"lnb{l}"] = _col8(ln_b[l])
            m[f"win{l}"] = np.ascontiguousarray(win)
            m[f"cw{l}"] = np.ascontiguousarray(
                conv_w[l][chs].reshape(8, 128, 4).transpose(1, 0, 2).reshape(128, 32))
            m[f"cb{l}"] = _col8(conv_b[l][chs])
            m[f"xpw{l}"] = np.ascontiguousarray(xp_w[l][:, chs].T)
            m[f"dtw{l}"] = np.ascontiguousarray(dtp_w[l][chs].T)
            m[f"dtb{l}"] = _col8(dtp_b[l][chs])
            m[f"alog{l}"] = np.ascontiguousarray(
                A_log[l][chs].reshape(8, 128, 16).transpose(1, 0, 2).reshape(128, 128))
            m[f"dsk{l}"] = _col8(D_skip[l][chs])
            m[f"wout{l}"] = np.ascontiguousarray(out_w[l].T)
        maps.append(m)
    ra = _launch(nc_a, maps)
    x3 = [ra[c]["xout3"] for c in range(NCORES)]
    parts = np.stack([ra[c]["part"][:, 0] for c in range(NCORES)])
    pw_t = np.ascontiguousarray(parts[:, 0:MOE_E].T)
    pc_t = np.ascontiguousarray(parts[:, MOE_E:2 * MOE_E].T)

    nc_fb, _ = _get("ffn1b", lambda: build_ffn1b(tsh=TSH))
    sgw_t = np.ascontiguousarray(inp["L1_ffn_shared_gate_w"].T)
    suw_t = np.ascontiguousarray(inp["L1_ffn_shared_up_w"].T)
    sdw_t = np.ascontiguousarray(inp["L1_ffn_shared_down_w"].T)
    egw_t = np.ascontiguousarray(inp["L1_ffn_expert_gate_w"].transpose(0, 2, 1))
    euw_t = np.ascontiguousarray(inp["L1_ffn_expert_up_w"].transpose(0, 2, 1))
    edw_t = np.ascontiguousarray(inp["L1_ffn_expert_down_w"].transpose(0, 2, 1))
    maps = []
    for c in range(NCORES):
        maps.append({
            "x": x3[c], "pw_t": pw_t, "pc_t": pc_t,
            "wgt": pre["wgt"], "lb": pre["lb"],
            "sgw": sgw_t, "suw": suw_t, "sdw": sdw_t,
            "egw": egw_t, "euw": euw_t, "edw": edw_t,
            "ident": CP["ident"], "identf": CP["identf"], "consts": CP["consts"],
        })
    r = _launch(nc_fb, maps)
    out = np.empty((B, T, C), np.float32)
    for c in range(NCORES):
        b, h = c // 2, c % 2
        out[b, h * TSH:(h + 1) * TSH] = r[c]["xout"]
    return out
